# version 12
# baseline (speedup 1.0000x reference)
import math
from contextlib import ExitStack

import numpy as np
import ml_dtypes
import concourse.bass as bass
import concourse.mybir as mybir
from concourse.bass_utils import run_bass_kernel_spmd

F32 = mybir.dt.float32
BF16 = mybir.dt.bfloat16
AF = mybir.ActivationFunctionType
ALU = mybir.AluOpType
AX = mybir.AxisListType

D = 1024
N_TOK = 4096
N_TILES = N_TOK // 128


class Prog:

    ENGINES = ("sync", "vector", "scalar", "tensor", "gpsimd")

    def __init__(self, nc, stack):
        self.nc = nc
        self.stack = stack
        self.ops = {e: [] for e in self.ENGINES}
        self.sems = {}
        self.counts = {}

    def sem(self, name):
        if name not in self.sems:
            self.sems[name] = self.stack.enter_context(self.nc.semaphore(name))
            self.counts[name] = 0
        return self.sems[name]

    def emit(self, eng, fn, waits=(), incs=()):
        out = {}
        incs = list(incs)
        for s, n in incs:
            self.sem(s)
            self.counts[s] += n
            out[s] = self.counts[s]
        waits = [(s, v) for (s, v) in waits if v and v > 0]
        for s, _v in waits:
            self.sem(s)
        self.ops[eng].append((list(waits), fn, incs))
        return out

    def ld(self, fn, waits=()):
        return self.emit("sync", fn, waits, [("LD", 16)])["LD"]

    def st(self, fn, waits=()):
        return self.emit("sync", fn, waits, [("ST", 16)])["ST"]

    def gld(self, fn, waits=()):
        return self.emit("gpsimd", fn, waits, [("GD", 16)])["GD"]

    def stv(self, fn, waits=()):
        return self.emit("scalar", fn, waits, [("ST", 16)])["ST"]

    def te(self, fn, waits=(), tick=False):
        incs = [("TE", 1)] if tick else []
        r = self.emit("tensor", fn, waits, incs)
        return r.get("TE")

    def ve(self, fn, waits=(), tick=True):
        incs = [("VE", 1)] if tick else []
        r = self.emit("vector", fn, waits, incs)
        return r.get("VE")

    def sc(self, fn, waits=(), tick=True):
        incs = [("SC", 1)] if tick else []
        r = self.emit("scalar", fn, waits, incs)
        return r.get("SC")

    def gp(self, fn, waits=(), tick=True):
        incs = [("GP", 1)] if tick else []
        r = self.emit("gpsimd", fn, waits, incs)
        return r.get("GP")

    def c(self, name):
        return self.counts.get(name, 0)

    def fence(self):
        marks = {}
        for eng, sem in [("sync", "FS"), ("vector", "FV"), ("scalar", "FC"),
                         ("tensor", "FT"), ("gpsimd", "FG")]:
            marks[sem] = self.emit(eng, lambda e: e.nop(), [], [(sem, 1)])[sem]
        dma_waits = [(s, self.c(s)) for s in ("LD", "ST", "GD") if self.c(s) > 0]
        for eng in self.ENGINES:
            waits = [(s, v) for s, v in marks.items()] + dma_waits
            self.emit(eng, lambda e: e.nop(), waits, [])
        for k in [k for k in self.counts if k.startswith("_")]:
            del self.counts[k]

    def flush(self, block):
        self.build(block)
        self.ops = {e: [] for e in self.ENGINES}

    def build(self, block):
        for eng_name in self.ENGINES:
            ops = self.ops[eng_name]
            if not ops:
                continue

            def body(eng, ops=ops):
                done = {}
                for waits, fn, incs in ops:
                    for s, v in waits:
                        if done.get(s, -1) < v:
                            eng.wait_ge(self.sems[s], v)
                            done[s] = v
                    inst = fn(eng)
                    for s, n in incs:
                        inst = inst.then_inc(self.sems[s], n)

            getattr(block, eng_name)(body)


def ceil_div(a, b):
    return (a + b - 1) // b


def bcast_row(p, psum, dst, src_row, ones_col, n, waits=(), npart=128):
    key = "_bcfree%d" % id(psum)
    for c0 in range(0, n, 512):
        w = min(512, n - c0)
        t = p.te(lambda e, c0=c0, w=w: e.matmul(
            psum[0:npart, c0:c0 + w], ones_col[0:1, 0:npart], src_row[0:1, c0:c0 + w],
            start=True, stop=True),
            waits=list(waits) + [("VE", p.counts.get(key, 0))], tick=(c0 + 512 >= n))
    v = p.ve(lambda e: e.tensor_copy(dst[0:npart, 0:n], psum[0:npart, 0:n]),
             waits=[("TE", p.c("TE"))])
    p.counts[key] = v
    return v


def emit_ln_tile(p, dst, xa, scratch, stat, eps_ap, g_ap=None, b_ap=None, waits=(),
                 out_dtype_bf=None):
    c_sc = p.sc(lambda e: e.activation(scratch[:, 0:D], xa[:, 0:D], AF.Square,
                                              accum_out=stat[:, 0:1]), waits=waits)
    c_ve = p.ve(lambda e: e.reduce_sum(stat[:, 1:2], xa[:, 0:D], axis=AX.X),
                waits=waits)
    c_ve = p.ve(lambda e: e.tensor_scalar(stat[:, 2:3], stat[:, 1:2], 1.0 / D, None, ALU.mult),
                waits=[("VE", c_ve)])
    c_sc2 = p.sc(lambda e: e.activation(stat[:, 3:4], stat[:, 2:3], AF.Square),
                 waits=[("VE", c_ve)])
    c_ve = p.ve(lambda e: e.tensor_scalar(stat[:, 4:5], stat[:, 0:1], 1.0 / D, None, ALU.mult),
                waits=[("SC", c_sc)])
    c_ve = p.ve(lambda e: e.tensor_tensor(stat[:, 4:5], stat[:, 4:5], stat[:, 3:4], ALU.subtract),
                waits=[("SC", c_sc2), ("VE", c_ve)])
    c_sc3 = p.sc(lambda e: e.activation(stat[:, 5:6], stat[:, 4:5], AF.Sqrt, bias=eps_ap),
                 waits=[("VE", c_ve)])
    c_ve = p.ve(lambda e: e.reciprocal(stat[:, 6:7], stat[:, 5:6]),
                waits=[("SC", c_sc3)])
    c_ve = p.ve(lambda e: e.tensor_scalar(stat[:, 7:8], stat[:, 2:3], stat[:, 6:7], -1.0, ALU.mult, ALU.mult),
                waits=[("VE", c_ve)])
    tgt = dst if g_ap is None else scratch
    c_sc4 = p.sc(lambda e: e.activation(tgt[:, 0:D], xa[:, 0:D], AF.Identity,
                                               bias=stat[:, 7:8], scale=stat[:, 6:7]),
                 waits=[("VE", c_ve)])
    if g_ap is not None:
        c_ve = p.ve(lambda e: e.tensor_tensor(scratch[:, 0:D], scratch[:, 0:D], g_ap, ALU.mult),
                    waits=[("SC", c_sc4)])
        c_ve = p.ve(lambda e: e.tensor_tensor(dst[:, 0:D], scratch[:, 0:D], b_ap, ALU.add),
                    waits=[("VE", c_ve)])
        return {"VE": c_ve}
    return {"SC": c_sc4}


def emit_transpose_to_fm(p, dst_view_of_half, src_f32, pst, ident, waits=()):
    key = "_ptfree%d" % id(pst)
    for half in range(2):
        tecnt = None
        for i in range(4):
            c = half * 4 + i
            tecnt = p.te(lambda e, c=c, i=i: e.transpose(
                pst[:, i * 128:(i + 1) * 128], src_f32[:, c * 128:(c + 1) * 128], ident),
                waits=list(waits) + [("VE", p.counts.get(key, 0))],
                tick=(i == 3))
        v = p.ve(lambda e, half=half: e.tensor_copy(
            dst_view_of_half(half),
            pst[:, 0:512].rearrange("p (c t) -> p c t", t=128)),
            waits=[("TE", tecnt)])
        p.counts[key] = v
    return p.c("VE")


class WeightLoader:

    def __init__(self, p, stg, stg_bytes_cols):
        self.p = p
        self.stg = stg
        self.cols = stg_bytes_cols
        self.busy_until = 0

    def load(self, dram_ap, dst_bf, dst_col0, ncols, extra_waits=()):
        p = self.p
        assert ncols <= self.cols
        shp = dram_ap.shape
        if len(shp) == 3:
            stg_view = self.stg[:, 0:ncols].rearrange("p (a n) -> p a n", n=shp[2])
        else:
            stg_view = self.stg[:, 0:ncols]
        ld = p.ld(lambda e: e.dma_start(out=stg_view, in_=dram_ap),
                  waits=[("GP", self.busy_until)] + list(extra_waits))
        g = p.gp(lambda e: e.tensor_copy(dst_bf[:, dst_col0:dst_col0 + ncols],
                                                self.stg[:, 0:ncols]),
                 waits=[("LD", ld)])
        self.busy_until = g
        return g


def emit_wconv(nc, p, stgs, bscs, jobs, tag, via_gp=False):
    n = len(stgs)
    emit_ld = p.gld if via_gp else p.ld
    ldsem = "GD" if via_gp else "LD"
    for i, (src, dst, ncols) in enumerate(jobs):
        bi = i % n
        stg_view = (stgs[bi][:, 0:ncols] if len(src.shape) == 2 else
                    stgs[bi][:, 0:ncols].rearrange("p (a n) -> p a n", n=src.shape[-1]))
        ld = emit_ld(lambda e, src=src, sv=stg_view: e.dma_start(out=sv, in_=src),
                     waits=[("GP", p.counts.get("_wcg%s%d" % (tag, bi), 0))])
        g = p.gp(lambda e, bi=bi, ncols=ncols: e.tensor_copy(
            bscs[bi][:, 0:ncols], stgs[bi][:, 0:ncols]),
            waits=[(ldsem, ld), ("GD", p.counts.get("_wcd%s%d" % (tag, bi), 0))])
        p.counts["_wcg%s%d" % (tag, bi)] = g
        st = p.gld(lambda e, dst=dst, bi=bi, ncols=ncols: e.dma_start(
            out=dst, in_=bscs[bi][:, 0:ncols].rearrange("p (a n) -> p a n", n=dst.shape[-1])
            if len(dst.shape) == 3 else bscs[bi][:, 0:ncols]),
            waits=[("GP", g)])
        p.counts["_wcd%s%d" % (tag, bi)] = st
    return p.c("GD")



def _f(x):
    return np.ascontiguousarray(x, dtype=np.float32)


def make_consts():
    c = {}
    c["ident"] = np.eye(128, dtype=np.float32)
    c["identb"] = np.eye(128, dtype=np.float32)
    c["ones_row"] = np.ones((1, 128), dtype=np.float32)
    hd = 64
    inv_freq = 1.0 / (10000.0 ** (np.arange(0, hd, 2, dtype=np.float32) / hd))
    t = np.arange(N_TOK, dtype=np.float32)
    freqs = np.einsum("i,j->ij", t, inv_freq)
    emb = np.concatenate([freqs, freqs], axis=1)
    cosT = np.cos(emb).T.astype(np.float32)
    sinT = np.sin(emb).T.astype(np.float32)
    c["cosF"] = np.tile(cosT, (2, 1))
    c["sinF"] = np.tile(sinT, (2, 1))
    return c


def per_part(v, width=128):
    v = np.asarray(v, dtype=np.float32)
    n = v.shape[0] // width
    return np.ascontiguousarray(v.reshape(n, width).T)


def emit_l0_ffn_half(nc, p, io):
    w1scr_d = io["w1scr_d"]
    out1_d = io["out1_d"]
    x_d = io["x_d"]
    w1_d = io["w1_d"]
    w2_d = io["w2_d"]
    b1_d = io["b1_d"]
    b2_d = io["b2_d"]
    g2_d = io["g2_d"]
    bb2_d = io["bb2_d"]
    ident_d = io["ident_d"]
    ones_d = io["ones_d"]
    x1_d = io["x1_d"]


    stack = ExitStack()
    with stack:
        sb = lambda name, shape, dt=F32: stack.enter_context(nc.sbuf_tensor(name, shape, dt, allow_name_mangling=True))
        ps = lambda name, shape: stack.enter_context(nc.psum_tensor(name, shape, F32, allow_name_mangling=True))
        W1S = [sb("W1S0", [128, 8 * 128], BF16), sb("W1S1", [128, 8 * 128], BF16)]
        W2B = sb("W2B", [128, 32 * 1024], BF16)
        STG = sb("STG", [128, 2048], F32)
        OC = sb("OC", [128, 4 * 1024], F32)
        XT = sb("XT", [128, 8 * 512], BF16)
        HFM = sb("HFM", [128, 32 * 512], BF16)
        B1 = sb("B1", [128, 32], F32)
        B2T = sb("B2T", [128, 1024], F32)
        G2T = sb("G2T", [128, 1024], F32)
        BB2T = sb("BB2T", [128, 1024], F32)
        IDENT = sb("IDENT", [128, 128], F32)
        ONESR = sb("ONESR", [1, 128], F32)
        XTILE = sb("XTILE", [128, 1024], F32)
        ES = sb("ES", [128, 2 * 512], F32)
        ES2 = sb("ES2", [128, 2 * 512], F32)
        YS = sb("YS", [128, 1024], F32)
        LNS = sb("LNS", [128, 1024], F32)
        STAT = sb("STAT", [128, 16], F32)
        CONST = sb("CONST", [128, 4], F32)
        PT = ps("PT", [128, 512])
        PH = [ps("PH0", [128, 512]), ps("PH1", [128, 512])]
        PY = [ps("PY0", [128, 1024]), ps("PY1", [128, 1024])]
        PB = PY[0]

        p.ve(lambda e: e.memset(CONST[:, 0:1], 1e-6))
        p.ve(lambda e: e.memset(CONST[:, 1:2], 1e-5))
        p.ld(lambda e: e.dma_start(out=IDENT[:, :], in_=ident_d))
        p.ld(lambda e: e.dma_start(out=ONESR[:, :], in_=ones_d))
        p.ld(lambda e: e.dma_start(out=B1[:, :], in_=b1_d))
        ROW = sb("ROW", [1, 3 * 1024], F32)
        p.ld(lambda e: e.dma_start(out=ROW[0:1, 0:1024], in_=b2_d))
        p.ld(lambda e: e.dma_start(out=ROW[0:1, 1024:2048], in_=g2_d))
        ldc = p.ld(lambda e: e.dma_start(out=ROW[0:1, 2048:3072], in_=bb2_d))
        bcast_row(p, PB, B2T, ROW[0:1, 0:1024], ONESR[0:1, :], 1024, waits=[("LD", ldc)])
        bcast_row(p, PB, G2T, ROW[0:1, 1024:2048], ONESR[0:1, :], 1024, waits=[("LD", ldc)])
        bcast_row(p, PB, BB2T, ROW[0:1, 2048:3072], ONESR[0:1, :], 1024, waits=[("LD", ldc)])

        wl = WeightLoader(p, STG, 2048)
        for c in range(16):
            wl.load(w2_d[c * 256:(c + 1) * 256, :].rearrange("(a p) n -> p a n", p=128),
                    W2B, c * 2048, 2048)
        w_ready = p.c("GP")
        BSC1 = sb("BSC1", [128, 1024], BF16)
        BSC2 = sb("BSC2", [128, 1024], BF16)
        STG2 = sb("STG2", [128, 1024], F32)
        w1jobs = [(w1_d[:, dffc * 128:(dffc + 1) * 128].rearrange("(c p) n -> p c n", p=128),
                   w1scr_d[dffc].rearrange("c p f -> p c f"), 1024) for dffc in range(32)]
        p.counts["_wcga0"] = wl.busy_until
        emit_wconv(nc, p, [STG, STG2], [BSC1, BSC2], w1jobs, "a")
        w1scr_ready = p.c("GD")

        def load_w1(dffc, buf):
            ld = p.ld(lambda e, dffc=dffc, buf=buf: e.dma_start(
                out=W1S[buf][:, :].rearrange("p (c f) -> p c f", f=128),
                in_=w1scr_d[dffc].rearrange("c p f -> p c f")),
                waits=[("GD", w1scr_ready), ("TE", p.counts.get("_w1used%d" % buf, 0))])
            return ld

        n_chunk = 8
        XTV = XT[:, :].rearrange("p (c t) -> p c t", t=512)
        for ch in range(n_chunk):
            t0 = ch * 512
            lo = p.ld(lambda e, t0=t0: e.dma_start(out=OC[:, :].rearrange("p (s n) -> p s n", n=1024),
                                                        in_=out1_d[t0:t0 + 512, :].rearrange("(s p) n -> p s n", p=128)),
                      waits=[("TE", p.counts.get("_octe", 0)), ("VE", p.counts.get("_ocve", 0))])
            for sub in range(4):
                emit_transpose_to_fm(
                    p, lambda half, sub=sub: XTV[:, :, sub * 128:sub * 128 + 128][:, half * 4:half * 4 + 4, :],
                    OC[:, sub * 1024:(sub + 1) * 1024], PT, IDENT[:, :], waits=[("LD", lo)])
            p.counts["_octe"] = p.c("TE")
            xt_ready = p.c("VE")
            w1g = [None, None]
            w1g[0] = load_w1(0, 0)
            w1g[1] = load_w1(1, 1)
            for dffc in range(32):
                pb = dffc % 2
                eb = pb * 512
                for kc in range(8):
                    p.te(lambda e, pb=pb, kc=kc: e.matmul(
                        PH[pb][:, :],
                        W1S[pb][:, kc * 128:(kc + 1) * 128],
                        XT[:, kc * 512:(kc + 1) * 512],
                        start=(kc == 0), stop=(kc == 7)),
                        waits=[("LD", w1g[pb]), ("GP", w_ready), ("VE", xt_ready),
                               ("SC", p.counts.get("_phfree%d" % pb, 0))],
                        tick=(kc == 7))
                mmc = p.c("TE")
                p.counts["_w1used%d" % pb] = mmc
                if dffc + 2 < 32:
                    w1g[pb] = load_w1(dffc + 2, pb)
                s1 = p.sc(lambda e, pb=pb, eb=eb, dffc=dffc: e.activation(
                    ES[:, eb:eb + 512], PH[pb][:, :], AF.Identity, bias=B1[:, dffc:dffc + 1]),
                    waits=[("TE", mmc), ("VE", p.counts.get("_esfree%d" % pb, 0))])
                p.counts["_phfree%d" % pb] = s1
                v1 = p.ve(lambda e, eb=eb: e.tensor_scalar(
                    ES2[:, eb:eb + 512], ES[:, eb:eb + 512], 0.0, None, ALU.min),
                    waits=[("SC", s1)])
                s2 = p.sc(lambda e, eb=eb: e.activation(
                    ES2[:, eb:eb + 512], ES2[:, eb:eb + 512], AF.Exp),
                    waits=[("VE", v1)])
                v2 = p.ve(lambda e, eb=eb: e.tensor_scalar(
                    ES[:, eb:eb + 512], ES[:, eb:eb + 512], 0.0, -1.0, ALU.max, ALU.add),
                    waits=[("SC", s1), ("VE", v1)])
                vh = p.ve(lambda e, eb=eb, dffc=dffc: e.tensor_tensor(
                    HFM[:, dffc * 512:(dffc + 1) * 512], ES2[:, eb:eb + 512], ES[:, eb:eb + 512], ALU.add),
                    waits=[("SC", s2), ("VE", v2)])
                p.counts["_esfree%d" % pb] = vh
            h_ready = p.c("VE")
            for sub in range(4):
                pb = sub % 2
                for dffc in range(32):
                    for nh in range(2):
                        p.te(lambda e, pb=pb, dffc=dffc, nh=nh, sub=sub: e.matmul(
                            PY[pb][:, nh * 512:(nh + 1) * 512],
                            HFM[:, dffc * 512 + sub * 128:dffc * 512 + (sub + 1) * 128],
                            W2B[:, dffc * 1024 + nh * 512:dffc * 1024 + (nh + 1) * 512],
                            start=(dffc == 0), stop=(dffc == 31)),
                            waits=[("VE", h_ready), ("SC", p.counts.get("_pyfree%d" % pb, 0))],
                            tick=(dffc == 31 and nh == 1))
                mmy = p.c("TE")
                lx = p.ld(lambda e, t0=t0, sub=sub: e.dma_start(
                    out=XTILE[:, :], in_=x_d[t0 + sub * 128:t0 + sub * 128 + 128, :]),
                    waits=[("VE", p.counts.get("_xtile_free", 0))])
                sy = p.sc(lambda e, pb=pb: e.activation(
                    YS[:, :], PY[pb][:, :], AF.Identity),
                    waits=[("TE", mmy), ("ST", p.counts.get("_ysfree", 0))])
                p.counts["_pyfree%d" % pb] = sy
                vy = p.ve(lambda e: e.tensor_tensor(YS[:, :], YS[:, :], B2T[:, :], ALU.add),
                          waits=[("SC", sy)])
                vy = p.ve(lambda e, sub=sub: e.tensor_tensor(
                    YS[:, :], YS[:, :], OC[:, sub * 1024:(sub + 1) * 1024], ALU.add),
                    waits=[("VE", vy)])
                p.counts["_ocve"] = vy
                r = emit_ln_tile(p, YS, YS, LNS, STAT, CONST[:, 0:1], g_ap=G2T[:, :], b_ap=BB2T[:, :],
                                 waits=[("VE", vy)])
                vv = p.ve(lambda e: e.tensor_tensor(YS[:, :], YS[:, :], XTILE[:, :], ALU.add),
                          waits=[("LD", lx), ("VE", r.get("VE", 0)), ("SC", r.get("SC", 0))])
                p.counts["_xtile_free"] = vv
                r2 = emit_ln_tile(p, YS, YS, LNS, STAT, CONST[:, 1:2], waits=[("VE", vv)])
                p.counts["_ysfree"] = p.stv(
                    lambda e, t0=t0, sub=sub: e.dma_start(
                        out=x1_d[t0 + sub * 128:t0 + sub * 128 + 128, :], in_=YS[:, :]),
                    waits=[("SC", r2.get("SC", 0)), ("VE", r2.get("VE", 0))])
        p.emit("sync", lambda e: e.nop(), waits=[("ST", p.c("ST"))])

        p.fence()
        with nc.Block() as block:
            p.flush(block)


def build_l0_ffn_half():
    nc = bass.Bass("TRN2", target_bir_lowering=False)
    out1_d = nc.dram_tensor("out1", [N_TOK, D], F32, kind="ExternalInput").ap()
    x_d = nc.dram_tensor("x", [N_TOK, D], F32, kind="ExternalInput").ap()
    w1_d = nc.dram_tensor("W1", [D, 4096], F32, kind="ExternalInput").ap()
    w2_d = nc.dram_tensor("W2", [4096, D], F32, kind="ExternalInput").ap()
    b1_d = nc.dram_tensor("b1pt", [128, 32], F32, kind="ExternalInput").ap()
    b2_d = nc.dram_tensor("b2row", [1, D], F32, kind="ExternalInput").ap()
    g2_d = nc.dram_tensor("g2row", [1, D], F32, kind="ExternalInput").ap()
    bb2_d = nc.dram_tensor("bb2row", [1, D], F32, kind="ExternalInput").ap()
    ident_d = nc.dram_tensor("ident", [128, 128], F32, kind="ExternalInput").ap()
    ones_d = nc.dram_tensor("ones_row", [1, 128], F32, kind="ExternalInput").ap()
    x1_d = nc.dram_tensor("x1", [N_TOK, D], F32, kind="ExternalOutput").ap()
    w1scr_d = nc.dram_tensor("w1scr", [32, 8, 128, 128], BF16)
    io = dict(out1_d=out1_d, x_d=x_d, w1_d=w1_d, w2_d=w2_d, b1_d=b1_d, b2_d=b2_d, g2_d=g2_d, bb2_d=bb2_d, ident_d=ident_d, ones_d=ones_d, x1_d=x1_d, w1scr_d=w1scr_d)
    with ExitStack() as _st:
        p = Prog(nc, _st)
        emit_l0_ffn_half(nc, p, io)
    return nc


def run_l0_ffn_half(out1_full, x_full, W1, W2, b1, b2, g2, bb2, n_launch_counter=None):
    consts = make_consts()
    nc = build_l0_ffn_half()
    in_maps = []
    for c in range(8):
        in_maps.append({
            "out1": _f(out1_full[c]), "x": _f(x_full[c]),
            "W1": _f(W1), "W2": _f(W2),
            "b1pt": per_part(b1), "b2row": _f(b2).reshape(1, D),
            "g2row": _f(g2).reshape(1, D), "bb2row": _f(bb2).reshape(1, D),
            "ident": consts["ident"], "ones_row": consts["ones_row"],
        })
    res = run_bass_kernel_spmd(nc, in_maps, core_ids=list(range(8)))
    return np.stack([np.asarray(r["x1"]) for r in res.results])


def emit_l0_moe(nc, p, io):
    x1_d = io["x1_d"]
    w1_d = io["w1_d"]
    w2_d = io["w2_d"]
    b1_d = io["b1_d"]
    b2_d = io["b2_d"]
    cen_d = io["cen_d"]
    crt_d = io["crt_d"]
    ident_d = io["ident_d"]
    ones_d = io["ones_d"]
    x2_d = io["x2_d"]


    stack = ExitStack()
    with stack:
        sb = lambda name, shape, dt=F32: stack.enter_context(nc.sbuf_tensor(name, shape, dt, allow_name_mangling=True))
        ps = lambda name, shape: stack.enter_context(nc.psum_tensor(name, shape, F32, allow_name_mangling=True))
        W1B = sb("W1B", [128, 8 * 4096], BF16)
        W2B = sb("W2B", [128, 8 * 4096], BF16)
        STG = sb("STG", [128, 2048], F32)
        X1C = sb("X1C", [128, 4 * 1024], F32)
        X1T = sb("X1T", [128, 8 * 512], BF16)
        HBF = sb("HBF", [128, 4 * 512], BF16)
        YS = sb("YSB", [128, 4 * 1024], F32)
        CRS = sb("CRS", [128, 8 * 16], BF16)
        CEN = sb("CEN", [8, 1024], F32)
        B2A = sb("B2A", [8, 1024], BF16)
        INVCB = sb("INVCB", [128, 8], F32)
        IDENT = sb("IDENT", [128, 128], F32)
        ONESR = sb("ONESR", [1, 128], F32)
        B1PT = sb("B1PT", [128, 32], F32)
        GATES = sb("GATES", [128, 4 * 8], F32)
        GTB = sb("GTB", [8, 4 * 128], BF16)
        SSV = sb("SSV", [128, 32], F32)
        ROWS = sb("ROWS", [8, 16], F32)
        LNS = sb("LNS", [128, 1024], F32)
        STAT = sb("STAT", [128, 32], F32)
        CONST = sb("CONST", [128, 4], F32)
        PT = ps("PT", [128, 512])
        PHE = [ps("PHE0", [128, 512]), ps("PHE1", [128, 512])]
        PYE = [ps("PYE0", [128, 1024]), ps("PYE1", [128, 1024])]
        PLOG = ps("PLOG", [128, 512])

        p.ve(lambda e: e.memset(CONST[:, 0:1], 1e-6))
        p.ve(lambda e: e.memset(CONST[:, 1:2], 1e-5))
        p.ve(lambda e: e.memset(CONST[:, 2:3], 1e-12))
        p.ld(lambda e: e.dma_start(out=IDENT[:, :], in_=ident_d))
        p.ld(lambda e: e.dma_start(out=ONESR[:, :], in_=ones_d))
        lcen = p.ld(lambda e: e.dma_start(out=CEN[:, :], in_=cen_d))
        p.ld(lambda e: e.dma_start(out=B1PT[:, :], in_=b1_d))
        wl = WeightLoader(p, STG, 2048)
        for c in range(8):
            wl.load(crt_d[c * 128:(c + 1) * 128, :], CRS, c * 16, 16)
        ldb2 = p.ld(lambda e: e.dma_start(out=STG[0:8, 0:1024], in_=b2_d[:, :]),
                    waits=[("GP", wl.busy_until)])
        wl.busy_until = p.gp(lambda e: e.tensor_copy(B2A[0:8, :], STG[0:8, 0:1024]),
                             waits=[("LD", ldb2)])
        for e in range(8):
            for c in range(4):
                wl.load(w1_d[e, c * 256:(c + 1) * 256, :].rearrange("(a p) n -> p a n", p=128),
                        W1B, e * 4096 + c * 1024, 1024)
        for e in range(8):
            for c in range(4):
                wl.load(w2_d[e, c * 128:(c + 1) * 128, :], W2B, e * 4096 + c * 1024, 1024)
        w_ready = p.c("GP")

        s0 = p.sc(lambda e: e.activation(LNS[0:8, 0:1024], CEN[:, :], AF.Square,
                                         accum_out=ROWS[0:8, 0:1]), waits=[("LD", lcen)])
        s1 = p.sc(lambda e: e.activation(ROWS[0:8, 1:2], ROWS[0:8, 0:1], AF.Sqrt),
                  waits=[("SC", s0)])
        v1 = p.ve(lambda e: e.tensor_scalar(ROWS[0:8, 2:3], ROWS[0:8, 1:2], 1e-12, None, ALU.max),
                  waits=[("SC", s1)])
        v1 = p.ve(lambda e: e.reciprocal(ROWS[0:8, 3:4], ROWS[0:8, 2:3]), waits=[("VE", v1)])
        t1 = p.te(lambda e: e.transpose(PT[0:1, 0:8], ROWS[0:8, 3:4], IDENT[0:8, 0:8]),
                  waits=[("VE", v1), ("LD", p.c("LD"))], tick=True)
        v1 = p.ve(lambda e: e.tensor_copy(SSV[0:1, 0:8], PT[0:1, 0:8]), waits=[("TE", t1)])
        bcast_row(p, PLOG, INVCB, SSV[0:1, 0:8], ONESR[0:1, :], 8, waits=[("VE", v1)])

        n_chunk = 8
        X1TV = X1T[:, :].rearrange("p (c t) -> p c t", t=512)
        for ch in range(n_chunk):
            t0 = ch * 512
            lo = p.ld(lambda e, t0=t0: e.dma_start(out=X1C[:, :].rearrange("p (s n) -> p s n", n=1024),
                                                   in_=x1_d[t0:t0 + 512, :].rearrange("(s p) n -> p s n", p=128)),
                      waits=[("TE", p.counts.get("_x1cte", 0)), ("VE", p.counts.get("_x1cve", 0)),
                             ("SC", p.counts.get("_x1csc", 0))])
            for sub in range(4):
                emit_transpose_to_fm(
                    p, lambda half, sub=sub: X1TV[:, :, sub * 128:sub * 128 + 128][:, half * 4:half * 4 + 4, :],
                    X1C[:, sub * 1024:(sub + 1) * 1024], PT, IDENT[:, :], waits=[("LD", lo)])
            p.counts["_x1cte"] = p.c("TE")
            xt_ready = p.c("VE")
            for sub in range(4):
                sq = p.sc(lambda e, sub=sub: e.activation(LNS[:, 0:1024], X1C[:, sub * 1024:(sub + 1) * 1024],
                                                          AF.Square, accum_out=SSV[:, sub * 4:sub * 4 + 1]),
                          waits=[("LD", lo), ("VE", p.counts.get("_lnsv", 0))])
                p.counts["_x1csc"] = sq
                sq = p.sc(lambda e, sub=sub: e.activation(SSV[:, sub * 4 + 1:sub * 4 + 2],
                                                          SSV[:, sub * 4:sub * 4 + 1], AF.Sqrt),
                          waits=[("SC", sq)])
                vq = p.ve(lambda e, sub=sub: e.tensor_scalar(SSV[:, sub * 4 + 1:sub * 4 + 2],
                                                             SSV[:, sub * 4 + 1:sub * 4 + 2],
                                                             CONST[:, 2:3], None, ALU.max),
                          waits=[("SC", sq)])
                vq = p.ve(lambda e, sub=sub: e.reciprocal(SSV[:, sub * 4 + 2:sub * 4 + 3],
                                                          SSV[:, sub * 4 + 1:sub * 4 + 2]),
                          waits=[("VE", vq)])
                for c in range(8):
                    p.te(lambda e, c=c, sub=sub: e.matmul(
                        PLOG[:, 0:16], X1TV[:, c, sub * 128:sub * 128 + 128],
                        CRS[:, c * 16:(c + 1) * 16],
                        start=(c == 0), stop=(c == 7)),
                        waits=[("GP", w_ready), ("VE", xt_ready),
                               ("VE", p.counts.get("_plogfree", 0))],
                        tick=(c == 7))
                tl = p.c("TE")
                vq = p.ve(lambda e: e.tensor_copy(SSV[:, 16:32], PLOG[:, 0:16]),
                          waits=[("TE", tl), ("VE", vq)])
                p.counts["_plogfree"] = vq
                vq = p.ve(lambda e, sub=sub: e.tensor_scalar(SSV[:, 16:24], SSV[:, 16:24],
                                                             SSV[:, sub * 4 + 2:sub * 4 + 3], None, ALU.mult),
                          waits=[("VE", vq)])
                vq = p.ve(lambda e: e.tensor_tensor(SSV[:, 16:24], SSV[:, 16:24], INVCB[:, 0:8], ALU.mult),
                          waits=[("VE", vq)])
                vq = p.ve(lambda e: e.tensor_tensor(SSV[:, 16:24], SSV[:, 16:24], SSV[:, 24:32], ALU.add),
                          waits=[("VE", vq)])
                vq = p.ve(lambda e: e.max(STAT[:, 0:8], SSV[:, 16:24]), waits=[("VE", vq)])
                vq = p.ve(lambda e: e.tensor_scalar(STAT[:, 8:16], SSV[:, 16:24], STAT[:, 1:2], None, ALU.is_ge),
                          waits=[("VE", vq)])
                vq = p.ve(lambda e: e.tensor_scalar(STAT[:, 1:2], STAT[:, 0:1], -1.0, None, ALU.mult),
                          waits=[("VE", vq)])
                se = p.sc(lambda e: e.activation(SSV[:, 16:24], SSV[:, 16:24], AF.Exp, bias=STAT[:, 1:2]),
                          waits=[("VE", vq)])
                vq = p.ve(lambda e: e.tensor_tensor(SSV[:, 16:24], SSV[:, 16:24], STAT[:, 8:16], ALU.mult),
                          waits=[("SC", se)])
                vq = p.ve(lambda e: e.reduce_sum(STAT[:, 2:3], SSV[:, 16:24], axis=AX.X),
                          waits=[("VE", vq)])
                vq = p.ve(lambda e: e.reciprocal(STAT[:, 3:4], STAT[:, 2:3]), waits=[("VE", vq)])
                vq = p.ve(lambda e, sub=sub: e.tensor_scalar(GATES[:, sub * 8:(sub + 1) * 8],
                                                             SSV[:, 16:24], STAT[:, 3:4], None, ALU.mult),
                          waits=[("VE", vq)])
                tg = p.te(lambda e, sub=sub: e.transpose(PT[0:8, 0:128], GATES[:, sub * 8:(sub + 1) * 8],
                                                         IDENT[:, :]),
                          waits=[("VE", vq), ("VE", p.counts.get("_ptfree%d" % id(PT), 0))], tick=True)
                vq = p.ve(lambda e, sub=sub: e.tensor_copy(GTB[0:8, sub * 128:(sub + 1) * 128], PT[0:8, 0:128]),
                          waits=[("TE", tg)])
                p.counts["_ptfree%d" % id(PT)] = vq
                tb = None
                for nh in range(2):
                    tb = p.te(lambda e, sub=sub, nh=nh: e.matmul(
                        PYE[sub % 2][:, nh * 512:(nh + 1) * 512],
                        GTB[0:8, sub * 128:(sub + 1) * 128],
                        B2A[0:8, nh * 512:(nh + 1) * 512],
                        start=True, stop=True),
                        waits=[("VE", vq), ("SC", p.counts.get("_pyefree%d" % (sub % 2), 0)),
                               ("VE", p.counts.get("_pyefree_v%d" % (sub % 2), 0))],
                        tick=(nh == 1))
                vy = p.ve(lambda e, sub=sub: e.tensor_copy(YS[:, sub * 1024:(sub + 1) * 1024],
                                                           PYE[sub % 2][:, :]),
                          waits=[("TE", tb), ("ST", p.counts.get("_ys_st%d" % sub, 0))])
                p.counts["_pyefree_v%d" % (sub % 2)] = vy
            gates_ready_ve = p.c("VE")
            for ex in range(8):
                for dffc in range(4):
                    pb = dffc % 2
                    for c in range(8):
                        p.te(lambda e, ex=ex, dffc=dffc, c=c, pb=pb: e.matmul(
                            PHE[pb][:, :],
                            W1B[:, ex * 4096 + c * 512 + dffc * 128:ex * 4096 + c * 512 + (dffc + 1) * 128],
                            X1T[:, c * 512:(c + 1) * 512],
                            start=(c == 0), stop=(c == 7)),
                            waits=[("GP", w_ready), ("VE", xt_ready),
                                   ("SC", p.counts.get("_phefree%d" % pb, 0))],
                            tick=(c == 7))
                    th = p.c("TE")
                    sg = p.sc(lambda e, ex=ex, dffc=dffc, pb=pb: e.activation(
                        HBF[:, dffc * 512:(dffc + 1) * 512], PHE[pb][:, :], AF.Gelu,
                        bias=B1PT[:, ex * 4 + dffc:ex * 4 + dffc + 1]),
                        waits=[("TE", th), ("TE", p.counts.get("_hbfused", 0))])
                    p.counts["_phefree%d" % pb] = sg
                for sub in range(4):
                    pb = sub % 2
                    for dffc in range(4):
                        for nh in range(2):
                            p.te(lambda e, ex=ex, dffc=dffc, nh=nh, sub=sub, pb=pb: e.matmul(
                                PYE[pb][:, nh * 512:(nh + 1) * 512],
                                HBF[:, dffc * 512 + sub * 128:dffc * 512 + (sub + 1) * 128],
                                W2B[:, ex * 4096 + dffc * 1024 + nh * 512:ex * 4096 + dffc * 1024 + (nh + 1) * 512],
                                start=(dffc == 0), stop=(dffc == 3)),
                                waits=[("SC", p.c("SC")), ("SC", p.counts.get("_pyefree%d" % pb, 0)),
                                       ("VE", p.counts.get("_pyefree_v%d" % pb, 0))],
                                tick=(dffc == 3 and nh == 1))
                    ty = p.c("TE")
                    sg = p.sc(lambda e, ex=ex, sub=sub, pb=pb: e.activation(
                        LNS[:, 0:1024], PYE[pb][:, :], AF.Identity,
                        scale=GATES[:, sub * 8 + ex:sub * 8 + ex + 1]),
                        waits=[("TE", ty), ("VE", p.counts.get("_lnsv", 0))])
                    p.counts["_pyefree%d" % pb] = sg
                    vy = p.ve(lambda e, sub=sub: e.tensor_tensor(
                        YS[:, sub * 1024:(sub + 1) * 1024], YS[:, sub * 1024:(sub + 1) * 1024],
                        LNS[:, 0:1024], ALU.add),
                        waits=[("SC", sg), ("VE", gates_ready_ve)])
                    p.counts["_lnsv"] = vy
                p.counts["_hbfused"] = p.c("TE")
            for sub in range(4):
                ysub = YS[:, sub * 1024:(sub + 1) * 1024]
                vv = p.ve(lambda e, ysub=ysub, sub=sub: e.tensor_tensor(
                    ysub, ysub, X1C[:, sub * 1024:(sub + 1) * 1024], ALU.add),
                    waits=[("VE", p.c("VE"))])
                p.counts["_x1cve"] = vv
                r2 = emit_ln_tile(p, ysub, ysub, LNS, STAT[:, 16:24], CONST[:, 1:2],
                                  waits=[("VE", vv)])
                p.counts["_lnsv"] = max(p.counts.get("_lnsv", 0), r2.get("VE", 0))
                p.counts["_ys_st%d" % sub] = p.stv(
                    lambda e, t0=t0, sub=sub, ysub=ysub: e.dma_start(
                        out=x2_d[t0 + sub * 128:t0 + sub * 128 + 128, :], in_=ysub),
                    waits=[("SC", r2.get("SC", 0)), ("VE", r2.get("VE", 0))])
        p.emit("sync", lambda e: e.nop(), waits=[("ST", p.c("ST"))])
        p.fence()
        with nc.Block() as block:
            p.flush(block)


def build_l0_moe():
    nc = bass.Bass("TRN2", target_bir_lowering=False)
    x1_d = nc.dram_tensor("x1", [N_TOK, D], F32, kind="ExternalInput").ap()
    w1_d = nc.dram_tensor("mw1", [8, D, 512], F32, kind="ExternalInput").ap()
    w2_d = nc.dram_tensor("mw2", [8, 512, D], F32, kind="ExternalInput").ap()
    b1_d = nc.dram_tensor("mb1pt", [128, 32], F32, kind="ExternalInput").ap()
    b2_d = nc.dram_tensor("mb2", [8, D], F32, kind="ExternalInput").ap()
    cen_d = nc.dram_tensor("cen", [8, D], F32, kind="ExternalInput").ap()
    crt_d = nc.dram_tensor("crt", [D, 16], F32, kind="ExternalInput").ap()
    ident_d = nc.dram_tensor("ident", [128, 128], F32, kind="ExternalInput").ap()
    ones_d = nc.dram_tensor("ones_row", [1, 128], F32, kind="ExternalInput").ap()
    x2_d = nc.dram_tensor("x2", [N_TOK, D], F32, kind="ExternalOutput").ap()
    io = dict(x1_d=x1_d, w1_d=w1_d, w2_d=w2_d, b1_d=b1_d, b2_d=b2_d, cen_d=cen_d, crt_d=crt_d, ident_d=ident_d, ones_d=ones_d, x2_d=x2_d)
    with ExitStack() as _st:
        p = Prog(nc, _st)
        emit_l0_moe(nc, p, io)
    return nc


def run_l0_moe(x1_full, w1, b1, w2, b2, centroids, w_route):
    consts = make_consts()
    nc = build_l0_moe()
    crt = np.ascontiguousarray(np.concatenate([np.asarray(centroids).T, np.asarray(w_route).T], axis=1), dtype=np.float32)
    b1pt = np.ascontiguousarray(np.asarray(b1, dtype=np.float32).reshape(8, 4, 128).transpose(2, 0, 1).reshape(128, 32))
    in_maps = []
    for c in range(8):
        in_maps.append({
            "x1": _f(x1_full[c]), "mw1": _f(w1), "mw2": _f(w2),
            "mb1pt": b1pt, "mb2": _f(b2), "cen": _f(centroids), "crt": crt,
            "ident": consts["ident"], "ones_row": consts["ones_row"],
        })
    res = run_bass_kernel_spmd(nc, in_maps, core_ids=list(range(8)))
    return np.stack([np.asarray(r["x2"]) for r in res.results])


def emit_l1_moe(nc, p, io):
    x3_d = io["x3_d"]
    w1_d = io["w1_d"]
    w3_d = io["w3_d"]
    w2_d = io["w2_d"]
    b1_d = io["b1_d"]
    b3_d = io["b3_d"]
    b2_d = io["b2_d"]
    gw_d = io["gw_d"]
    gb_d = io["gb_d"]
    ident_d = io["ident_d"]
    ones_d = io["ones_d"]
    out_d = io["out_d"]
    w13s = io["w13s"]
    w2s = io["w2s"]
    DFF = 1408
    NE = 9


    stack = ExitStack()
    with stack:
        sb = lambda name, shape, dt=F32: stack.enter_context(nc.sbuf_tensor(name, shape, dt, allow_name_mangling=True))
        ps = lambda name, shape: stack.enter_context(nc.psum_tensor(name, shape, F32, allow_name_mangling=True))
        STG = sb("STG", [128, 1536], F32)
        BSC = sb("BSC", [128, 1536], BF16)
        X3C = sb("X3C", [128, 4 * 1024], F32)
        X3T = sb("X3T", [128, 8 * 1024], BF16)
        HBF = sb("HBF", [128, 11 * 1024], BF16)
        YS = sb("YSB", [128, 8 * 1024], F32)
        W2E = sb("W2E", [128, 11 * 1024], BF16)
        W13C = [sb("W13C%d" % _i, [128, 2 * 8 * 128], BF16) for _i in range(4)]
        GWB = sb("GWB", [128, 8 * 8], BF16)
        GBC = sb("GBC", [128, 8], F32)
        B1PT = sb("B1PT", [128, NE * 11], F32)
        B3PT = sb("B3PT", [128, NE * 11], F32)
        B2P = sb("B2P", [128, 3 * 1024], F32)
        RW = sb("RW", [128, 16 * 8], F32)
        SSV = sb("SSV", [128, 48], F32)
        ES = sb("ES", [128, 512], F32)
        ES2 = sb("ES2", [128, 512], F32)
        LNS = sb("LNS", [128, 1024], F32)
        STAT = sb("STAT", [128, 32], F32)
        CONST = sb("CONST", [128, 4], F32)
        IDENT = sb("IDENT", [128, 128], F32)
        ONESR = sb("ONESR", [1, 128], F32)
        ONESB = sb("ONESB", [128, 128], F32)
        PT = ps("PT", [128, 512])
        PH1 = ps("PH1", [128, 512])
        PH3 = ps("PH3", [128, 512])
        PY = [ps("PY0", [128, 1024]), ps("PY1", [128, 1024])]
        PLOG = ps("PLOG", [128, 512])

        p.ve(lambda e: e.memset(CONST[:, 1:2], 1e-5))
        ones_ready = p.ve(lambda e: e.memset(ONESB[:, :], 1.0))
        p.ld(lambda e: e.dma_start(out=IDENT[:, :], in_=ident_d))
        p.ld(lambda e: e.dma_start(out=ONESR[:, :], in_=ones_d))
        p.ld(lambda e: e.dma_start(out=B1PT[:, :], in_=b1_d))
        p.ld(lambda e: e.dma_start(out=B3PT[:, :], in_=b3_d))
        p.ld(lambda e: e.dma_start(out=B2P[:, :], in_=b2_d))
        lgb = p.ld(lambda e: e.dma_start(out=SSV[0:1, 0:8], in_=gb_d))
        bcast_row(p, PLOG, GBC, SSV[0:1, 0:8], ONESR[0:1, :], 8, waits=[("LD", lgb)])
        wl = WeightLoader(p, STG, 1536)
        for c in range(8):
            wl.load(gw_d[c * 128:(c + 1) * 128, :], GWB, c * 8, 8)
        gw_ready = p.c("GP")

        for e in (range(NE) if not io.get("preconverted") else []):
            for c in range(8):
                ldw = p.ld(lambda e_, e=e, c=c: e_.dma_start(
                    out=STG[:, 0:1408], in_=w1_d[e, c * 128:(c + 1) * 128, :]),
                    waits=[("GD", p.counts.get("_bscfree", 0))])
                g = p.gp(lambda e_: e_.tensor_copy(BSC[:, 0:1408], STG[:, 0:1408]),
                         waits=[("LD", ldw), ("GD", p.counts.get("_bscfree", 0))])
                st = p.gld(lambda e_, e=e, c=c: e_.dma_start(
                    out=w13s[e, :, :, c * 128:(c + 1) * 128].rearrange("k p f -> p k f"),
                    in_=BSC[:, 0:1408].rearrange("p (k f) -> p k f", f=128)),
                    waits=[("GP", g)])
                p.counts["_bscfree"] = st
                ldw = p.ld(lambda e_, e=e, c=c: e_.dma_start(
                    out=STG[:, 0:1408], in_=w3_d[e, c * 128:(c + 1) * 128, :]),
                    waits=[("GD", p.counts.get("_bscfree", 0))])
                g = p.gp(lambda e_: e_.tensor_copy(BSC[:, 0:1408], STG[:, 0:1408]),
                         waits=[("LD", ldw), ("GD", p.counts.get("_bscfree", 0))])
                st = p.gld(lambda e_, e=e, c=c: e_.dma_start(
                    out=w13s[e, :, :, 1024 + c * 128:1024 + (c + 1) * 128].rearrange("k p f -> p k f"),
                    in_=BSC[:, 0:1408].rearrange("p (k f) -> p k f", f=128)),
                    waits=[("GP", g)])
                p.counts["_bscfree"] = st
            for k in range(11):
                ldw = p.ld(lambda e_, e=e, k=k: e_.dma_start(
                    out=STG[:, 0:1024], in_=w2_d[e, k * 128:(k + 1) * 128, :]),
                    waits=[("GD", p.counts.get("_bscfree", 0))])
                g = p.gp(lambda e_: e_.tensor_copy(BSC[:, 0:1024], STG[:, 0:1024]),
                         waits=[("LD", ldw), ("GD", p.counts.get("_bscfree", 0))])
                st = p.gld(lambda e_, e=e, k=k: e_.dma_start(
                    out=w2s[e, :, k * 1024:(k + 1) * 1024], in_=BSC[:, 0:1024]),
                    waits=[("GP", g)])
                p.counts["_bscfree"] = st
        scr_ready = p.c("GD")

        X3TV = X3T[:, :].rearrange("p (c t) -> p c t", t=1024)
        for half in range(4):
            h0 = half * 1024
            for qc in range(2):
                t0 = h0 + qc * 512
                lo = p.ld(lambda e, t0=t0: e.dma_start(
                    out=X3C[:, :].rearrange("p (s n) -> p s n", n=1024),
                    in_=x3_d[t0:t0 + 512, :].rearrange("(s p) n -> p s n", p=128)),
                    waits=[("TE", p.counts.get("_x3cte", 0)), ("VE", p.counts.get("_x3cve", 0))])
                for sub in range(4):
                    emit_transpose_to_fm(
                        p, lambda hh, sub=sub, qc=qc: X3TV[:, :, qc * 512 + sub * 128:qc * 512 + sub * 128 + 128][:, hh * 4:hh * 4 + 4, :],
                        X3C[:, sub * 1024:(sub + 1) * 1024], PT, IDENT[:, :], waits=[("LD", lo)])
                p.counts["_x3cte"] = p.c("TE")
                for sub in range(4):
                    st_i = qc * 4 + sub
                    for c in range(8):
                        p.te(lambda e, c=c, qc=qc, sub=sub: e.matmul(
                            PLOG[:, 0:8], X3TV[:, c, qc * 512 + sub * 128:qc * 512 + sub * 128 + 128],
                            GWB[:, c * 8:(c + 1) * 8],
                            start=(c == 0), stop=(c == 7)),
                            waits=[("GP", gw_ready), ("VE", p.c("VE")),
                                   ("VE", p.counts.get("_plogfree", 0))],
                            tick=(c == 7))
                    tl = p.c("TE")
                    vq = p.ve(lambda e: e.tensor_copy(SSV[:, 8:16], PLOG[:, 0:8]), waits=[("TE", tl)])
                    p.counts["_plogfree"] = vq
                    vq = p.ve(lambda e: e.reduce_max(STAT[:, 0:1], SSV[:, 8:16], axis=AX.X, negate=True),
                              waits=[("VE", vq)])
                    se = p.sc(lambda e: e.activation(SSV[:, 16:24], SSV[:, 8:16], AF.Exp, bias=STAT[:, 0:1]),
                              waits=[("VE", vq)])
                    vq = p.ve(lambda e: e.reduce_sum(STAT[:, 1:2], SSV[:, 16:24], axis=AX.X),
                              waits=[("SC", se)])
                    vq = p.ve(lambda e: e.reciprocal(STAT[:, 2:3], STAT[:, 1:2]), waits=[("VE", vq)])
                    vq = p.ve(lambda e: e.tensor_scalar(SSV[:, 16:24], SSV[:, 16:24], STAT[:, 2:3], None, ALU.mult),
                              waits=[("VE", vq)])
                    vq = p.ve(lambda e: e.tensor_tensor(SSV[:, 24:32], SSV[:, 16:24], GBC[:, 0:8], ALU.add),
                              waits=[("VE", vq)])
                    vq = p.ve(lambda e: e.max(STAT[:, 8:16], SSV[:, 24:32]), waits=[("VE", vq)])
                    vq = p.ve(lambda e: e.tensor_scalar(SSV[:, 32:40], SSV[:, 24:32], STAT[:, 9:10], None, ALU.is_ge),
                              waits=[("VE", vq)])
                    vq = p.ve(lambda e, st_i=st_i: e.tensor_tensor(RW[:, st_i * 8:(st_i + 1) * 8],
                                                                   SSV[:, 32:40], SSV[:, 16:24], ALU.mult),
                              waits=[("VE", vq)])
            xt_ready = p.c("VE")
            rw_ready = p.c("VE")
            vz = None
            for sub in range(8):
                vz = p.ve(lambda e, sub=sub: e.memset(YS[:, sub * 1024:(sub + 1) * 1024], 0.0),
                          waits=[("SC", p.counts.get("_ysread", 0)), ("ST", p.counts.get("_ysst", 0))])
            for ex in range(NE):
                for _h3 in range(3):
                    _c0, _c1 = _h3 * 4096, min((_h3 + 1) * 4096, 11 * 1024)
                    lw2 = p.ld(lambda e, ex=ex, _c0=_c0, _c1=_c1: e.dma_start(
                        out=W2E[:, _c0:_c1], in_=w2s[ex, :, _c0:_c1]),
                        waits=[("GD", scr_ready), ("TE", p.counts.get("_w2used", 0))])
                wg = [None, None, None, None]

                def load_w13(dffc, buf, ex=ex):
                    ld = p.ld(lambda e, dffc=dffc, buf=buf, ex=ex: e.dma_start(
                        out=W13C[buf][:, :], in_=w13s[ex, dffc, :, :]),
                        waits=[("GD", scr_ready), ("TE", p.counts.get("_w13used%d" % buf, 0))])
                    return ld

                for _i in range(4):
                    wg[_i] = load_w13(_i, _i)
                for dffc in range(11):
                    buf = dffc % 4
                    for qc in range(2):
                        for c in range(8):
                            p.te(lambda e, c=c, qc=qc, buf=buf: e.matmul(
                                PH1[:, :], W13C[buf][:, c * 128:(c + 1) * 128],
                                X3TV[:, c, qc * 512:(qc + 1) * 512],
                                start=(c == 0), stop=(c == 7)),
                                waits=[("LD", wg[buf]), ("VE", xt_ready),
                                       ("SC", p.counts.get("_ph1free", 0))])
                        for c in range(8):
                            p.te(lambda e, c=c, qc=qc, buf=buf: e.matmul(
                                PH3[:, :], W13C[buf][:, 1024 + c * 128:1024 + (c + 1) * 128],
                                X3TV[:, c, qc * 512:(qc + 1) * 512],
                                start=(c == 0), stop=(c == 7)),
                                waits=[("SC", p.counts.get("_ph3free", 0))],
                                tick=(c == 7))
                        th = p.c("TE")
                        s1 = p.sc(lambda e, ex=ex, dffc=dffc: e.activation(
                            ES[:, :], PH1[:, :], AF.Identity, bias=B1PT[:, ex * 11 + dffc:ex * 11 + dffc + 1]),
                            waits=[("TE", th), ("VE", p.counts.get("_esfreev", 0))])
                        p.counts["_ph1free"] = s1
                        s3 = p.sc(lambda e, ex=ex, dffc=dffc: e.activation(
                            ES2[:, :], PH3[:, :], AF.Identity, bias=B3PT[:, ex * 11 + dffc:ex * 11 + dffc + 1]),
                            waits=[("TE", th), ("VE", p.counts.get("_esfreev", 0))])
                        p.counts["_ph3free"] = s3
                        v1 = p.ve(lambda e: e.tensor_tensor(ES[:, :], ES[:, :], ES2[:, :], ALU.mult),
                                  waits=[("SC", s3), ("SC", s1)])
                        p.counts["_esfreev"] = v1
                        sh = p.sc(lambda e, dffc=dffc, qc=qc: e.activation(
                            HBF[:, dffc * 1024 + qc * 512:dffc * 1024 + (qc + 1) * 512], ES[:, :], AF.Silu),
                            waits=[("VE", v1), ("TE", p.counts.get("_hbfused", 0))])
                    p.counts["_w13used%d" % buf] = p.c("TE")
                    if dffc + 4 < 11:
                        wg[buf] = load_w13(dffc + 4, buf)
                for _i in range(4):
                    p.counts["_w13used%d" % _i] = p.c("TE")
                h_ready = p.c("SC")
                for sub in range(8):
                    pb = sub % 2
                    for dffc in range(11):
                        for nh in range(2):
                            p.te(lambda e, dffc=dffc, nh=nh, sub=sub, pb=pb: e.matmul(
                                PY[pb][:, nh * 512:(nh + 1) * 512],
                                HBF[:, dffc * 1024 + sub * 128:dffc * 1024 + (sub + 1) * 128],
                                W2E[:, dffc * 1024 + nh * 512:dffc * 1024 + (nh + 1) * 512],
                                start=(dffc == 0), stop=False),
                                waits=[("SC", h_ready), ("LD", lw2),
                                       ("SC", p.counts.get("_pyfree%d" % pb, 0)),
                                       ("VE", p.counts.get("_pyfreev%d" % pb, 0))])
                    bpart = 32 * (ex // 3)
                    bcol = (ex % 3) * 1024
                    for nh in range(2):
                        p.te(lambda e, nh=nh, sub=sub, pb=pb, bpart=bpart, bcol=bcol: e.matmul(
                            PY[pb][:, nh * 512:(nh + 1) * 512],
                            ONESB[bpart:bpart + 1, 0:128],
                            B2P[bpart:bpart + 1, bcol + nh * 512:bcol + (nh + 1) * 512],
                            start=False, stop=(nh == 1)),
                            waits=[("VE", ones_ready)], tick=(nh == 1))
                    ty = p.c("TE")
                    if ex < 8:
                        sg = p.sc(lambda e, ex=ex, sub=sub, pb=pb: e.activation(
                            LNS[:, 0:1024], PY[pb][:, :], AF.Identity,
                            scale=RW[:, sub * 8 + ex:sub * 8 + ex + 1]),
                            waits=[("TE", ty), ("VE", p.counts.get("_lnsv", 0)), ("VE", rw_ready)])
                    else:
                        sg = p.sc(lambda e, pb=pb: e.activation(
                            LNS[:, 0:1024], PY[pb][:, :], AF.Identity),
                            waits=[("TE", ty), ("VE", p.counts.get("_lnsv", 0))])
                    p.counts["_pyfree%d" % pb] = sg
                    vy = p.ve(lambda e, sub=sub: e.tensor_tensor(
                        YS[:, sub * 1024:(sub + 1) * 1024], YS[:, sub * 1024:(sub + 1) * 1024],
                        LNS[:, 0:1024], ALU.add),
                        waits=[("SC", sg), ("VE", vz)])
                    p.counts["_lnsv"] = vy
                    p.counts["_pyfreev%d" % pb] = vy
                p.counts["_hbfused"] = p.c("TE")
                p.counts["_w2used"] = p.c("TE")
            for sub in range(8):
                lx = p.ld(lambda e, h0=h0, sub=sub: e.dma_start(
                    out=X3C[:, 0:1024], in_=x3_d[h0 + sub * 128:h0 + sub * 128 + 128, :]),
                    waits=[("VE", p.counts.get("_x3cfree", 0)), ("TE", p.counts.get("_x3cte", 0))])
                ysub = YS[:, sub * 1024:(sub + 1) * 1024]
                vv = p.ve(lambda e, ysub=ysub: e.tensor_tensor(ysub, ysub, X3C[:, 0:1024], ALU.add),
                          waits=[("LD", lx), ("VE", p.c("VE"))])
                p.counts["_x3cfree"] = vv
                p.counts["_x3cve"] = vv
                r2 = emit_ln_tile(p, ysub, ysub, LNS, STAT[:, 16:24], CONST[:, 1:2],
                                  waits=[("VE", vv)])
                p.counts["_lnsv"] = max(p.counts.get("_lnsv", 0), vv)
                p.counts["_ysst"] = p.stv(
                    lambda e, h0=h0, sub=sub, ysub=ysub: e.dma_start(
                        out=out_d[h0 + sub * 128:h0 + sub * 128 + 128, :], in_=ysub),
                    waits=[("SC", r2.get("SC", 0)), ("VE", r2.get("VE", 0))])
            p.counts["_ysread"] = p.c("SC")
        p.emit("sync", lambda e: e.nop(), waits=[("ST", p.c("ST"))])
        p.fence()
        with nc.Block() as block:
            p.flush(block)


def build_l1_moe():
    nc = bass.Bass("TRN2", target_bir_lowering=False)
    x3_d = nc.dram_tensor("x3", [N_TOK, D], F32, kind="ExternalInput").ap()
    w1_d = nc.dram_tensor("fw1", [NE, D, DFF], F32, kind="ExternalInput").ap()
    w3_d = nc.dram_tensor("fw3", [NE, D, DFF], F32, kind="ExternalInput").ap()
    w2_d = nc.dram_tensor("fw2", [NE, DFF, D], F32, kind="ExternalInput").ap()
    b1_d = nc.dram_tensor("fb1pt", [128, NE * 11], F32, kind="ExternalInput").ap()
    b3_d = nc.dram_tensor("fb3pt", [128, NE * 11], F32, kind="ExternalInput").ap()
    b2_d = nc.dram_tensor("fb2p", [128, 3 * D], F32, kind="ExternalInput").ap()
    gw_d = nc.dram_tensor("fgwt", [D, 8], F32, kind="ExternalInput").ap()
    gb_d = nc.dram_tensor("fgbrow", [1, 8], F32, kind="ExternalInput").ap()
    ident_d = nc.dram_tensor("ident", [128, 128], F32, kind="ExternalInput").ap()
    ones_d = nc.dram_tensor("ones_row", [1, 128], F32, kind="ExternalInput").ap()
    out_d = nc.dram_tensor("out", [N_TOK, D], F32, kind="ExternalOutput").ap()
    w13s = nc.dram_tensor("w13scr", [NE, 11, 128, 2048], BF16)
    w2s = nc.dram_tensor("w2scr", [NE, 128, 11 * D], BF16)
    io = dict(x3_d=x3_d, w1_d=w1_d, w3_d=w3_d, w2_d=w2_d, b1_d=b1_d, b3_d=b3_d, b2_d=b2_d, gw_d=gw_d, gb_d=gb_d, ident_d=ident_d, ones_d=ones_d, out_d=out_d, w13s=w13s, w2s=w2s)
    with ExitStack() as _st:
        p = Prog(nc, _st)
        emit_l1_moe(nc, p, io)
    return nc


def run_l1_moe(x3_full, gate_w, gate_b, w1, b1, w3, b3, w2, b2, sw1, sb1, sw3, sb3, sw2, sb2):
    consts = make_consts()
    nc = build_l1_moe()
    NE = 9
    w1a = np.concatenate([w1, sw1[None]], 0)
    w3a = np.concatenate([w3, sw3[None]], 0)
    w2a = np.concatenate([w2, sw2[None]], 0)
    b1a = np.concatenate([b1, sb1[None]], 0)
    b3a = np.concatenate([b3, sb3[None]], 0)
    b2a = np.concatenate([b2, sb2[None]], 0)

    def pt11(b):
        out = np.zeros((128, NE * 11), np.float32)
        for e in range(NE):
            out[:, e * 11:(e + 1) * 11] = b[e].reshape(11, 128).T
        return np.ascontiguousarray(out)

    b2p = np.zeros((128, 3 * 1024), np.float32)
    for e in range(NE):
        b2p[32 * (e // 3), (e % 3) * 1024:(e % 3 + 1) * 1024] = b2a[e]
    in_maps = []
    for c in range(8):
        in_maps.append({
            "x3": _f(x3_full[c]), "fw1": _f(w1a), "fw3": _f(w3a), "fw2": _f(w2a),
            "fb1pt": pt11(b1a), "fb3pt": pt11(b3a), "fb2p": _f(b2p),
            "fgwt": _f(np.asarray(gate_w).T), "fgbrow": _f(gate_b).reshape(1, 8),
            "ident": consts["ident"], "ones_row": consts["ones_row"],
        })
    res = run_bass_kernel_spmd(nc, in_maps, core_ids=list(range(8)))
    return np.stack([np.asarray(r["out"]) for r in res.results])


def emit_l1_attn(nc, p, io):
    x2_d = io["x2_d"]
    wq_d = io["wq_d"]
    wk_d = io["wk_d"]
    wv_d = io["wv_d"]
    wo_d = io["wo_d"]
    bq_d = io["bq_d"]
    bk_d = io["bk_d"]
    bv_d = io["bv_d"]
    bo_d = io["bo_d"]
    cos_d = io["cos_d"]
    sin_d = io["sin_d"]
    rot_d = io["rot_d"]
    ind_d = io["ind_d"]
    ident_d = io["ident_d"]
    ones_d = io["ones_d"]
    x3_d = io["x3_d"]
    qsp = io["qsp"]
    wscr = io["wscr"]


    stack = ExitStack()
    with stack:
        sb = lambda name, shape, dt=F32: stack.enter_context(nc.sbuf_tensor(name, shape, dt, allow_name_mangling=True))
        ps = lambda name, shape: stack.enter_context(nc.psum_tensor(name, shape, F32, allow_name_mangling=True))
        KT = sb("KT", [128, 8 * N_TOK], BF16)
        VTM = sb("VTM", [128, 32 * 16 * 65], BF16)
        WB = sb("WB", [128, 8 * 1024], BF16)
        STG = sb("STG", [128, 1024], F32)
        BSC = sb("BSC", [128, 1024], BF16)
        X2TILE = sb("X2TILE", [128, 1024], F32)
        BB = sb("BB", [128, 1024], F32)
        RQK = sb("RQK", [2, 32], F32)
        NEGB = sb("NEGB", [128, 1], F32)
        LNS = sb("LNS", [128, 1024], F32)
        STAT = sb("STAT", [128, 32], F32)
        CONST = sb("CONST", [128, 4], F32)
        IDENT = sb("IDENT", [128, 128], F32)
        ONESR = sb("ONESR", [1, 128], F32)
        LNS2 = STG
        PTR = ps("PTR", [128, 512])
        PQK = ps("PQK", [128, 512])
        PV = ps("PV", [128, 1024])
        SPS1 = ps("SPS1", [128, 1024])
        OPS = [ps("OPS0", [65, 512]), ps("OPS1", [65, 512])]

        SCRH = [0]
        stackA = ExitStack()
        with stackA:
            sbA = lambda name, shape, dt=F32: stackA.enter_context(nc.sbuf_tensor(name, shape, dt, allow_name_mangling=True))
            X2T = sbA("X2T", [128, 8 * 512], BF16)
            QF = sbA("QF", [128, 512], F32)
            TMPA = sbA("TMPA", [128, 512], F32)
            TMPB = sbA("TMPB", [128, 512], F32)
            SQF = sbA("SQF", [128, 512], F32)
            QSTG = sbA("QSTG", [128, 512], BF16)
            COSC = sbA("COSC", [128, 512], F32)
            SINC = sbA("SINC", [128, 512], F32)
            ROTM = sbA("ROTM", [128, 128], F32)
            INDM = sbA("INDM", [128, 2], F32)
            BQPT = sbA("BQPT", [128, 8], F32)
            BKPT = sbA("BKPT", [128, 8], F32)
            PTRKEY = "_ptfree%d" % id(PTR)
            p.ve(lambda e: e.memset(CONST[:, 1:2], 1e-5))
            vtmv = VTM[:, :].rearrange("p (g c) -> p g c", c=65)
            p.ve(lambda e: e.memset(vtmv[:, :, 64:65], 1.0))
            p.ve(lambda e: e.memset(RQK[0:2, 0:32], 0.0))
            for d, dst in [(ident_d, IDENT), (ones_d, ONESR), (rot_d, ROTM), (ind_d, INDM),
                           (bq_d, BQPT), (bk_d, BKPT)]:
                p.ld(lambda e, d=d, dst=dst: e.dma_start(out=dst[:, :], in_=d))
            lbv = p.ld(lambda e: e.dma_start(out=LNS[0:1, 0:1024], in_=bv_d))
            bcast_row(p, PV, BB, LNS[0:1, 0:1024], ONESR[0:1, :], 1024, waits=[("LD", lbv)])
            bv_done = p.c("VE")
            p.counts["_pvfree"] = bv_done

            STGB = sbA("STGB", [128, 1024], F32)
            BSCB = sbA("BSCB", [128, 1024], BF16)
            jobs = [(wd[c * 128:(c + 1) * 128, :], wscr[wi, c, :, :], 1024)
                    for wi, wd in enumerate([wq_d, wk_d, wv_d, wo_d]) for c in range(8)]
            SCRH[0] = emit_wconv(nc, p, [STG, STGB], [BSC, BSCB], jobs, "a")
            if "mw1_d" in io:
                STGC = sbA("STGC", [128, 1408], F32)
                BSCC = sbA("BSCC", [128, 1408], BF16)
                mjobs = []
                for e_ in range(9):
                    for c in range(8):
                        mjobs.append((io["mw1_d"][e_, c * 128:(c + 1) * 128, :],
                                      io["mw13s"][e_, :, :, c * 128:(c + 1) * 128].rearrange("k p f -> p k f"), 1408))
                        mjobs.append((io["mw3_d"][e_, c * 128:(c + 1) * 128, :],
                                      io["mw13s"][e_, :, :, 1024 + c * 128:1024 + (c + 1) * 128].rearrange("k p f -> p k f"), 1408))
                    for k_ in range(11):
                        mjobs.append((io["mw2_d"][e_, k_ * 128:(k_ + 1) * 128, :],
                                      io["mw2s"][e_, :, k_ * 1024:(k_ + 1) * 1024], 1024))
                emit_wconv(nc, p, [STGC], [BSCC], mjobs, "m", via_gp=True)

            def load_wb(wi):
                return p.ld(lambda e, wi=wi: e.dma_start(
                    out=WB[:, :].rearrange("p (c n) -> p c n", n=1024),
                    in_=wscr[wi].rearrange("c p n -> p c n")),
                    waits=[("GD", SCRH[0]), ("TE", p.counts.get("_wbused", 0))])

            X2TV = X2T[:, :].rearrange("p (c t) -> p c t", t=512)
            for ch in range(8):
                t0 = ch * 512
                lc = p.ld(lambda e, t0=t0: e.dma_start(out=COSC[:, :], in_=cos_d[:, t0:t0 + 512]),
                          waits=[("VE", p.counts.get("_coscfree", 0))])
                lc = p.ld(lambda e, t0=t0: e.dma_start(out=SINC[:, :], in_=sin_d[:, t0:t0 + 512]))
                for sub in range(4):
                    lx = p.ld(lambda e, t0=t0, sub=sub: e.dma_start(
                        out=X2TILE[:, :], in_=x2_d[t0 + sub * 128:t0 + sub * 128 + 128, :]),
                        waits=[("TE", p.counts.get("_x2tfree", 0))])
                    emit_transpose_to_fm(
                        p, lambda hh, sub=sub: X2TV[:, :, sub * 128:sub * 128 + 128][:, hh * 4:hh * 4 + 4, :],
                        X2TILE[:, :], PTR, IDENT[:, :], waits=[("LD", lx)])
                    p.counts["_x2tfree"] = p.c("TE")
                xt_ready = p.c("VE")
                for which, bpt in [(0, BQPT), (1, BKPT)]:
                    wld = load_wb(which)
                    for j in range(8):
                        for c in range(8):
                            p.te(lambda e, j=j, c=c: e.matmul(
                                PQK[:, :], WB[:, c * 1024 + j * 128:c * 1024 + (j + 1) * 128],
                                X2TV[:, c, :], start=(c == 0), stop=(c == 7)),
                                waits=[("LD", wld), ("VE", xt_ready),
                                       ("SC", p.counts.get("_pqkfree", 0))],
                                tick=(c == 7))
                        tq = p.c("TE")
                        sev = p.sc(lambda e, j=j, bpt=bpt: e.activation(
                            QF[:, :], PQK[:, :], AF.Identity, bias=bpt[:, j:j + 1]),
                            waits=[("TE", tq), ("VE", p.counts.get("_qffree", 0))])
                        p.counts["_pqkfree"] = sev
                        ssq = p.sc(lambda e: e.activation(SQF[:, :], QF[:, :], AF.Square),
                                   waits=[("SC", sev), ("TE", p.counts.get("_sqffree", 0))])
                        tn = p.te(lambda e: e.matmul(PTR[0:2, 0:512], INDM[:, :], SQF[:, :],
                                                     start=True, stop=True),
                                  waits=[("SC", ssq), ("VE", p.counts.get(PTRKEY, 0))], tick=True)
                        p.counts["_sqffree"] = tn
                        vn = p.ve(lambda e: e.tensor_reduce(STAT[0:2, 0:1], PTR[0:2, 0:512],
                                                            axis=AX.X, op=ALU.max),
                                  waits=[("TE", tn)])
                        p.counts[PTRKEY] = vn
                        col = 16 * which + j
                        vn = p.ve(lambda e, col=col: e.tensor_tensor(
                            RQK[0:2, col:col + 1], RQK[0:2, col:col + 1], STAT[0:2, 0:1], ALU.max),
                            waits=[("VE", vn)])
                        tr = p.te(lambda e: e.matmul(PTR[:, :], ROTM[:, :], QF[:, :],
                                                     start=True, stop=True),
                                  waits=[("SC", sev), ("VE", p.counts.get(PTRKEY, 0))], tick=True)
                        va = p.ve(lambda e: e.tensor_tensor(TMPA[:, :], QF[:, :], COSC[:, :], ALU.mult),
                                  waits=[("SC", sev), ("LD", lc), ("VE", p.counts.get("_tmpafree", 0))])
                        p.counts["_qffree"] = va
                        vb = p.ve(lambda e: e.tensor_tensor(TMPB[:, :], PTR[:, :], SINC[:, :], ALU.mult),
                                  waits=[("TE", tr), ("LD", lc), ("VE", va)])
                        p.counts[PTRKEY] = vb
                        if which == 1:
                            vc = p.ve(lambda e, j=j, t0=t0: e.tensor_tensor(
                                KT[:, j * N_TOK + t0:j * N_TOK + t0 + 512], TMPA[:, :], TMPB[:, :], ALU.add),
                                waits=[("VE", vb)])
                        else:
                            vc = p.ve(lambda e: e.tensor_tensor(QSTG[:, :], TMPA[:, :], TMPB[:, :], ALU.add),
                                      waits=[("VE", vb), ("ST", p.counts.get("_qstgfree", 0))])
                            p.counts["_qstgfree"] = p.st(
                                lambda e, j=j, t0=t0: e.dma_start(
                                    out=qsp[j, :, t0:t0 + 512], in_=QSTG[:, :]),
                                waits=[("VE", vc)])
                        p.counts["_tmpafree"] = vc
                    p.counts["_wbused"] = p.c("TE")
                wld = load_wb(2)
                for sub in range(4):
                    for c in range(8):
                        for nh in range(2):
                            p.te(lambda e, sub=sub, c=c, nh=nh: e.matmul(
                                PV[:, nh * 512:(nh + 1) * 512],
                                X2TV[:, c, sub * 128:sub * 128 + 128],
                                WB[:, c * 1024 + nh * 512:c * 1024 + (nh + 1) * 512],
                                start=(c == 0), stop=(c == 7)),
                                waits=[("LD", wld), ("VE", xt_ready),
                                       ("VE", p.counts.get("_pvfree", 0))],
                                tick=(c == 7 and nh == 1))
                    tv = p.c("TE")
                    tile = ch * 4 + sub
                    vv = p.ve(lambda e, tile=tile: e.tensor_tensor(
                        vtmv[:, tile * 16:(tile + 1) * 16, 0:64],
                        PV[:, :].rearrange("p (h c) -> p h c", c=64),
                        BB[:, :].rearrange("p (h c) -> p h c", c=64), ALU.add),
                        waits=[("TE", tv), ("VE", bv_done)])
                    p.counts["_pvfree"] = vv
                p.counts["_wbused"] = p.c("TE")
            vq = p.ve(lambda e: e.tensor_reduce(RQK[0:2, 16:17], RQK[0:2, 0:8], axis=AX.X, op=ALU.max),
                      waits=[("VE", p.c("VE"))])
            vq = p.ve(lambda e: e.tensor_reduce(RQK[0:2, 17:18], RQK[0:2, 8:16], axis=AX.X, op=ALU.max),
                      waits=[("VE", vq)])
            tb = p.te(lambda e: e.transpose(PTR[0:2, 0:2], RQK[0:2, 16:18], IDENT[0:2, 0:2]),
                      waits=[("VE", vq), ("VE", p.counts.get(PTRKEY, 0))], tick=True)
            vq = p.ve(lambda e: e.tensor_copy(STAT[0:2, 0:2], PTR[0:2, 0:2]), waits=[("TE", tb)])
            p.counts[PTRKEY] = vq
            vq = p.ve(lambda e: e.tensor_reduce(STAT[0:2, 2:3], STAT[0:2, 0:2], axis=AX.X, op=ALU.max),
                      waits=[("VE", vq)])
            sq = p.sc(lambda e: e.activation(STAT[0:2, 3:4], STAT[0:2, 2:3], AF.Sqrt),
                      waits=[("VE", vq)])
            tb = p.te(lambda e: e.transpose(PTR[0:1, 0:2], STAT[0:2, 3:4], IDENT[0:2, 0:2]),
                      waits=[("SC", sq), ("VE", p.counts.get(PTRKEY, 0))], tick=True)
            vq = p.ve(lambda e: e.tensor_copy(STAT[0:1, 4:6], PTR[0:1, 0:2]), waits=[("TE", tb)])
            p.counts[PTRKEY] = vq
            vq = p.ve(lambda e: e.tensor_tensor(STAT[0:1, 6:7], STAT[0:1, 4:5], STAT[0:1, 5:6], ALU.mult),
                      waits=[("VE", vq)])
            vq = p.ve(lambda e: e.tensor_scalar(STAT[0:1, 7:8], STAT[0:1, 6:7], -0.125, None, ALU.mult),
                      waits=[("VE", vq)])
            tb = p.te(lambda e: e.matmul(PQK[:, 0:1], ONESR[0:1, :], STAT[0:1, 7:8],
                                         start=True, stop=True),
                      waits=[("VE", vq), ("SC", p.counts.get("_pqkfree", 0))], tick=True)
            vq = p.ve(lambda e: e.tensor_copy(NEGB[:, :], PQK[:, 0:1]), waits=[("TE", tb)])
            lbo = p.ld(lambda e: e.dma_start(out=LNS[0:1, 0:1024], in_=bo_d),
                       waits=[("VE", p.c("VE"))])
            for c0 in range(0, 1024, 512):
                tb = p.te(lambda e, c0=c0: e.matmul(PQK[:, 0:512], ONESR[0:1, :],
                                                    LNS[0:1, c0:c0 + 512], start=True, stop=True),
                          waits=[("LD", lbo), ("VE", p.counts.get("_bbk", 0))], tick=True)
                vq = p.ve(lambda e, c0=c0: e.tensor_copy(BB[:, c0:c0 + 512], PQK[:, 0:512]),
                          waits=[("TE", tb)])
                p.counts["_bbk"] = vq
            bo_done = p.c("VE")
            p.fence()
            with nc.Block() as blockA:
                p.flush(blockA)
        stackB = ExitStack()
        with stackB:
            sbB = lambda name, shape, dt=F32: stackB.enter_context(nc.sbuf_tensor(name, shape, dt, allow_name_mangling=True))
            QBD = [sbB("QBD0", [128, 512], BF16), sbB("QBD1", [128, 512], BF16)]
            PT16 = [sbB("PT16_0", [128, 1024], BF16), sbB("PT16_1", [128, 1024], BF16)]
            OA = sbB("OA", [64, 512], F32)
            RL = sbB("RL", [1, 512], F32)
            ATT = sbB("ATT", [128, 8 * 512], BF16)
            wld_o = load_wb(3)

            zq = p.ve(lambda e: e.memset(QBD[0][64:128, :], 0.0))
            zq = p.ve(lambda e: e.memset(QBD[1][0:64, :], 0.0))

            def qbd_load(h, q0, b):
                j = h // 2
                par = (h % 2) * 64
                l1 = p.ld(lambda e, j=j, par=par, q0=q0, b=b: e.dma_start(
                    out=QBD[b][par:par + 64, :], in_=qsp[j, par:par + 64, q0:q0 + 512]),
                    waits=[("TE", p.counts.get("_qbdfree%d" % b, 0)), ("VE", zq)])
                return l1

            qbl = {(0, 0): qbd_load(0, 0, 0)}
            sps = [PV, SPS1]
            for qc in range(8):
                q0 = qc * 512

                def emit_s_exp(h, kcp):
                    b = h % 2
                    par = (h % 2) * 64
                    sbk = kcp % 2
                    for kk in range(2):
                        kc = kcp * 2 + kk
                        p.te(lambda e, h=h, kc=kc, kk=kk, sbk=sbk, b=b: e.matmul(
                            sps[sbk][:, kk * 512:(kk + 1) * 512],
                            KT[:, (h // 2) * N_TOK + kc * 128:(h // 2) * N_TOK + (kc + 1) * 128],
                            QBD[b][:, :], start=True, stop=True),
                            waits=[("LD", qbl[(qc, h)]),
                                   ("SC", p.counts.get("_spsfree%d" % sbk, 0))],
                            tick=(kk == 1))
                    ts_ = p.c("TE")
                    se = p.sc(lambda e, sbk=sbk: e.activation(
                        PT16[sbk][:, :], sps[sbk][:, :], AF.Exp, bias=NEGB[:, 0:1], scale=0.125),
                        waits=[("TE", ts_), ("TE", p.counts.get("_pt16free%d" % sbk, 0))])
                    p.counts["_spsfree%d" % sbk] = se
                    return se

                def emit_av(h, kcp, se):
                    sbk = kcp % 2
                    ob = h % 2
                    for kk in range(2):
                        kc = kcp * 2 + kk
                        p.te(lambda e, h=h, kc=kc, kk=kk, sbk=sbk, ob=ob: e.matmul(
                            OPS[ob][:, :],
                            VTM[:, (kc * 16 + h) * 65:(kc * 16 + h) * 65 + 65],
                            PT16[sbk][:, kk * 512:(kk + 1) * 512],
                            start=(kc == 0), stop=(kc == 31)),
                            waits=[("SC", se), ("SC", p.counts.get("_opsfree%d" % ob, 0)),
                                   ("VE", p.counts.get("_opsfreev%d" % ob, 0))],
                            tick=(kk == 1))
                    p.counts["_pt16free%d" % sbk] = p.c("TE")

                def emit_norm(h, to_):
                    ob = h % 2
                    par = (h % 2) * 64
                    vr = p.ve(lambda e, ob=ob: e.reciprocal(RL[0:1, :], OPS[ob][64:65, :]),
                              waits=[("TE", to_), ("TE", p.counts.get("_rlfree", 0))])
                    so = p.sc(lambda e, ob=ob: e.activation(OA[0:64, :], OPS[ob][0:64, :], AF.Identity),
                              waits=[("TE", to_), ("VE", p.counts.get("_oafree", 0))])
                    p.counts["_opsfree%d" % ob] = so
                    tb2 = p.te(lambda e: e.matmul(PTR[0:64, :], ONESR[0:1, 0:64], RL[0:1, :],
                                                  start=True, stop=True),
                               waits=[("VE", vr), ("VE", p.counts.get(PTRKEY, 0))], tick=True)
                    p.counts["_rlfree"] = tb2
                    va = p.ve(lambda e, h=h, par=par: e.tensor_tensor(
                        ATT[par:par + 64, (h // 2) * 512:(h // 2) * 512 + 512],
                        OA[0:64, :], PTR[0:64, :], ALU.mult),
                        waits=[("TE", tb2), ("SC", so), ("TE", p.counts.get("_attfree", 0))])
                    p.counts[PTRKEY] = va
                    p.counts["_oafree"] = va
                    p.counts["_opsfreev%d" % ob] = va

                pend_norm = None
                for h in range(16):
                    b = h % 2
                    pend_av = None
                    for kcp in range(16):
                        se = emit_s_exp(h, kcp)
                        if pend_av is not None:
                            emit_av(h, pend_av[0], pend_av[1])
                        pend_av = (kcp, se)
                        if kcp == 0 and pend_norm is not None:
                            emit_norm(*pend_norm)
                            pend_norm = None
                    emit_av(h, pend_av[0], pend_av[1])
                    to_ = p.c("TE")
                    p.counts["_qbdfree%d" % b] = to_
                    if h + 1 < 16:
                        qbl[(qc, h + 1)] = qbd_load(h + 1, q0, (h + 1) % 2)
                    elif qc + 1 < 8:
                        qbl[(qc + 1, 0)] = qbd_load(0, (qc + 1) * 512, 0)
                    pend_norm = (h, to_)
                emit_norm(*pend_norm)
                att_ready = p.c("VE")
                for sub in range(4):
                    pw = [SPS1, PV][sub % 2]
                    for jc in range(8):
                        for nh in range(2):
                            p.te(lambda e, jc=jc, nh=nh, sub=sub, pw=pw: e.matmul(
                                pw[:, nh * 512:(nh + 1) * 512],
                                ATT[:, jc * 512 + sub * 128:jc * 512 + (sub + 1) * 128],
                                WB[:, jc * 1024 + nh * 512:jc * 1024 + (nh + 1) * 512],
                                start=(jc == 0), stop=(jc == 7)),
                                waits=[("VE", att_ready), ("LD", wld_o),
                                       ("SC", p.counts.get("_pwfree%d" % (sub % 2), 0)),
                                       ("SC", p.counts.get("_spsfree%d" % (1 - sub % 2), 0))],
                                tick=(jc == 7 and nh == 1))
                    tw = p.c("TE")
                    lx = p.ld(lambda e, q0=q0, sub=sub: e.dma_start(
                        out=X2TILE[:, :], in_=x2_d[q0 + sub * 128:q0 + sub * 128 + 128, :]),
                        waits=[("VE", p.counts.get("_x2resfree", 0))])
                    sy = p.sc(lambda e, pw=pw: e.activation(LNS[:, :], pw[:, :], AF.Identity),
                              waits=[("TE", tw), ("VE", p.counts.get("_lnsv", 0)),
                                     ("ST", p.counts.get("_lnst", 0))])
                    p.counts["_pwfree%d" % (sub % 2)] = sy
                    vy = p.ve(lambda e: e.tensor_tensor(LNS[:, :], LNS[:, :], BB[:, :], ALU.add),
                              waits=[("SC", sy), ("VE", bo_done)])
                    vy = p.ve(lambda e: e.tensor_tensor(LNS[:, :], LNS[:, :], X2TILE[:, :], ALU.add),
                              waits=[("VE", vy), ("LD", lx)])
                    p.counts["_x2resfree"] = vy
                    r2 = emit_ln_tile(p, LNS, LNS, LNS2, STAT[:, 16:24], CONST[:, 1:2],
                                      waits=[("VE", vy)])
                    p.counts["_lnsv"] = max(p.counts.get("_lnsv", 0), vy)
                    p.counts["_lnst"] = p.st(
                        lambda e, q0=q0, sub=sub: e.dma_start(
                            out=x3_d[q0 + sub * 128:q0 + sub * 128 + 128, :], in_=LNS[:, :]),
                        waits=[("SC", r2.get("SC", 0)), ("VE", r2.get("VE", 0))])
                p.counts["_attfree"] = p.c("TE")
                p.counts["_spsfree0"] = max(p.counts.get("_spsfree0", 0), p.counts.get("_pwfree1", 0))
                p.counts["_spsfree1"] = max(p.counts.get("_spsfree1", 0), p.counts.get("_pwfree0", 0))
            p.emit("sync", lambda e: e.nop(), waits=[("ST", p.c("ST"))])
            p.fence()
            with nc.Block() as blockB:
                p.flush(blockB)


def build_l1_attn():
    nc = bass.Bass("TRN2", target_bir_lowering=False)
    x2_d = nc.dram_tensor("x2", [N_TOK, D], F32, kind="ExternalInput").ap()
    wq_d = nc.dram_tensor("awqT", [D, D], F32, kind="ExternalInput").ap()
    wk_d = nc.dram_tensor("awkT", [D, D], F32, kind="ExternalInput").ap()
    wv_d = nc.dram_tensor("awvT", [D, D], F32, kind="ExternalInput").ap()
    wo_d = nc.dram_tensor("awoT", [D, D], F32, kind="ExternalInput").ap()
    bq_d = nc.dram_tensor("abqpt", [128, 8], F32, kind="ExternalInput").ap()
    bk_d = nc.dram_tensor("abkpt", [128, 8], F32, kind="ExternalInput").ap()
    bv_d = nc.dram_tensor("abvrow", [1, D], F32, kind="ExternalInput").ap()
    bo_d = nc.dram_tensor("aborow", [1, D], F32, kind="ExternalInput").ap()
    cos_d = nc.dram_tensor("cosFt", [128, N_TOK], F32, kind="ExternalInput").ap()
    sin_d = nc.dram_tensor("sinFt", [128, N_TOK], F32, kind="ExternalInput").ap()
    rot_d = nc.dram_tensor("rotM", [128, 128], F32, kind="ExternalInput").ap()
    ind_d = nc.dram_tensor("indM", [128, 2], F32, kind="ExternalInput").ap()
    ident_d = nc.dram_tensor("ident", [128, 128], F32, kind="ExternalInput").ap()
    ones_d = nc.dram_tensor("ones_row", [1, 128], F32, kind="ExternalInput").ap()
    x3_d = nc.dram_tensor("x3", [N_TOK, D], F32, kind="ExternalOutput").ap()
    qsp = nc.dram_tensor("qspill", [8, 128, N_TOK], BF16)
    wscr = nc.dram_tensor("wscr", [4, 8, 128, D], BF16)
    io = dict(x2_d=x2_d, wq_d=wq_d, wk_d=wk_d, wv_d=wv_d, wo_d=wo_d, bq_d=bq_d, bk_d=bk_d, bv_d=bv_d, bo_d=bo_d, cos_d=cos_d, sin_d=sin_d, rot_d=rot_d, ind_d=ind_d, ident_d=ident_d, ones_d=ones_d, x3_d=x3_d, qsp=qsp, wscr=wscr)
    with ExitStack() as _st:
        p = Prog(nc, _st)
        emit_l1_attn(nc, p, io)
    return nc


def run_l1_attn(x2_full, wq, bq, wk, bk, wv, bv, wo, bo):
    consts = make_consts()
    nc = build_l1_attn()
    rotm = np.zeros((128, 128), np.float32)
    for base in (0, 64):
        for dd in range(32):
            rotm[base + 32 + dd, base + dd] = -1.0
            rotm[base + dd, base + 32 + dd] = 1.0
    indm = np.zeros((128, 2), np.float32)
    indm[0:64, 0] = 1.0
    indm[64:128, 1] = 1.0
    in_maps = []
    for c in range(8):
        in_maps.append({
            "x2": _f(x2_full[c]),
            "awqT": _f(np.asarray(wq).T), "awkT": _f(np.asarray(wk).T),
            "awvT": _f(np.asarray(wv).T), "awoT": _f(np.asarray(wo).T),
            "abqpt": per_part(bq), "abkpt": per_part(bk),
            "abvrow": _f(bv).reshape(1, D), "aborow": _f(bo).reshape(1, D),
            "cosFt": consts["cosF"], "sinFt": consts["sinF"],
            "rotM": rotm, "indM": indm,
            "ident": consts["ident"], "ones_row": consts["ones_row"],
        })
    res = run_bass_kernel_spmd(nc, in_maps, core_ids=list(range(8)))
    return np.stack([np.asarray(r["x3"]) for r in res.results])


def emit_l0_attn(nc, p, io):
    x_d = io["x_d"]
    wq_d = io["wq_d"]
    wk_d = io["wk_d"]
    wv_d = io["wv_d"]
    wo_d = io["wo_d"]
    prj_d = io["prj_d"]
    g1_d = io["g1_d"]
    b1_d = io["b1_d"]
    ident_d = io["ident_d"]
    identb_d = io["identb_d"]
    ones_d = io["ones_d"]
    out1_d = io["out1_d"]
    qsp = io["qsp"]
    wscr = io["wscr"]
    ccin = io["ccin"]
    ccout = io["ccout"]

    DN = 64.0 ** -0.25
    RATIO = 0.125
    EPSK = 1e-6
    REPS = RATIO * EPSK
    REPSN = RATIO * EPSK * N_TOK


    stack = ExitStack()
    with stack:
        sb = lambda name, shape, dt=F32: stack.enter_context(nc.sbuf_tensor(name, shape, dt, allow_name_mangling=True))
        ps = lambda name, shape, dt=F32: stack.enter_context(nc.psum_tensor(name, shape, dt, allow_name_mangling=True))
        KT = sb("KT", [128, 8 * N_TOK], BF16)
        VTM = sb("VTM", [128, 32 * 16 * 65], BF16)
        WB = sb("WB", [128, 8 * 1024], BF16)
        STG = sb("STG", [128, 1024], F32)
        BSC = sb("BSC", [128, 1024], BF16)
        XTILE = sb("XTILE", [128, 1024], F32)
        X0T = sb("X0T", [128, 8 * 512], BF16)
        QKSTG = [sb("QKSTG0", [128, 512], BF16), sb("QKSTG1", [128, 512], BF16)]
        RHS1 = sb("RHS1", [128, 65], BF16)
        RHS2 = sb("RHS2", [128, 65], BF16)
        PRJ = sb("PRJ", [64, 64], F32)
        SQW = [sb("SQW0", [64, 128], BF16), sb("SQW1", [64, 128], BF16)]
        PHK = [sb("PHK0", [128, 65], BF16), sb("PHK1", [128, 65], BF16)]
        PHQF = [sb("PHQF0", [128, 64], F32), sb("PHQF1", [128, 64], F32)]
        PHQB = [sb("PHQB0", [128, 64], BF16), sb("PHQB1", [128, 64], BF16)]
        QFT = [sb("QFT0", [64, 128], BF16), sb("QFT1", [64, 128], BF16)]
        QTL = [sb("QTL0", [128, 8 * 128], BF16), sb("QTL1", [128, 8 * 128], BF16)]
        KVS = sb("KVS", [65, 16 * 65], F32)
        KVM = sb("KVM", [64, 16 * 65], BF16)
        TMPK = sb("TMPK", [64, 64], F32)
        TMPK2 = sb("TMPK2", [64, 64], F32)
        RMX = sb("RMX", [128, 4], F32)
        MXS = sb("MXS", [1, 8], F32)
        MXLB = sb("MXLB", [128, 1], F32)
        SPB = sb("SPB", [128, 1], F32)
        BIA = sb("BIA", [128, 2], F32)
        RD = sb("RD", [128, 2], F32)
        ATT = sb("ATT", [128, 16 * 64], BF16)
        ATF = sb("ATF", [128, 8 * 128], BF16)
        CCS = sb("CCS", [2, 1024], F32)
        G1B = sb("G1B", [128, 1024], F32)
        B1B = sb("B1B", [128, 1024], F32)
        ONESB = sb("ONESB", [128, 128], F32)
        LNS = sb("LNS", [128, 1024], F32)
        STAT = sb("STAT", [128, 32], F32)
        CONST = sb("CONST", [128, 4], F32)
        IDENT = sb("IDENT", [128, 128], F32)
        IDB = sb("IDB", [128, 128], BF16)
        ONESR = sb("ONESR", [1, 128], F32)
        PTR = ps("PTR", [128, 512])
        PQK = ps("PQK", [128, 512])
        PV = ps("PV", [128, 1024])
        DD = [ps("DD0", [128, 512]), ps("DD1", [128, 512])]
        KVP = ps("KVP", [65, 512])
        PTB = ps("PTB", [128, 128], BF16)

        PTRKEY = "_ptfree%d" % id(PTR)
        p.ve(lambda e: e.memset(CONST[:, 0:1], 1e-6))
        vtmv = VTM[:, :].rearrange("p (g c) -> p g c", c=65)
        p.ve(lambda e: e.memset(vtmv[:, :, 64:65], 1.0))
        p.ve(lambda e: e.memset(PHK[0][:, 64:65], 1.0))
        p.ve(lambda e: e.memset(PHK[1][:, 64:65], 1.0))
        p.ve(lambda e: e.memset(RMX[:, 0:1], -1e30))
        p.ve(lambda e: e.memset(CCS[:, :], -1e30))
        p.ve(lambda e: e.memset(ONESB[:, :], 1.0))
        p.ve(lambda e: e.memset(RHS1[:, :], 0.0))
        p.ve(lambda e: e.memset(RHS2[:, 0:64], 0.0))
        p.ve(lambda e: e.memset(RHS2[:, 64:65], -0.0625))
        mset = p.c("VE")
        for d, dst in [(ident_d, IDENT), (identb_d, IDB), (ones_d, ONESR)]:
            p.ld(lambda e, d=d, dst=dst: e.dma_start(out=dst[:, :], in_=d))
        lpr = p.ld(lambda e: e.dma_start(out=PRJ[:, :], in_=prj_d))
        vq = p.ve(lambda e: e.tensor_scalar(PRJ[:, :], PRJ[:, :], DN, None, ALU.mult),
                  waits=[("LD", lpr), ("VE", mset)])
        vq = p.ve(lambda e: e.tensor_copy(RHS1[0:64, 0:64], PRJ[:, :]), waits=[("VE", vq)])
        vq = p.ve(lambda e: e.tensor_copy(RHS1[64:128, 0:64], PRJ[:, :]), waits=[("VE", vq)])
        rhs_ready = vq
        lg = p.ld(lambda e: e.dma_start(out=LNS[0:1, 0:1024], in_=g1_d))
        bcast_row(p, PV, G1B, LNS[0:1, 0:1024], ONESR[0:1, :], 1024, waits=[("LD", lg)])
        lg = p.ld(lambda e: e.dma_start(out=LNS[0:1, 0:1024], in_=b1_d),
                  waits=[("TE", p.c("TE"))])
        bcast_row(p, PV, B1B, LNS[0:1, 0:1024], ONESR[0:1, :], 1024, waits=[("LD", lg)])
        p.counts["_pvfree"] = p.c("VE")

        STGB = sb("STGB", [128, 1024], F32)
        BSCB = sb("BSCB", [128, 1024], BF16)
        jobs = [(wd[c * 128:(c + 1) * 128, :], wscr[wi, c, :, :], 1024)
                for wi, wd in enumerate([wq_d, wk_d, wv_d, wo_d]) for c in range(8)]
        scr_ready = emit_wconv(nc, p, [STG, STGB], [BSC, BSCB], jobs, "a")

        def load_wb(wi):
            return p.ld(lambda e, wi=wi: e.dma_start(
                out=WB[:, :].rearrange("p (c n) -> p c n", n=1024),
                in_=wscr[wi].rearrange("c p n -> p c n")),
                waits=[("GD", scr_ready), ("TE", p.counts.get("_wbused", 0))])

        X0TV = X0T[:, :].rearrange("p (c t) -> p c t", t=512)
        for ch in range(8):
            t0 = ch * 512
            for sub in range(4):
                lx = p.ld(lambda e, t0=t0, sub=sub: e.dma_start(
                    out=XTILE[:, :], in_=x_d[t0 + sub * 128:t0 + sub * 128 + 128, :]),
                    waits=[("TE", p.counts.get("_xtfree", 0))])
                emit_transpose_to_fm(
                    p, lambda hh, sub=sub: X0TV[:, :, sub * 128:sub * 128 + 128][:, hh * 4:hh * 4 + 4, :],
                    XTILE[:, :], PTR, IDENT[:, :], waits=[("LD", lx)])
                p.counts["_xtfree"] = p.c("TE")
            xt_ready = p.c("VE")
            for which in (0, 1):
                wld = load_wb(which)
                for j in range(8):
                    db = j % 2
                    for c in range(8):
                        p.te(lambda e, j=j, c=c, db=db: e.matmul(
                            DD[db][:, :], WB[:, c * 1024 + j * 128:c * 1024 + (j + 1) * 128],
                            X0TV[:, c, :], start=(c == 0), stop=(c == 7)),
                            waits=[("LD", wld), ("VE", xt_ready),
                                   ("SC", p.counts.get("_pqkf%d" % db, 0))],
                            tick=(c == 7))
                    tq = p.c("TE")
                    if which == 1:
                        sev = p.sc(lambda e, j=j, t0=t0, db=db: e.activation(
                            KT[:, j * N_TOK + t0:j * N_TOK + t0 + 512], DD[db][:, :], AF.Identity),
                            waits=[("TE", tq)])
                    else:
                        sev = p.sc(lambda e, db=db: e.activation(QKSTG[db][:, :], DD[db][:, :], AF.Identity),
                                   waits=[("TE", tq), ("ST", p.counts.get("_qkstgfree%d" % db, 0))])
                        p.counts["_qkstgfree%d" % db] = p.st(
                            lambda e, j=j, t0=t0, db=db: e.dma_start(
                                out=qsp[j, :, t0:t0 + 512], in_=QKSTG[db][:, :]),
                            waits=[("SC", sev)])
                    p.counts["_pqkf%d" % db] = sev
                p.counts["_wbused"] = p.c("TE")
            wld = load_wb(2)
            for sub in range(4):
                for c in range(8):
                    for nh in range(2):
                        p.te(lambda e, sub=sub, c=c, nh=nh: e.matmul(
                            PV[:, nh * 512:(nh + 1) * 512],
                            X0TV[:, c, sub * 128:sub * 128 + 128],
                            WB[:, c * 1024 + nh * 512:c * 1024 + (nh + 1) * 512],
                            start=(c == 0), stop=(c == 7)),
                            waits=[("LD", wld), ("VE", xt_ready),
                                   ("VE", p.counts.get("_pvfree", 0))],
                            tick=(c == 7 and nh == 1))
                tv = p.c("TE")
                tile = ch * 4 + sub
                vv = p.ve(lambda e, tile=tile: e.tensor_copy(
                    vtmv[:, tile * 16:(tile + 1) * 16, 0:64],
                    PV[:, :].rearrange("p (h c) -> p h c", c=64)),
                    waits=[("TE", tv), ("VE", mset)])
                p.counts["_pvfree"] = vv
            p.counts["_wbused"] = p.c("TE")
        p.fence()
        for h in range(16):
            par = (h % 2) * 64
            j = h // 2
            for tile in range(32):
                db = tile % 2
                tm = p.te(lambda e, par=par, j=j, tile=tile, db=db: e.matmul(
                    DD[db][:, 0:65],
                    KT[par:par + 64, j * N_TOK + tile * 128:j * N_TOK + (tile + 1) * 128],
                    RHS1[par:par + 64, :], start=True, stop=True),
                    waits=[("VE", rhs_ready), ("VE", p.counts.get("_ddv%d" % db, 0))],
                    tick=True)
                vm = p.ve(lambda e, db=db: e.tensor_reduce(STAT[:, 0:1], DD[db][:, 0:64],
                                                           axis=AX.X, op=ALU.max),
                          waits=[("TE", tm), ("VE", p.counts.get("_statmx", 0))])
                p.counts["_ddv%d" % db] = vm
                vm = p.ve(lambda e: e.tensor_tensor(RMX[:, 0:1], RMX[:, 0:1], STAT[:, 0:1], ALU.max),
                          waits=[("VE", vm)])
                p.counts["_statmx"] = vm
        tmx = p.te(lambda e: e.transpose(PTR[0:1, 0:128], RMX[:, 0:1], IDENT[:, :]),
                   waits=[("VE", p.c("VE")), ("VE", p.counts.get(PTRKEY, 0))], tick=True)
        vm = p.ve(lambda e: e.tensor_copy(LNS[0:1, 0:128], PTR[0:1, 0:128]), waits=[("TE", tmx)])
        p.counts[PTRKEY] = vm
        vm = p.ve(lambda e: e.tensor_reduce(MXS[0:1, 0:1], LNS[0:1, 0:128], axis=AX.X, op=ALU.max),
                  waits=[("VE", vm)])
        vm = p.ve(lambda e: e.tensor_copy(CCS[0:1, 0:1], MXS[0:1, 0:1]), waits=[("VE", vm)])
        gcc = p.gld(lambda e: e.dma_start(out=ccin[:, :], in_=CCS[:, :]), waits=[("VE", vm)])
        p.emit("gpsimd", lambda e: e.collective_compute(
            "AllReduce", mybir.AluOpType.max, replica_groups=[list(range(8))],
            ins=[ccin.ap()], outs=[ccout.ap()]), waits=[("GD", gcc)], incs=[("CC", 1)])
        gcc = p.gld(lambda e: e.dma_start(out=MXS[0:1, 1:2], in_=ccout[0:1, 0:1]),
                    waits=[("CC", 1)])
        tmb = p.te(lambda e: e.matmul(PQK[:, 0:1], ONESR[0:1, :], MXS[0:1, 0:1],
                                      start=True, stop=True),
                   waits=[("VE", vm), ("SC", p.counts.get("_pqkfree", 0))], tick=True)
        vmb = p.ve(lambda e: e.tensor_copy(MXLB[:, :], PQK[:, 0:1]), waits=[("TE", tmb)])
        def emit_ddk(h, tile, db):
            par = (h % 2) * 64
            j = h // 2
            ssq = p.sc(lambda e, par=par, j=j, tile=tile, db=db: e.activation(
                SQW[db][0:64, :], KT[par:par + 64, j * N_TOK + tile * 128:j * N_TOK + (tile + 1) * 128],
                AF.Square),
                waits=[("TE", p.counts.get("_sqwused%d" % db, 0))])
            p.te(lambda e, par=par, j=j, tile=tile, db=db: e.matmul(
                DD[db][:, 0:65],
                KT[par:par + 64, j * N_TOK + tile * 128:j * N_TOK + (tile + 1) * 128],
                RHS1[par:par + 64, :], start=True, stop=False),
                waits=[("VE", p.counts.get("_ddv%d" % db, 0)),
                       ("SC", p.counts.get("_ddexp%d" % db, 0))])
            tm = p.te(lambda e, db=db: e.matmul(
                DD[db][:, 0:65], SQW[db][0:64, :], RHS2[0:64, :], start=False, stop=True),
                waits=[("SC", ssq)], tick=True)
            p.counts["_sqwused%d" % db] = tm
            vb = p.ve(lambda e, db=db: e.tensor_tensor(
                BIA[:, db:db + 1], DD[db][:, 64:65], MXLB[:, :], ALU.subtract),
                waits=[("TE", tm), ("VE", vmb), ("SC", p.counts.get("_biused%d" % db, 0))])
            p.counts["_ddv%d" % db] = vb
            sx = p.sc(lambda e, db=db: e.activation(
                PHK[db][:, 0:64], DD[db][:, 0:64], AF.Exp, bias=BIA[:, db:db + 1]),
                waits=[("VE", vb), ("TE", p.counts.get("_phkused%d" % db, 0))])
            p.counts["_ddexp%d" % db] = sx
            p.counts["_biused%d" % db] = sx
            return sx

        for h in range(16):
            pend = None
            for tile in range(32):
                db = tile % 2
                sx = emit_ddk(h, tile, db)
                if pend is not None:
                    ptile, psx = pend
                    tkv = p.te(lambda e, h=h, ptile=ptile, pdb=ptile % 2: e.matmul(
                        KVP[:, 0:65], PHK[pdb][:, :],
                        VTM[:, (ptile * 16 + h) * 65:(ptile * 16 + h) * 65 + 65],
                        start=(ptile == 0), stop=False),
                        waits=[("SC", psx), ("SC", p.counts.get("_kvpfree", 0))], tick=True)
                    p.counts["_phkused%d" % (ptile % 2)] = tkv
                pend = (tile, sx)
            tkv = p.te(lambda e, h=h: e.matmul(
                KVP[:, 0:65], PHK[31 % 2][:, :], VTM[:, (31 * 16 + h) * 65:(31 * 16 + h) * 65 + 65],
                start=False, stop=True),
                waits=[("SC", pend[1])], tick=True)
            p.counts["_phkused%d" % (31 % 2)] = tkv
            skv = p.sc(lambda e, h=h: e.activation(KVS[0:65, h * 65:(h + 1) * 65], KVP[:, 0:65],
                                                   AF.Identity),
                       waits=[("TE", p.c("TE"))])
            p.counts["_kvpfree"] = skv
        kvs_ready = p.c("SC")
        vm = p.ve(lambda e: e.tensor_tensor(MXS[0:1, 2:3], MXS[0:1, 0:1], MXS[0:1, 1:2], ALU.subtract),
                  waits=[("GD", gcc), ("SC", kvs_ready)])
        sm = p.sc(lambda e: e.activation(MXS[0:1, 3:4], MXS[0:1, 2:3], AF.Exp), waits=[("VE", vm)])
        vm = p.ve(lambda e: e.tensor_scalar(MXS[0:1, 4:5], MXS[0:1, 3:4], RATIO, None, ALU.mult),
                  waits=[("SC", sm)])
        tsp = p.te(lambda e: e.matmul(PQK[:, 0:1], ONESR[0:1, :], MXS[0:1, 4:5],
                                      start=True, stop=True),
                   waits=[("VE", vm)], tick=True)
        vsp = p.ve(lambda e: e.tensor_copy(SPB[:, :], PQK[:, 0:1]), waits=[("TE", tsp)])
        for h in range(16):
            tb = p.te(lambda e, h=h: e.matmul(
                DD[0][:, 0:64][0:64, :], ONESB[64:65, 0:64], KVS[64:65, h * 65:h * 65 + 64],
                start=True, stop=True),
                waits=[("SC", kvs_ready), ("VE", p.counts.get("_ddv0", 0))], tick=True)
            v1 = p.ve(lambda e, h=h: e.tensor_scalar(
                TMPK[:, :], KVS[0:64, h * 65:h * 65 + 64], SPB[0:64, 0:1], None, ALU.mult),
                waits=[("VE", vsp), ("SC", kvs_ready), ("VE", p.counts.get("_tmpkv", 0))])
            v2 = p.ve(lambda e: e.tensor_scalar(
                TMPK2[:, :], DD[0][0:64, 0:64], REPS, None, ALU.mult),
                waits=[("TE", tb), ("VE", v1)])
            p.counts["_ddv0"] = v2
            v3 = p.ve(lambda e, h=h: e.tensor_tensor(
                KVM[0:64, h * 65:h * 65 + 64], TMPK[:, :], TMPK2[:, :], ALU.add),
                waits=[("VE", v2)])
            p.counts["_tmpkv"] = v3
            v4 = p.ve(lambda e, h=h: e.tensor_scalar(
                KVM[0:64, h * 65 + 64:h * 65 + 65], KVS[0:64, h * 65 + 64:h * 65 + 65],
                SPB[0:64, 0:1], REPSN, ALU.mult, ALU.add),
                waits=[("VE", vsp), ("SC", kvs_ready)])
        kvm_ready = p.c("VE")
        p.fence()
        wld_o = load_wb(3)
        for tile in range(32):
            t0 = tile * 128
            qb_ = tile % 2
            QTLB = QTL[qb_]
            lq = p.ld(lambda e, t0=t0, QTLB=QTLB: e.dma_start(
                out=QTLB[:, :].rearrange("p (j t) -> p j t", t=128),
                in_=qsp[:, :, t0:t0 + 128].rearrange("j p t -> p j t")),
                waits=[("TE", p.counts.get("_qtlfree%d" % qb_, 0)), ("SC", p.counts.get("_qtlfree_sc%d" % qb_, 0))])
            lx = p.ld(lambda e, t0=t0: e.dma_start(
                out=XTILE[:, :], in_=x_d[t0:t0 + 128, :]),
                waits=[("VE", p.counts.get("_xtilev", 0))])
            def q_stage1(h, lq):
                par = (h % 2) * 64
                jj = h // 2
                db = h % 2
                ssq = p.sc(lambda e, par=par, jj=jj, db=db, QTLB=QTLB: e.activation(
                    SQW[db][0:64, :], QTLB[par:par + 64, jj * 128:(jj + 1) * 128], AF.Square),
                    waits=[("LD", lq), ("TE", p.counts.get("_sqwused%d" % db, 0))])
                p.te(lambda e, par=par, jj=jj, db=db, QTLB=QTLB: e.matmul(
                    DD[db][:, 0:65], QTLB[par:par + 64, jj * 128:(jj + 1) * 128],
                    RHS1[par:par + 64, :], start=True, stop=False),
                    waits=[("LD", lq), ("VE", p.counts.get("_ddv%d" % db, 0)),
                           ("SC", p.counts.get("_ddexp%d" % db, 0))])
                tm = p.te(lambda e, db=db: e.matmul(
                    DD[db][:, 0:65], SQW[db][0:64, :], RHS2[0:64, :], start=False, stop=True),
                    waits=[("SC", ssq)], tick=True)
                p.counts["_sqwused%d" % db] = tm
                vmx = p.ve(lambda e, db=db: e.tensor_reduce(
                    STAT[:, 1 + db:2 + db], DD[db][:, 0:64], axis=AX.X, op=ALU.max),
                    waits=[("TE", tm), ("SC", p.counts.get("_statq%d" % db, 0))])
                vb = p.ve(lambda e, db=db: e.tensor_tensor(
                    BIA[:, db:db + 1], DD[db][:, 64:65], STAT[:, 1 + db:2 + db], ALU.subtract),
                    waits=[("VE", vmx), ("SC", p.counts.get("_biq%d" % db, 0))])
                p.counts["_ddv%d" % db] = vb
                sx = p.sc(lambda e, db=db: e.activation(
                    PHQF[db][:, :], DD[db][:, 0:64], AF.Exp, bias=BIA[:, db:db + 1]),
                    waits=[("VE", vb), ("VE", p.counts.get("_phqused%d" % db, 0))])
                p.counts["_statq%d" % db] = sx
                p.counts["_biq%d" % db] = sx
                p.counts["_ddexp%d" % db] = sx
                vph = p.ve(lambda e, db=db: e.tensor_scalar(
                    PHQB[db][:, :], PHQF[db][:, :], EPSK, None, ALU.add),
                    waits=[("SC", sx), ("TE", p.counts.get("_phqbused%d" % db, 0))])
                p.counts["_phqused%d" % db] = vph
                return vph

            def q_stage2(h, vph):
                db = h % 2
                tph = p.te(lambda e, db=db: e.transpose(PTB[0:64, :], PHQB[db][:, :], IDB[:, :]),
                           waits=[("VE", vph), ("VE", p.counts.get("_ptbv", 0))], tick=True)
                p.counts["_phqbused%d" % db] = tph
                vqf = p.ve(lambda e, db=db: e.tensor_copy(QFT[db][0:64, :], PTB[0:64, :]),
                           waits=[("TE", tph), ("TE", p.counts.get("_qftused%d" % db, 0))])
                p.counts["_ptbv"] = vqf
                tnm = p.te(lambda e, h=h, db=db: e.matmul(
                    DD[db][:, 0:65], QFT[db][0:64, :], KVM[0:64, h * 65:(h + 1) * 65],
                    start=True, stop=True),
                    waits=[("VE", vqf), ("VE", p.counts.get("_ddv%d" % db, 0))], tick=True)
                p.counts["_qftused%d" % db] = tnm
                vrd = p.ve(lambda e, db=db: e.reciprocal(RD[:, db:db + 1], DD[db][:, 64:65]),
                           waits=[("TE", tnm)])
                vat = p.ve(lambda e, h=h, db=db: e.tensor_scalar(
                    ATT[:, h * 64:(h + 1) * 64], DD[db][:, 0:64], RD[:, db:db + 1], None, ALU.mult),
                    waits=[("VE", vrd), ("TE", p.counts.get("_attused", 0))])
                p.counts["_ddv%d" % db] = vat

            pend_q = None
            for h in range(16):
                s1 = q_stage1(h, lq)
                if pend_q is not None:
                    q_stage2(*pend_q)
                pend_q = (h, s1)
            q_stage2(*pend_q)
            att_ready = p.c("VE")
            for jc in range(8):
                tat = p.te(lambda e, jc=jc: e.transpose(
                    PTB[:, :], ATT[:, jc * 128:(jc + 1) * 128], IDB[:, :]),
                    waits=[("VE", att_ready), ("VE", p.counts.get("_ptbv", 0))], tick=True)
                vaf = p.ve(lambda e, jc=jc: e.tensor_copy(ATF[:, jc * 128:(jc + 1) * 128], PTB[:, :]),
                           waits=[("TE", tat), ("TE", p.counts.get("_atfused", 0))])
                p.counts["_ptbv"] = vaf
            atf_ready = p.c("VE")
            for jc in range(8):
                for nh in range(2):
                    p.te(lambda e, jc=jc, nh=nh: e.matmul(
                        PV[:, nh * 512:(nh + 1) * 512],
                        ATF[:, jc * 128:(jc + 1) * 128],
                        WB[:, jc * 1024 + nh * 512:jc * 1024 + (nh + 1) * 512],
                        start=(jc == 0), stop=(jc == 7)),
                        waits=[("VE", atf_ready), ("LD", wld_o),
                               ("SC", p.counts.get("_pwfree", 0))],
                        tick=(jc == 7 and nh == 1))
            tw = p.c("TE")
            p.counts["_attused"] = tw
            p.counts["_atfused"] = tw
            sy = p.sc(lambda e: e.activation(LNS[:, :], PV[:, :], AF.Identity),
                      waits=[("TE", tw), ("VE", p.counts.get("_lnsv", 0)),
                             ("ST", p.counts.get("_lnst", 0))])
            p.counts["_pwfree"] = sy
            p.counts["_qtlfree_sc%d" % qb_] = sy
            vy = p.ve(lambda e: e.tensor_tensor(LNS[:, :], LNS[:, :], XTILE[:, :], ALU.add),
                      waits=[("SC", sy), ("LD", lx)])
            p.counts["_xtilev"] = vy
            r2 = emit_ln_tile(p, LNS, LNS, STG, STAT[:, 16:24], CONST[:, 0:1],
                              g_ap=G1B[:, :], b_ap=B1B[:, :], waits=[("VE", vy)])
            p.counts["_lnsv"] = max(p.counts.get("_lnsv", 0), r2.get("VE", 0), vy)
            p.counts["_qtlfree%d" % qb_] = p.c("TE")
            p.counts["_lnst"] = p.stv(
                lambda e, t0=t0: e.dma_start(out=out1_d[t0:t0 + 128, :], in_=LNS[:, :]),
                waits=[("SC", r2.get("SC", 0)), ("VE", r2.get("VE", 0))])
        p.emit("sync", lambda e: e.nop(), waits=[("ST", p.c("ST"))])
        p.fence()
        with nc.Block() as block:
            p.flush(block)


def build_l0_attn():
    nc = bass.Bass("TRN2", target_bir_lowering=False)
    x_d = nc.dram_tensor("x", [N_TOK, D], F32, kind="ExternalInput").ap()
    wq_d = nc.dram_tensor("pWq", [D, D], F32, kind="ExternalInput").ap()
    wk_d = nc.dram_tensor("pWk", [D, D], F32, kind="ExternalInput").ap()
    wv_d = nc.dram_tensor("pWv", [D, D], F32, kind="ExternalInput").ap()
    wo_d = nc.dram_tensor("pWo", [D, D], F32, kind="ExternalInput").ap()
    prj_d = nc.dram_tensor("projT", [64, 64], F32, kind="ExternalInput").ap()
    g1_d = nc.dram_tensor("g1row", [1, D], F32, kind="ExternalInput").ap()
    b1_d = nc.dram_tensor("b1row", [1, D], F32, kind="ExternalInput").ap()
    ident_d = nc.dram_tensor("ident", [128, 128], F32, kind="ExternalInput").ap()
    identb_d = nc.dram_tensor("identb", [128, 128], BF16, kind="ExternalInput").ap()
    ones_d = nc.dram_tensor("ones_row", [1, 128], F32, kind="ExternalInput").ap()
    out1_d = nc.dram_tensor("out1", [N_TOK, D], F32, kind="ExternalOutput").ap()
    qsp = nc.dram_tensor("qspill", [8, 128, N_TOK], BF16)
    wscr = nc.dram_tensor("wscr", [4, 8, 128, D], BF16)
    ccin = nc.dram_tensor("ccin", [2, 1024], F32)
    ccout = nc.dram_tensor("ccout", [2, 1024], F32, addr_space="Shared")
    io = dict(x_d=x_d, wq_d=wq_d, wk_d=wk_d, wv_d=wv_d, wo_d=wo_d, prj_d=prj_d, g1_d=g1_d, b1_d=b1_d, ident_d=ident_d, identb_d=identb_d, ones_d=ones_d, out1_d=out1_d, qsp=qsp, wscr=wscr, ccin=ccin, ccout=ccout)
    with ExitStack() as _st:
        p = Prog(nc, _st)
        emit_l0_attn(nc, p, io)
    return nc


def run_l0_attn(x_full, Wq, Wk, Wv, Wo, proj, g1, b1):
    consts = make_consts()
    nc = build_l0_attn()
    in_maps = []
    for c in range(8):
        in_maps.append({
            "x": _f(x_full[c]),
            "pWq": _f(np.asarray(Wq).reshape(D, D)), "pWk": _f(np.asarray(Wk).reshape(D, D)),
            "pWv": _f(np.asarray(Wv).reshape(D, D)), "pWo": _f(np.asarray(Wo).reshape(D, D)),
            "projT": _f(np.asarray(proj).T),
            "g1row": _f(g1).reshape(1, D), "b1row": _f(b1).reshape(1, D),
            "ident": consts["ident"],
            "identb": consts["ident"].astype(ml_dtypes.bfloat16),
            "ones_row": consts["ones_row"],
        })
    res = run_bass_kernel_spmd(nc, in_maps, core_ids=list(range(8)))
    return np.stack([np.asarray(r["out1"]) for r in res.results])


DEBUG_TAPS = False
_PROG_CACHE = {}


def _prog(name, builder):
    if name not in _PROG_CACHE:
        _PROG_CACHE[name] = builder()
    return _PROG_CACHE[name]


def build_fused():
    nc = bass.Bass("TRN2", target_bir_lowering=False)
    T = lambda n, shp, dt=F32, kind=None, space=None: (
        nc.dram_tensor(n, shp, dt, kind=kind, addr_space=space).ap() if kind and space else
        nc.dram_tensor(n, shp, dt, kind=kind).ap() if kind else
        nc.dram_tensor(n, shp, dt, addr_space=space).ap() if space else
        nc.dram_tensor(n, shp, dt).ap())
    EI = "ExternalInput"
    x_d = T("x", [N_TOK, D], kind=EI)
    ident_d = T("ident", [128, 128], kind=EI)
    identb_d = T("identb", [128, 128], BF16, kind=EI)
    ones_d = T("ones_row", [1, 128], kind=EI)
    _ik = "ExternalOutput" if DEBUG_TAPS else None
    out1_d = T("i_out1", [N_TOK, D], kind=_ik)
    x1_d = T("i_x1", [N_TOK, D], kind=_ik)
    x2_d = T("i_x2", [N_TOK, D], kind=_ik)
    x3_d = T("i_x3", [N_TOK, D], kind=_ik)
    out_d = T("out", [N_TOK, D], kind="ExternalOutput")

    io1 = dict(
        x_d=x_d, ident_d=ident_d, identb_d=identb_d, ones_d=ones_d, out1_d=out1_d,
        wq_d=T("pWq", [D, D], kind=EI), wk_d=T("pWk", [D, D], kind=EI),
        wv_d=T("pWv", [D, D], kind=EI), wo_d=T("pWo", [D, D], kind=EI),
        prj_d=T("projT", [64, 64], kind=EI),
        g1_d=T("g1row", [1, D], kind=EI), b1_d=T("b1row", [1, D], kind=EI),
        qsp=nc.dram_tensor("l0qspill", [8, 128, N_TOK], BF16),
        wscr=nc.dram_tensor("l0wscr", [4, 8, 128, D], BF16),
        ccin=nc.dram_tensor("ccin", [2, 1024], F32),
        ccout=nc.dram_tensor("ccout", [2, 1024], F32, addr_space="Shared"),
    )
    io2 = dict(
        out1_d=out1_d, x_d=x_d, ident_d=ident_d, ones_d=ones_d, x1_d=x1_d,
        w1_d=T("W1", [D, 4096], kind=EI), w2_d=T("W2", [4096, D], kind=EI),
        w1scr_d=nc.dram_tensor("w1scr", [32, 8, 128, 128], BF16).ap(),
        b1_d=T("b1pt", [128, 32], kind=EI), b2_d=T("b2row", [1, D], kind=EI),
        g2_d=T("g2row", [1, D], kind=EI), bb2_d=T("bb2row", [1, D], kind=EI),
    )
    io3 = dict(
        x1_d=x1_d, ident_d=ident_d, ones_d=ones_d, x2_d=x2_d,
        w1_d=T("mw1", [8, D, 512], kind=EI), w2_d=T("mw2", [8, 512, D], kind=EI),
        b1_d=T("mb1pt", [128, 32], kind=EI), b2_d=T("mb2", [8, D], kind=EI),
        cen_d=T("cen", [8, D], kind=EI), crt_d=T("crt", [D, 16], kind=EI),
    )
    io4 = dict(
        x2_d=x2_d, ident_d=ident_d, ones_d=ones_d, x3_d=x3_d,
        wq_d=T("awqT", [D, D], kind=EI), wk_d=T("awkT", [D, D], kind=EI),
        wv_d=T("awvT", [D, D], kind=EI), wo_d=T("awoT", [D, D], kind=EI),
        bq_d=T("abqpt", [128, 8], kind=EI), bk_d=T("abkpt", [128, 8], kind=EI),
        bv_d=T("abvrow", [1, D], kind=EI), bo_d=T("aborow", [1, D], kind=EI),
        cos_d=T("cosFt", [128, N_TOK], kind=EI), sin_d=T("sinFt", [128, N_TOK], kind=EI),
        rot_d=T("rotM", [128, 128], kind=EI), ind_d=T("indM", [128, 2], kind=EI),
        qsp=nc.dram_tensor("l1qspill", [8, 128, N_TOK], BF16),
        wscr=nc.dram_tensor("l1wscr", [4, 8, 128, D], BF16),
    )
    NE = 9
    io5 = dict(
        x3_d=x3_d, ident_d=ident_d, ones_d=ones_d, out_d=out_d,
        w1_d=T("fw1", [NE, D, 1408], kind=EI), w3_d=T("fw3", [NE, D, 1408], kind=EI),
        w2_d=T("fw2", [NE, 1408, D], kind=EI),
        b1_d=T("fb1pt", [128, NE * 11], kind=EI), b3_d=T("fb3pt", [128, NE * 11], kind=EI),
        b2_d=T("fb2p", [128, 3 * D], kind=EI),
        gw_d=T("fgwt", [D, 8], kind=EI), gb_d=T("fgbrow", [1, 8], kind=EI),
        w13s=nc.dram_tensor("w13scr", [NE, 11, 128, 2048], BF16),
        w2s=nc.dram_tensor("w2scr", [NE, 128, 11 * D], BF16),
    )
    io4["mw1_d"] = io5["w1_d"]
    io4["mw3_d"] = io5["w3_d"]
    io4["mw2_d"] = io5["w2_d"]
    io4["mw13s"] = io5["w13s"]
    io4["mw2s"] = io5["w2s"]
    io5["preconverted"] = True
    with ExitStack() as _st:
        p = Prog(nc, _st)
        emit_l0_attn(nc, p, io1)
        emit_l0_ffn_half(nc, p, io2)
        emit_l0_moe(nc, p, io3)
        emit_l1_attn(nc, p, io4)
        emit_l1_moe(nc, p, io5)
    return nc


def kernel(**inp):
    consts = make_consts()
    cores = list(range(8))
    x = np.ascontiguousarray(inp["L0_attn_x"], dtype=np.float32)

    rotm = np.zeros((128, 128), np.float32)
    for base in (0, 64):
        for dd in range(32):
            rotm[base + 32 + dd, base + dd] = -1.0
            rotm[base + dd, base + 32 + dd] = 1.0
    indm = np.zeros((128, 2), np.float32)
    indm[0:64, 0] = 1.0
    indm[64:128, 1] = 1.0
    crt = np.ascontiguousarray(np.concatenate(
        [np.asarray(inp["L0_ffn_centroids"]).T, np.asarray(inp["L0_ffn_w_route"]).T], axis=1),
        dtype=np.float32)
    b1pt3 = np.ascontiguousarray(
        np.asarray(inp["L0_ffn_b1"], dtype=np.float32).reshape(8, 4, 128).transpose(2, 0, 1).reshape(128, 32))
    NE = 9
    w1a = np.concatenate([inp["L1_ffn_w1"], inp["L1_ffn_sw1"][None]], 0)
    w3a = np.concatenate([inp["L1_ffn_w3"], inp["L1_ffn_sw3"][None]], 0)
    w2a = np.concatenate([inp["L1_ffn_w2"], inp["L1_ffn_sw2"][None]], 0)
    b1a = np.concatenate([inp["L1_ffn_b1"], inp["L1_ffn_sb1"][None]], 0)
    b3a = np.concatenate([inp["L1_ffn_b3"], inp["L1_ffn_sb3"][None]], 0)
    b2a = np.concatenate([inp["L1_ffn_b2"], inp["L1_ffn_sb2"][None]], 0)

    def pt11(b):
        out = np.zeros((128, NE * 11), np.float32)
        for e in range(NE):
            out[:, e * 11:(e + 1) * 11] = np.asarray(b[e], dtype=np.float32).reshape(11, 128).T
        return np.ascontiguousarray(out)

    b2p = np.zeros((128, 3 * 1024), np.float32)
    for e in range(NE):
        b2p[32 * (e // 3), (e % 3) * 1024:(e % 3 + 1) * 1024] = b2a[e]

    common = {
        "ident": consts["ident"], "identb": consts["ident"].astype(ml_dtypes.bfloat16),
        "ones_row": consts["ones_row"],
        "pWq": _f(inp["L0_attn_Wq"].reshape(D, D)), "pWk": _f(inp["L0_attn_Wk"].reshape(D, D)),
        "pWv": _f(inp["L0_attn_Wv"].reshape(D, D)), "pWo": _f(inp["L0_attn_Wo"].reshape(D, D)),
        "projT": _f(np.asarray(inp["L0_attn_proj"]).T),
        "g1row": _f(inp["L0_attn_ln1_g"]).reshape(1, D), "b1row": _f(inp["L0_attn_ln1_b"]).reshape(1, D),
        "W1": _f(inp["L0_attn_W1"]), "W2": _f(inp["L0_attn_W2"]),
        "b1pt": per_part(inp["L0_attn_b1"]), "b2row": _f(inp["L0_attn_b2"]).reshape(1, D),
        "g2row": _f(inp["L0_attn_ln2_g"]).reshape(1, D), "bb2row": _f(inp["L0_attn_ln2_b"]).reshape(1, D),
        "mw1": _f(inp["L0_ffn_w1"]), "mw2": _f(inp["L0_ffn_w2"]),
        "mb1pt": b1pt3, "mb2": _f(inp["L0_ffn_b2"]), "cen": _f(inp["L0_ffn_centroids"]), "crt": crt,
        "awqT": _f(np.asarray(inp["L1_attn_wq"]).T), "awkT": _f(np.asarray(inp["L1_attn_wk"]).T),
        "awvT": _f(np.asarray(inp["L1_attn_wv"]).T), "awoT": _f(np.asarray(inp["L1_attn_wo"]).T),
        "abqpt": per_part(inp["L1_attn_bq"]), "abkpt": per_part(inp["L1_attn_bk"]),
        "abvrow": _f(inp["L1_attn_bv"]).reshape(1, D), "aborow": _f(inp["L1_attn_bo"]).reshape(1, D),
        "cosFt": consts["cosF"], "sinFt": consts["sinF"], "rotM": rotm, "indM": indm,
        "fw1": _f(w1a), "fw3": _f(w3a), "fw2": _f(w2a),
        "fb1pt": pt11(b1a), "fb3pt": pt11(b3a), "fb2p": _f(b2p),
        "fgwt": _f(np.asarray(inp["L1_ffn_gate_w"]).T), "fgbrow": _f(inp["L1_ffn_gate_b"]).reshape(1, 8),
    }
    nc = _prog("fused", build_fused)
    in_maps = [dict(common, x=_f(x[c])) for c in cores]
    res = run_bass_kernel_spmd(nc, in_maps, core_ids=cores)
    return np.stack([np.asarray(r["out"]) for r in res.results]).astype(np.float32)


# version 13
# speedup vs baseline: 1.0012x; 1.0012x over previous
import math
from contextlib import ExitStack

import numpy as np
import ml_dtypes
import concourse.bass as bass
import concourse.mybir as mybir
from concourse.bass_utils import run_bass_kernel_spmd

F32 = mybir.dt.float32
BF16 = mybir.dt.bfloat16
AF = mybir.ActivationFunctionType
ALU = mybir.AluOpType
AX = mybir.AxisListType

D = 1024
N_TOK = 4096
N_TILES = N_TOK // 128


class Prog:

    ENGINES = ("sync", "vector", "scalar", "tensor", "gpsimd")

    def __init__(self, nc, stack):
        self.nc = nc
        self.stack = stack
        self.ops = {e: [] for e in self.ENGINES}
        self.sems = {}
        self.counts = {}

    def sem(self, name):
        if name not in self.sems:
            self.sems[name] = self.stack.enter_context(self.nc.semaphore(name))
            self.counts[name] = 0
        return self.sems[name]

    def emit(self, eng, fn, waits=(), incs=()):
        out = {}
        incs = list(incs)
        for s, n in incs:
            self.sem(s)
            self.counts[s] += n
            out[s] = self.counts[s]
        waits = [(s, v) for (s, v) in waits if v and v > 0]
        for s, _v in waits:
            self.sem(s)
        self.ops[eng].append((list(waits), fn, incs))
        return out

    def ld(self, fn, waits=()):
        return self.emit("sync", fn, waits, [("LD", 16)])["LD"]

    def st(self, fn, waits=()):
        return self.emit("sync", fn, waits, [("ST", 16)])["ST"]

    def gld(self, fn, waits=()):
        return self.emit("gpsimd", fn, waits, [("GD", 16)])["GD"]

    def stv(self, fn, waits=()):
        return self.emit("scalar", fn, waits, [("ST", 16)])["ST"]

    def te(self, fn, waits=(), tick=False):
        incs = [("TE", 1)] if tick else []
        r = self.emit("tensor", fn, waits, incs)
        return r.get("TE")

    def ve(self, fn, waits=(), tick=True):
        incs = [("VE", 1)] if tick else []
        r = self.emit("vector", fn, waits, incs)
        return r.get("VE")

    def sc(self, fn, waits=(), tick=True):
        incs = [("SC", 1)] if tick else []
        r = self.emit("scalar", fn, waits, incs)
        return r.get("SC")

    def gp(self, fn, waits=(), tick=True):
        incs = [("GP", 1)] if tick else []
        r = self.emit("gpsimd", fn, waits, incs)
        return r.get("GP")

    def c(self, name):
        return self.counts.get(name, 0)

    def fence(self):
        marks = {}
        for eng, sem in [("sync", "FS"), ("vector", "FV"), ("scalar", "FC"),
                         ("tensor", "FT"), ("gpsimd", "FG")]:
            marks[sem] = self.emit(eng, lambda e: e.nop(), [], [(sem, 1)])[sem]
        dma_waits = [(s, self.c(s)) for s in ("LD", "ST", "GD") if self.c(s) > 0]
        for eng in self.ENGINES:
            waits = [(s, v) for s, v in marks.items()] + dma_waits
            self.emit(eng, lambda e: e.nop(), waits, [])
        for k in [k for k in self.counts if k.startswith("_")]:
            del self.counts[k]

    def flush(self, block):
        self.build(block)
        self.ops = {e: [] for e in self.ENGINES}

    def build(self, block):
        for eng_name in self.ENGINES:
            ops = self.ops[eng_name]
            if not ops:
                continue

            def body(eng, ops=ops):
                done = {}
                for waits, fn, incs in ops:
                    for s, v in waits:
                        if done.get(s, -1) < v:
                            eng.wait_ge(self.sems[s], v)
                            done[s] = v
                    inst = fn(eng)
                    for s, n in incs:
                        inst = inst.then_inc(self.sems[s], n)

            getattr(block, eng_name)(body)


def ceil_div(a, b):
    return (a + b - 1) // b


def bcast_row(p, psum, dst, src_row, ones_col, n, waits=(), npart=128):
    key = "_bcfree%d" % id(psum)
    for c0 in range(0, n, 512):
        w = min(512, n - c0)
        t = p.te(lambda e, c0=c0, w=w: e.matmul(
            psum[0:npart, c0:c0 + w], ones_col[0:1, 0:npart], src_row[0:1, c0:c0 + w],
            start=True, stop=True),
            waits=list(waits) + [("VE", p.counts.get(key, 0))], tick=(c0 + 512 >= n))
    v = p.ve(lambda e: e.tensor_copy(dst[0:npart, 0:n], psum[0:npart, 0:n]),
             waits=[("TE", p.c("TE"))])
    p.counts[key] = v
    return v


def emit_ln_tile(p, dst, xa, scratch, stat, eps_ap, g_ap=None, b_ap=None, waits=(),
                 out_dtype_bf=None):
    c_sc = p.sc(lambda e: e.activation(scratch[:, 0:D], xa[:, 0:D], AF.Square,
                                              accum_out=stat[:, 0:1]), waits=waits)
    c_ve = p.ve(lambda e: e.reduce_sum(stat[:, 1:2], xa[:, 0:D], axis=AX.X),
                waits=waits)
    c_ve = p.ve(lambda e: e.tensor_scalar(stat[:, 2:3], stat[:, 1:2], 1.0 / D, None, ALU.mult),
                waits=[("VE", c_ve)])
    c_sc2 = p.sc(lambda e: e.activation(stat[:, 3:4], stat[:, 2:3], AF.Square),
                 waits=[("VE", c_ve)])
    c_ve = p.ve(lambda e: e.tensor_scalar(stat[:, 4:5], stat[:, 0:1], 1.0 / D, None, ALU.mult),
                waits=[("SC", c_sc)])
    c_ve = p.ve(lambda e: e.tensor_tensor(stat[:, 4:5], stat[:, 4:5], stat[:, 3:4], ALU.subtract),
                waits=[("SC", c_sc2), ("VE", c_ve)])
    c_sc3 = p.sc(lambda e: e.activation(stat[:, 5:6], stat[:, 4:5], AF.Sqrt, bias=eps_ap),
                 waits=[("VE", c_ve)])
    c_ve = p.ve(lambda e: e.reciprocal(stat[:, 6:7], stat[:, 5:6]),
                waits=[("SC", c_sc3)])
    c_ve = p.ve(lambda e: e.tensor_scalar(stat[:, 7:8], stat[:, 2:3], stat[:, 6:7], -1.0, ALU.mult, ALU.mult),
                waits=[("VE", c_ve)])
    tgt = dst if g_ap is None else scratch
    c_sc4 = p.sc(lambda e: e.activation(tgt[:, 0:D], xa[:, 0:D], AF.Identity,
                                               bias=stat[:, 7:8], scale=stat[:, 6:7]),
                 waits=[("VE", c_ve)])
    if g_ap is not None:
        c_ve = p.ve(lambda e: e.tensor_tensor(scratch[:, 0:D], scratch[:, 0:D], g_ap, ALU.mult),
                    waits=[("SC", c_sc4)])
        c_ve = p.ve(lambda e: e.tensor_tensor(dst[:, 0:D], scratch[:, 0:D], b_ap, ALU.add),
                    waits=[("VE", c_ve)])
        return {"VE": c_ve}
    return {"SC": c_sc4}


def emit_transpose_to_fm(p, dst_view_of_half, src_f32, pst, ident, waits=()):
    key = "_ptfree%d" % id(pst)
    for half in range(2):
        tecnt = None
        for i in range(4):
            c = half * 4 + i
            tecnt = p.te(lambda e, c=c, i=i: e.transpose(
                pst[:, i * 128:(i + 1) * 128], src_f32[:, c * 128:(c + 1) * 128], ident),
                waits=list(waits) + [("VE", p.counts.get(key, 0))],
                tick=(i == 3))
        v = p.ve(lambda e, half=half: e.tensor_copy(
            dst_view_of_half(half),
            pst[:, 0:512].rearrange("p (c t) -> p c t", t=128)),
            waits=[("TE", tecnt)])
        p.counts[key] = v
    return p.c("VE")


class WeightLoader:

    def __init__(self, p, stg, stg_bytes_cols):
        self.p = p
        self.stgs = stg if isinstance(stg, list) else [stg]
        self.cols = stg_bytes_cols
        self.i = 0
        self.busy = {}
        self.busy_until = 0

    def load(self, dram_ap, dst_bf, dst_col0, ncols, extra_waits=()):
        p = self.p
        assert ncols <= self.cols
        bi = self.i % len(self.stgs)
        self.i += 1
        stg = self.stgs[bi]
        shp = dram_ap.shape
        if len(shp) == 3:
            stg_view = stg[:, 0:ncols].rearrange("p (a n) -> p a n", n=shp[2])
        else:
            stg_view = stg[:, 0:ncols]
        ld = p.ld(lambda e, stg_view=stg_view: e.dma_start(out=stg_view, in_=dram_ap),
                  waits=[("GP", self.busy.get(bi, 0))] + list(extra_waits))
        g = p.gp(lambda e, stg=stg: e.tensor_copy(dst_bf[:, dst_col0:dst_col0 + ncols],
                                                  stg[:, 0:ncols]),
                 waits=[("LD", ld)])
        self.busy[bi] = g
        self.busy_until = g
        return g


def emit_wconv(nc, p, stgs, bscs, jobs, tag, via_gp=False):
    n = len(stgs)
    emit_ld = p.gld if via_gp else p.ld
    ldsem = "GD" if via_gp else "LD"
    for i, (src, dst, ncols) in enumerate(jobs):
        bi = i % n
        stg_view = (stgs[bi][:, 0:ncols] if len(src.shape) == 2 else
                    stgs[bi][:, 0:ncols].rearrange("p (a n) -> p a n", n=src.shape[-1]))
        ld = emit_ld(lambda e, src=src, sv=stg_view: e.dma_start(out=sv, in_=src),
                     waits=[("GP", p.counts.get("_wcg%s%d" % (tag, bi), 0))])
        g = p.gp(lambda e, bi=bi, ncols=ncols: e.tensor_copy(
            bscs[bi][:, 0:ncols], stgs[bi][:, 0:ncols]),
            waits=[(ldsem, ld), ("GD", p.counts.get("_wcd%s%d" % (tag, bi), 0))])
        p.counts["_wcg%s%d" % (tag, bi)] = g
        st = p.gld(lambda e, dst=dst, bi=bi, ncols=ncols: e.dma_start(
            out=dst, in_=bscs[bi][:, 0:ncols].rearrange("p (a n) -> p a n", n=dst.shape[-1])
            if len(dst.shape) == 3 else bscs[bi][:, 0:ncols]),
            waits=[("GP", g)])
        p.counts["_wcd%s%d" % (tag, bi)] = st
    return p.c("GD")



def _f(x):
    return np.ascontiguousarray(x, dtype=np.float32)


def make_consts():
    c = {}
    c["ident"] = np.eye(128, dtype=np.float32)
    c["identb"] = np.eye(128, dtype=np.float32)
    c["ones_row"] = np.ones((1, 128), dtype=np.float32)
    hd = 64
    inv_freq = 1.0 / (10000.0 ** (np.arange(0, hd, 2, dtype=np.float32) / hd))
    t = np.arange(N_TOK, dtype=np.float32)
    freqs = np.einsum("i,j->ij", t, inv_freq)
    emb = np.concatenate([freqs, freqs], axis=1)
    cosT = np.cos(emb).T.astype(np.float32)
    sinT = np.sin(emb).T.astype(np.float32)
    c["cosF"] = np.tile(cosT, (2, 1))
    c["sinF"] = np.tile(sinT, (2, 1))
    return c


def per_part(v, width=128):
    v = np.asarray(v, dtype=np.float32)
    n = v.shape[0] // width
    return np.ascontiguousarray(v.reshape(n, width).T)


def emit_l0_ffn_half(nc, p, io):
    w1scr_d = io["w1scr_d"]
    out1_d = io["out1_d"]
    x_d = io["x_d"]
    w1_d = io["w1_d"]
    w2_d = io["w2_d"]
    b1_d = io["b1_d"]
    b2_d = io["b2_d"]
    g2_d = io["g2_d"]
    bb2_d = io["bb2_d"]
    ident_d = io["ident_d"]
    ones_d = io["ones_d"]
    x1_d = io["x1_d"]


    stack = ExitStack()
    with stack:
        sb = lambda name, shape, dt=F32: stack.enter_context(nc.sbuf_tensor(name, shape, dt, allow_name_mangling=True))
        ps = lambda name, shape: stack.enter_context(nc.psum_tensor(name, shape, F32, allow_name_mangling=True))
        W1S = [sb("W1S0", [128, 8 * 128], BF16), sb("W1S1", [128, 8 * 128], BF16)]
        W2B = sb("W2B", [128, 32 * 1024], BF16)
        STG = sb("STG", [128, 2048], F32)
        OC = sb("OC", [128, 4 * 1024], F32)
        XT = sb("XT", [128, 8 * 512], BF16)
        HFM = sb("HFM", [128, 32 * 512], BF16)
        B1 = sb("B1", [128, 32], F32)
        B2T = sb("B2T", [128, 1024], F32)
        G2T = sb("G2T", [128, 1024], F32)
        BB2T = sb("BB2T", [128, 1024], F32)
        IDENT = sb("IDENT", [128, 128], F32)
        ONESR = sb("ONESR", [1, 128], F32)
        XTILE = sb("XTILE", [128, 1024], F32)
        ES = sb("ES", [128, 2 * 512], F32)
        ES2 = sb("ES2", [128, 2 * 512], F32)
        YS = sb("YS", [128, 1024], F32)
        LNS = sb("LNS", [128, 1024], F32)
        STAT = sb("STAT", [128, 16], F32)
        CONST = sb("CONST", [128, 4], F32)
        PT = ps("PT", [128, 512])
        PH = [ps("PH0", [128, 512]), ps("PH1", [128, 512])]
        PY = [ps("PY0", [128, 1024]), ps("PY1", [128, 1024])]
        PB = PY[0]

        p.ve(lambda e: e.memset(CONST[:, 0:1], 1e-6))
        p.ve(lambda e: e.memset(CONST[:, 1:2], 1e-5))
        p.ld(lambda e: e.dma_start(out=IDENT[:, :], in_=ident_d))
        p.ld(lambda e: e.dma_start(out=ONESR[:, :], in_=ones_d))
        p.ld(lambda e: e.dma_start(out=B1[:, :], in_=b1_d))
        ROW = sb("ROW", [1, 3 * 1024], F32)
        p.ld(lambda e: e.dma_start(out=ROW[0:1, 0:1024], in_=b2_d))
        p.ld(lambda e: e.dma_start(out=ROW[0:1, 1024:2048], in_=g2_d))
        ldc = p.ld(lambda e: e.dma_start(out=ROW[0:1, 2048:3072], in_=bb2_d))
        bcast_row(p, PB, B2T, ROW[0:1, 0:1024], ONESR[0:1, :], 1024, waits=[("LD", ldc)])
        bcast_row(p, PB, G2T, ROW[0:1, 1024:2048], ONESR[0:1, :], 1024, waits=[("LD", ldc)])
        bcast_row(p, PB, BB2T, ROW[0:1, 2048:3072], ONESR[0:1, :], 1024, waits=[("LD", ldc)])

        wl = WeightLoader(p, STG, 2048)
        for c in range(16):
            wl.load(w2_d[c * 256:(c + 1) * 256, :].rearrange("(a p) n -> p a n", p=128),
                    W2B, c * 2048, 2048)
        w_ready = p.c("GP")
        BSC1 = sb("BSC1", [128, 1024], BF16)
        BSC2 = sb("BSC2", [128, 1024], BF16)
        STG2 = sb("STG2", [128, 1024], F32)
        w1jobs = [(w1_d[:, dffc * 128:(dffc + 1) * 128].rearrange("(c p) n -> p c n", p=128),
                   w1scr_d[dffc].rearrange("c p f -> p c f"), 1024) for dffc in range(32)]
        p.counts["_wcga0"] = wl.busy_until
        emit_wconv(nc, p, [STG, STG2], [BSC1, BSC2], w1jobs, "a")
        w1scr_ready = p.c("GD")

        def load_w1(dffc, buf):
            ld = p.ld(lambda e, dffc=dffc, buf=buf: e.dma_start(
                out=W1S[buf][:, :].rearrange("p (c f) -> p c f", f=128),
                in_=w1scr_d[dffc].rearrange("c p f -> p c f")),
                waits=[("GD", w1scr_ready), ("TE", p.counts.get("_w1used%d" % buf, 0))])
            return ld

        n_chunk = 8
        XTV = XT[:, :].rearrange("p (c t) -> p c t", t=512)
        for ch in range(n_chunk):
            t0 = ch * 512
            lo = p.ld(lambda e, t0=t0: e.dma_start(out=OC[:, :].rearrange("p (s n) -> p s n", n=1024),
                                                        in_=out1_d[t0:t0 + 512, :].rearrange("(s p) n -> p s n", p=128)),
                      waits=[("TE", p.counts.get("_octe", 0)), ("VE", p.counts.get("_ocve", 0))])
            for sub in range(4):
                emit_transpose_to_fm(
                    p, lambda half, sub=sub: XTV[:, :, sub * 128:sub * 128 + 128][:, half * 4:half * 4 + 4, :],
                    OC[:, sub * 1024:(sub + 1) * 1024], PT, IDENT[:, :], waits=[("LD", lo)])
            p.counts["_octe"] = p.c("TE")
            xt_ready = p.c("VE")
            w1g = [None, None]
            w1g[0] = load_w1(0, 0)
            w1g[1] = load_w1(1, 1)
            for dffc in range(32):
                pb = dffc % 2
                eb = pb * 512
                for kc in range(8):
                    p.te(lambda e, pb=pb, kc=kc: e.matmul(
                        PH[pb][:, :],
                        W1S[pb][:, kc * 128:(kc + 1) * 128],
                        XT[:, kc * 512:(kc + 1) * 512],
                        start=(kc == 0), stop=(kc == 7)),
                        waits=[("LD", w1g[pb]), ("GP", w_ready), ("VE", xt_ready),
                               ("SC", p.counts.get("_phfree%d" % pb, 0))],
                        tick=(kc == 7))
                mmc = p.c("TE")
                p.counts["_w1used%d" % pb] = mmc
                if dffc + 2 < 32:
                    w1g[pb] = load_w1(dffc + 2, pb)
                s1 = p.sc(lambda e, pb=pb, eb=eb, dffc=dffc: e.activation(
                    ES[:, eb:eb + 512], PH[pb][:, :], AF.Identity, bias=B1[:, dffc:dffc + 1]),
                    waits=[("TE", mmc), ("VE", p.counts.get("_esfree%d" % pb, 0))])
                p.counts["_phfree%d" % pb] = s1
                v1 = p.ve(lambda e, eb=eb: e.tensor_scalar(
                    ES2[:, eb:eb + 512], ES[:, eb:eb + 512], 0.0, None, ALU.min),
                    waits=[("SC", s1)])
                s2 = p.sc(lambda e, eb=eb: e.activation(
                    ES2[:, eb:eb + 512], ES2[:, eb:eb + 512], AF.Exp),
                    waits=[("VE", v1)])
                v2 = p.ve(lambda e, eb=eb: e.tensor_scalar(
                    ES[:, eb:eb + 512], ES[:, eb:eb + 512], 0.0, -1.0, ALU.max, ALU.add),
                    waits=[("SC", s1), ("VE", v1)])
                vh = p.ve(lambda e, eb=eb, dffc=dffc: e.tensor_tensor(
                    HFM[:, dffc * 512:(dffc + 1) * 512], ES2[:, eb:eb + 512], ES[:, eb:eb + 512], ALU.add),
                    waits=[("SC", s2), ("VE", v2)])
                p.counts["_esfree%d" % pb] = vh
            h_ready = p.c("VE")
            for sub in range(4):
                pb = sub % 2
                for dffc in range(32):
                    for nh in range(2):
                        p.te(lambda e, pb=pb, dffc=dffc, nh=nh, sub=sub: e.matmul(
                            PY[pb][:, nh * 512:(nh + 1) * 512],
                            HFM[:, dffc * 512 + sub * 128:dffc * 512 + (sub + 1) * 128],
                            W2B[:, dffc * 1024 + nh * 512:dffc * 1024 + (nh + 1) * 512],
                            start=(dffc == 0), stop=(dffc == 31)),
                            waits=[("VE", h_ready), ("SC", p.counts.get("_pyfree%d" % pb, 0))],
                            tick=(dffc == 31 and nh == 1))
                mmy = p.c("TE")
                lx = p.ld(lambda e, t0=t0, sub=sub: e.dma_start(
                    out=XTILE[:, :], in_=x_d[t0 + sub * 128:t0 + sub * 128 + 128, :]),
                    waits=[("VE", p.counts.get("_xtile_free", 0))])
                sy = p.sc(lambda e, pb=pb: e.activation(
                    YS[:, :], PY[pb][:, :], AF.Identity),
                    waits=[("TE", mmy), ("ST", p.counts.get("_ysfree", 0))])
                p.counts["_pyfree%d" % pb] = sy
                vy = p.ve(lambda e: e.tensor_tensor(YS[:, :], YS[:, :], B2T[:, :], ALU.add),
                          waits=[("SC", sy)])
                vy = p.ve(lambda e, sub=sub: e.tensor_tensor(
                    YS[:, :], YS[:, :], OC[:, sub * 1024:(sub + 1) * 1024], ALU.add),
                    waits=[("VE", vy)])
                p.counts["_ocve"] = vy
                r = emit_ln_tile(p, YS, YS, LNS, STAT, CONST[:, 0:1], g_ap=G2T[:, :], b_ap=BB2T[:, :],
                                 waits=[("VE", vy)])
                vv = p.ve(lambda e: e.tensor_tensor(YS[:, :], YS[:, :], XTILE[:, :], ALU.add),
                          waits=[("LD", lx), ("VE", r.get("VE", 0)), ("SC", r.get("SC", 0))])
                p.counts["_xtile_free"] = vv
                r2 = emit_ln_tile(p, YS, YS, LNS, STAT, CONST[:, 1:2], waits=[("VE", vv)])
                p.counts["_ysfree"] = p.stv(
                    lambda e, t0=t0, sub=sub: e.dma_start(
                        out=x1_d[t0 + sub * 128:t0 + sub * 128 + 128, :], in_=YS[:, :]),
                    waits=[("SC", r2.get("SC", 0)), ("VE", r2.get("VE", 0))])
        p.emit("sync", lambda e: e.nop(), waits=[("ST", p.c("ST"))])

        p.fence()
        with nc.Block() as block:
            p.flush(block)


def build_l0_ffn_half():
    nc = bass.Bass("TRN2", target_bir_lowering=False)
    out1_d = nc.dram_tensor("out1", [N_TOK, D], F32, kind="ExternalInput").ap()
    x_d = nc.dram_tensor("x", [N_TOK, D], F32, kind="ExternalInput").ap()
    w1_d = nc.dram_tensor("W1", [D, 4096], F32, kind="ExternalInput").ap()
    w2_d = nc.dram_tensor("W2", [4096, D], F32, kind="ExternalInput").ap()
    b1_d = nc.dram_tensor("b1pt", [128, 32], F32, kind="ExternalInput").ap()
    b2_d = nc.dram_tensor("b2row", [1, D], F32, kind="ExternalInput").ap()
    g2_d = nc.dram_tensor("g2row", [1, D], F32, kind="ExternalInput").ap()
    bb2_d = nc.dram_tensor("bb2row", [1, D], F32, kind="ExternalInput").ap()
    ident_d = nc.dram_tensor("ident", [128, 128], F32, kind="ExternalInput").ap()
    ones_d = nc.dram_tensor("ones_row", [1, 128], F32, kind="ExternalInput").ap()
    x1_d = nc.dram_tensor("x1", [N_TOK, D], F32, kind="ExternalOutput").ap()
    w1scr_d = nc.dram_tensor("w1scr", [32, 8, 128, 128], BF16)
    io = dict(out1_d=out1_d, x_d=x_d, w1_d=w1_d, w2_d=w2_d, b1_d=b1_d, b2_d=b2_d, g2_d=g2_d, bb2_d=bb2_d, ident_d=ident_d, ones_d=ones_d, x1_d=x1_d, w1scr_d=w1scr_d)
    with ExitStack() as _st:
        p = Prog(nc, _st)
        emit_l0_ffn_half(nc, p, io)
    return nc


def run_l0_ffn_half(out1_full, x_full, W1, W2, b1, b2, g2, bb2, n_launch_counter=None):
    consts = make_consts()
    nc = build_l0_ffn_half()
    in_maps = []
    for c in range(8):
        in_maps.append({
            "out1": _f(out1_full[c]), "x": _f(x_full[c]),
            "W1": _f(W1), "W2": _f(W2),
            "b1pt": per_part(b1), "b2row": _f(b2).reshape(1, D),
            "g2row": _f(g2).reshape(1, D), "bb2row": _f(bb2).reshape(1, D),
            "ident": consts["ident"], "ones_row": consts["ones_row"],
        })
    res = run_bass_kernel_spmd(nc, in_maps, core_ids=list(range(8)))
    return np.stack([np.asarray(r["x1"]) for r in res.results])


def emit_l0_moe(nc, p, io):
    x1_d = io["x1_d"]
    w1_d = io["w1_d"]
    w2_d = io["w2_d"]
    b1_d = io["b1_d"]
    b2_d = io["b2_d"]
    cen_d = io["cen_d"]
    crt_d = io["crt_d"]
    ident_d = io["ident_d"]
    ones_d = io["ones_d"]
    x2_d = io["x2_d"]


    stack = ExitStack()
    with stack:
        sb = lambda name, shape, dt=F32: stack.enter_context(nc.sbuf_tensor(name, shape, dt, allow_name_mangling=True))
        ps = lambda name, shape: stack.enter_context(nc.psum_tensor(name, shape, F32, allow_name_mangling=True))
        W1B = sb("W1B", [128, 8 * 4096], BF16)
        W2B = sb("W2B", [128, 8 * 4096], BF16)
        STG = sb("STG", [128, 2048], F32)
        X1C = sb("X1C", [128, 4 * 1024], F32)
        X1T = sb("X1T", [128, 8 * 512], BF16)
        HBF = sb("HBF", [128, 4 * 512], BF16)
        YS = sb("YSB", [128, 4 * 1024], F32)
        CRS = sb("CRS", [128, 8 * 16], BF16)
        CEN = sb("CEN", [8, 1024], F32)
        B2A = sb("B2A", [8, 1024], BF16)
        INVCB = sb("INVCB", [128, 8], F32)
        IDENT = sb("IDENT", [128, 128], F32)
        ONESR = sb("ONESR", [1, 128], F32)
        B1PT = sb("B1PT", [128, 32], F32)
        GATES = sb("GATES", [128, 4 * 8], F32)
        GTB = sb("GTB", [8, 4 * 128], BF16)
        SSV = sb("SSV", [128, 32], F32)
        ROWS = sb("ROWS", [8, 16], F32)
        LNS = sb("LNS", [128, 1024], F32)
        STAT = sb("STAT", [128, 32], F32)
        CONST = sb("CONST", [128, 4], F32)
        PT = ps("PT", [128, 512])
        PHE = [ps("PHE0", [128, 512]), ps("PHE1", [128, 512])]
        PYE = [ps("PYE0", [128, 1024]), ps("PYE1", [128, 1024])]
        PLOG = ps("PLOG", [128, 512])

        p.ve(lambda e: e.memset(CONST[:, 0:1], 1e-6))
        p.ve(lambda e: e.memset(CONST[:, 1:2], 1e-5))
        p.ve(lambda e: e.memset(CONST[:, 2:3], 1e-12))
        p.ld(lambda e: e.dma_start(out=IDENT[:, :], in_=ident_d))
        p.ld(lambda e: e.dma_start(out=ONESR[:, :], in_=ones_d))
        lcen = p.ld(lambda e: e.dma_start(out=CEN[:, :], in_=cen_d))
        p.ld(lambda e: e.dma_start(out=B1PT[:, :], in_=b1_d))
        STG2M = sb("STG2M", [128, 2048], F32)
        wl = WeightLoader(p, [STG, STG2M], 2048)
        for c in range(8):
            wl.load(crt_d[c * 128:(c + 1) * 128, :], CRS, c * 16, 16)
        ldb2 = p.ld(lambda e: e.dma_start(out=STG[0:8, 0:1024], in_=b2_d[:, :]),
                    waits=[("GP", wl.busy_until)])
        wl.busy_until = p.gp(lambda e: e.tensor_copy(B2A[0:8, :], STG[0:8, 0:1024]),
                             waits=[("LD", ldb2)])
        for e in range(8):
            for c in range(4):
                wl.load(w1_d[e, c * 256:(c + 1) * 256, :].rearrange("(a p) n -> p a n", p=128),
                        W1B, e * 4096 + c * 1024, 1024)
        for e in range(8):
            for c in range(4):
                wl.load(w2_d[e, c * 128:(c + 1) * 128, :], W2B, e * 4096 + c * 1024, 1024)
        w_ready = p.c("GP")
        if "mw1_d" in io:
            STGC = sb("STGCm", [128, 1024], F32)
            BSCC = sb("BSCCm", [128, 1024], BF16)
            mjobs = []
            for e_ in range(*io["mconv_range"]):
                for c in range(8):
                    for wofs, wd in ((0, io["mw1_d"]), (1024, io["mw3_d"])):
                        mjobs.append((wd[e_, c * 128:(c + 1) * 128, 0:1024],
                                      io["mw13s"][e_, 0:8, :, wofs + c * 128:wofs + (c + 1) * 128].rearrange("k p f -> p k f"), 1024))
                        mjobs.append((wd[e_, c * 128:(c + 1) * 128, 1024:1408],
                                      io["mw13s"][e_, 8:11, :, wofs + c * 128:wofs + (c + 1) * 128].rearrange("k p f -> p k f"), 384))
                for k_ in range(11):
                    mjobs.append((io["mw2_d"][e_, k_ * 128:(k_ + 1) * 128, :],
                                  io["mw2s"][e_, :, k_ * 1024:(k_ + 1) * 1024], 1024))
            emit_wconv(nc, p, [STGC], [BSCC], mjobs, "m", via_gp=True)

        s0 = p.sc(lambda e: e.activation(LNS[0:8, 0:1024], CEN[:, :], AF.Square,
                                         accum_out=ROWS[0:8, 0:1]), waits=[("LD", lcen)])
        s1 = p.sc(lambda e: e.activation(ROWS[0:8, 1:2], ROWS[0:8, 0:1], AF.Sqrt),
                  waits=[("SC", s0)])
        v1 = p.ve(lambda e: e.tensor_scalar(ROWS[0:8, 2:3], ROWS[0:8, 1:2], 1e-12, None, ALU.max),
                  waits=[("SC", s1)])
        v1 = p.ve(lambda e: e.reciprocal(ROWS[0:8, 3:4], ROWS[0:8, 2:3]), waits=[("VE", v1)])
        t1 = p.te(lambda e: e.transpose(PT[0:1, 0:8], ROWS[0:8, 3:4], IDENT[0:8, 0:8]),
                  waits=[("VE", v1), ("LD", p.c("LD"))], tick=True)
        v1 = p.ve(lambda e: e.tensor_copy(SSV[0:1, 0:8], PT[0:1, 0:8]), waits=[("TE", t1)])
        bcast_row(p, PLOG, INVCB, SSV[0:1, 0:8], ONESR[0:1, :], 8, waits=[("VE", v1)])

        n_chunk = 8
        X1TV = X1T[:, :].rearrange("p (c t) -> p c t", t=512)
        for ch in range(n_chunk):
            t0 = ch * 512
            lo = p.ld(lambda e, t0=t0: e.dma_start(out=X1C[:, :].rearrange("p (s n) -> p s n", n=1024),
                                                   in_=x1_d[t0:t0 + 512, :].rearrange("(s p) n -> p s n", p=128)),
                      waits=[("TE", p.counts.get("_x1cte", 0)), ("VE", p.counts.get("_x1cve", 0)),
                             ("SC", p.counts.get("_x1csc", 0))])
            for sub in range(4):
                emit_transpose_to_fm(
                    p, lambda half, sub=sub: X1TV[:, :, sub * 128:sub * 128 + 128][:, half * 4:half * 4 + 4, :],
                    X1C[:, sub * 1024:(sub + 1) * 1024], PT, IDENT[:, :], waits=[("LD", lo)])
            p.counts["_x1cte"] = p.c("TE")
            xt_ready = p.c("VE")
            for sub in range(4):
                sq = p.sc(lambda e, sub=sub: e.activation(LNS[:, 0:1024], X1C[:, sub * 1024:(sub + 1) * 1024],
                                                          AF.Square, accum_out=SSV[:, sub * 4:sub * 4 + 1]),
                          waits=[("LD", lo), ("VE", p.counts.get("_lnsv", 0))])
                p.counts["_x1csc"] = sq
                sq = p.sc(lambda e, sub=sub: e.activation(SSV[:, sub * 4 + 1:sub * 4 + 2],
                                                          SSV[:, sub * 4:sub * 4 + 1], AF.Sqrt),
                          waits=[("SC", sq)])
                vq = p.ve(lambda e, sub=sub: e.tensor_scalar(SSV[:, sub * 4 + 1:sub * 4 + 2],
                                                             SSV[:, sub * 4 + 1:sub * 4 + 2],
                                                             CONST[:, 2:3], None, ALU.max),
                          waits=[("SC", sq)])
                vq = p.ve(lambda e, sub=sub: e.reciprocal(SSV[:, sub * 4 + 2:sub * 4 + 3],
                                                          SSV[:, sub * 4 + 1:sub * 4 + 2]),
                          waits=[("VE", vq)])
                for c in range(8):
                    p.te(lambda e, c=c, sub=sub: e.matmul(
                        PLOG[:, 0:16], X1TV[:, c, sub * 128:sub * 128 + 128],
                        CRS[:, c * 16:(c + 1) * 16],
                        start=(c == 0), stop=(c == 7)),
                        waits=[("GP", w_ready), ("VE", xt_ready),
                               ("VE", p.counts.get("_plogfree", 0))],
                        tick=(c == 7))
                tl = p.c("TE")
                vq = p.ve(lambda e: e.tensor_copy(SSV[:, 16:32], PLOG[:, 0:16]),
                          waits=[("TE", tl), ("VE", vq)])
                p.counts["_plogfree"] = vq
                vq = p.ve(lambda e, sub=sub: e.tensor_scalar(SSV[:, 16:24], SSV[:, 16:24],
                                                             SSV[:, sub * 4 + 2:sub * 4 + 3], None, ALU.mult),
                          waits=[("VE", vq)])
                vq = p.ve(lambda e: e.tensor_tensor(SSV[:, 16:24], SSV[:, 16:24], INVCB[:, 0:8], ALU.mult),
                          waits=[("VE", vq)])
                vq = p.ve(lambda e: e.tensor_tensor(SSV[:, 16:24], SSV[:, 16:24], SSV[:, 24:32], ALU.add),
                          waits=[("VE", vq)])
                vq = p.ve(lambda e: e.max(STAT[:, 0:8], SSV[:, 16:24]), waits=[("VE", vq)])
                vq = p.ve(lambda e: e.tensor_scalar(STAT[:, 8:16], SSV[:, 16:24], STAT[:, 1:2], None, ALU.is_ge),
                          waits=[("VE", vq)])
                vq = p.ve(lambda e: e.tensor_scalar(STAT[:, 1:2], STAT[:, 0:1], -1.0, None, ALU.mult),
                          waits=[("VE", vq)])
                se = p.sc(lambda e: e.activation(SSV[:, 16:24], SSV[:, 16:24], AF.Exp, bias=STAT[:, 1:2]),
                          waits=[("VE", vq)])
                vq = p.ve(lambda e: e.tensor_tensor(SSV[:, 16:24], SSV[:, 16:24], STAT[:, 8:16], ALU.mult),
                          waits=[("SC", se)])
                vq = p.ve(lambda e: e.reduce_sum(STAT[:, 2:3], SSV[:, 16:24], axis=AX.X),
                          waits=[("VE", vq)])
                vq = p.ve(lambda e: e.reciprocal(STAT[:, 3:4], STAT[:, 2:3]), waits=[("VE", vq)])
                vq = p.ve(lambda e, sub=sub: e.tensor_scalar(GATES[:, sub * 8:(sub + 1) * 8],
                                                             SSV[:, 16:24], STAT[:, 3:4], None, ALU.mult),
                          waits=[("VE", vq)])
                tg = p.te(lambda e, sub=sub: e.transpose(PT[0:8, 0:128], GATES[:, sub * 8:(sub + 1) * 8],
                                                         IDENT[:, :]),
                          waits=[("VE", vq), ("VE", p.counts.get("_ptfree%d" % id(PT), 0))], tick=True)
                vq = p.ve(lambda e, sub=sub: e.tensor_copy(GTB[0:8, sub * 128:(sub + 1) * 128], PT[0:8, 0:128]),
                          waits=[("TE", tg)])
                p.counts["_ptfree%d" % id(PT)] = vq
                tb = None
                for nh in range(2):
                    tb = p.te(lambda e, sub=sub, nh=nh: e.matmul(
                        PYE[sub % 2][:, nh * 512:(nh + 1) * 512],
                        GTB[0:8, sub * 128:(sub + 1) * 128],
                        B2A[0:8, nh * 512:(nh + 1) * 512],
                        start=True, stop=True),
                        waits=[("VE", vq), ("SC", p.counts.get("_pyefree%d" % (sub % 2), 0)),
                               ("VE", p.counts.get("_pyefree_v%d" % (sub % 2), 0))],
                        tick=(nh == 1))
                vy = p.ve(lambda e, sub=sub: e.tensor_copy(YS[:, sub * 1024:(sub + 1) * 1024],
                                                           PYE[sub % 2][:, :]),
                          waits=[("TE", tb), ("ST", p.counts.get("_ys_st%d" % sub, 0))])
                p.counts["_pyefree_v%d" % (sub % 2)] = vy
            gates_ready_ve = p.c("VE")
            for ex in range(8):
                for dffc in range(4):
                    pb = dffc % 2
                    for c in range(8):
                        p.te(lambda e, ex=ex, dffc=dffc, c=c, pb=pb: e.matmul(
                            PHE[pb][:, :],
                            W1B[:, ex * 4096 + c * 512 + dffc * 128:ex * 4096 + c * 512 + (dffc + 1) * 128],
                            X1T[:, c * 512:(c + 1) * 512],
                            start=(c == 0), stop=(c == 7)),
                            waits=[("GP", w_ready), ("VE", xt_ready),
                                   ("SC", p.counts.get("_phefree%d" % pb, 0))],
                            tick=(c == 7))
                    th = p.c("TE")
                    sg = p.sc(lambda e, ex=ex, dffc=dffc, pb=pb: e.activation(
                        HBF[:, dffc * 512:(dffc + 1) * 512], PHE[pb][:, :], AF.Gelu,
                        bias=B1PT[:, ex * 4 + dffc:ex * 4 + dffc + 1]),
                        waits=[("TE", th), ("TE", p.counts.get("_hbfused", 0))])
                    p.counts["_phefree%d" % pb] = sg
                for sub in range(4):
                    pb = sub % 2
                    for dffc in range(4):
                        for nh in range(2):
                            p.te(lambda e, ex=ex, dffc=dffc, nh=nh, sub=sub, pb=pb: e.matmul(
                                PYE[pb][:, nh * 512:(nh + 1) * 512],
                                HBF[:, dffc * 512 + sub * 128:dffc * 512 + (sub + 1) * 128],
                                W2B[:, ex * 4096 + dffc * 1024 + nh * 512:ex * 4096 + dffc * 1024 + (nh + 1) * 512],
                                start=(dffc == 0), stop=(dffc == 3)),
                                waits=[("SC", p.c("SC")), ("SC", p.counts.get("_pyefree%d" % pb, 0)),
                                       ("VE", p.counts.get("_pyefree_v%d" % pb, 0))],
                                tick=(dffc == 3 and nh == 1))
                    ty = p.c("TE")
                    sg = p.sc(lambda e, ex=ex, sub=sub, pb=pb: e.activation(
                        LNS[:, 0:1024], PYE[pb][:, :], AF.Identity,
                        scale=GATES[:, sub * 8 + ex:sub * 8 + ex + 1]),
                        waits=[("TE", ty), ("VE", p.counts.get("_lnsv", 0))])
                    p.counts["_pyefree%d" % pb] = sg
                    vy = p.ve(lambda e, sub=sub: e.tensor_tensor(
                        YS[:, sub * 1024:(sub + 1) * 1024], YS[:, sub * 1024:(sub + 1) * 1024],
                        LNS[:, 0:1024], ALU.add),
                        waits=[("SC", sg), ("VE", gates_ready_ve)])
                    p.counts["_lnsv"] = vy
                p.counts["_hbfused"] = p.c("TE")
            for sub in range(4):
                ysub = YS[:, sub * 1024:(sub + 1) * 1024]
                vv = p.ve(lambda e, ysub=ysub, sub=sub: e.tensor_tensor(
                    ysub, ysub, X1C[:, sub * 1024:(sub + 1) * 1024], ALU.add),
                    waits=[("VE", p.c("VE"))])
                p.counts["_x1cve"] = vv
                r2 = emit_ln_tile(p, ysub, ysub, LNS, STAT[:, 16:24], CONST[:, 1:2],
                                  waits=[("VE", vv)])
                p.counts["_lnsv"] = max(p.counts.get("_lnsv", 0), r2.get("VE", 0))
                p.counts["_ys_st%d" % sub] = p.stv(
                    lambda e, t0=t0, sub=sub, ysub=ysub: e.dma_start(
                        out=x2_d[t0 + sub * 128:t0 + sub * 128 + 128, :], in_=ysub),
                    waits=[("SC", r2.get("SC", 0)), ("VE", r2.get("VE", 0))])
        p.emit("sync", lambda e: e.nop(), waits=[("ST", p.c("ST"))])
        p.fence()
        with nc.Block() as block:
            p.flush(block)


def build_l0_moe():
    nc = bass.Bass("TRN2", target_bir_lowering=False)
    x1_d = nc.dram_tensor("x1", [N_TOK, D], F32, kind="ExternalInput").ap()
    w1_d = nc.dram_tensor("mw1", [8, D, 512], F32, kind="ExternalInput").ap()
    w2_d = nc.dram_tensor("mw2", [8, 512, D], F32, kind="ExternalInput").ap()
    b1_d = nc.dram_tensor("mb1pt", [128, 32], F32, kind="ExternalInput").ap()
    b2_d = nc.dram_tensor("mb2", [8, D], F32, kind="ExternalInput").ap()
    cen_d = nc.dram_tensor("cen", [8, D], F32, kind="ExternalInput").ap()
    crt_d = nc.dram_tensor("crt", [D, 16], F32, kind="ExternalInput").ap()
    ident_d = nc.dram_tensor("ident", [128, 128], F32, kind="ExternalInput").ap()
    ones_d = nc.dram_tensor("ones_row", [1, 128], F32, kind="ExternalInput").ap()
    x2_d = nc.dram_tensor("x2", [N_TOK, D], F32, kind="ExternalOutput").ap()
    io = dict(x1_d=x1_d, w1_d=w1_d, w2_d=w2_d, b1_d=b1_d, b2_d=b2_d, cen_d=cen_d, crt_d=crt_d, ident_d=ident_d, ones_d=ones_d, x2_d=x2_d)
    with ExitStack() as _st:
        p = Prog(nc, _st)
        emit_l0_moe(nc, p, io)
    return nc


def run_l0_moe(x1_full, w1, b1, w2, b2, centroids, w_route):
    consts = make_consts()
    nc = build_l0_moe()
    crt = np.ascontiguousarray(np.concatenate([np.asarray(centroids).T, np.asarray(w_route).T], axis=1), dtype=np.float32)
    b1pt = np.ascontiguousarray(np.asarray(b1, dtype=np.float32).reshape(8, 4, 128).transpose(2, 0, 1).reshape(128, 32))
    in_maps = []
    for c in range(8):
        in_maps.append({
            "x1": _f(x1_full[c]), "mw1": _f(w1), "mw2": _f(w2),
            "mb1pt": b1pt, "mb2": _f(b2), "cen": _f(centroids), "crt": crt,
            "ident": consts["ident"], "ones_row": consts["ones_row"],
        })
    res = run_bass_kernel_spmd(nc, in_maps, core_ids=list(range(8)))
    return np.stack([np.asarray(r["x2"]) for r in res.results])


def emit_l1_moe(nc, p, io):
    x3_d = io["x3_d"]
    w1_d = io["w1_d"]
    w3_d = io["w3_d"]
    w2_d = io["w2_d"]
    b1_d = io["b1_d"]
    b3_d = io["b3_d"]
    b2_d = io["b2_d"]
    gw_d = io["gw_d"]
    gb_d = io["gb_d"]
    ident_d = io["ident_d"]
    ones_d = io["ones_d"]
    out_d = io["out_d"]
    w13s = io["w13s"]
    w2s = io["w2s"]
    DFF = 1408
    NE = 9


    stack = ExitStack()
    with stack:
        sb = lambda name, shape, dt=F32: stack.enter_context(nc.sbuf_tensor(name, shape, dt, allow_name_mangling=True))
        ps = lambda name, shape: stack.enter_context(nc.psum_tensor(name, shape, F32, allow_name_mangling=True))
        STG = sb("STG", [128, 1536], F32)
        BSC = sb("BSC", [128, 1536], BF16)
        X3C = sb("X3C", [128, 4 * 1024], F32)
        X3T = sb("X3T", [128, 8 * 1024], BF16)
        HBF = sb("HBF", [128, 11 * 1024], BF16)
        YS = sb("YSB", [128, 8 * 1024], F32)
        W2E = sb("W2E", [128, 11 * 1024], BF16)
        W13C = [sb("W13C%d" % _i, [128, 2 * 8 * 128], BF16) for _i in range(4)]
        GWB = sb("GWB", [128, 8 * 8], BF16)
        GBC = sb("GBC", [128, 8], F32)
        B1PT = sb("B1PT", [128, NE * 11], F32)
        B3PT = sb("B3PT", [128, NE * 11], F32)
        B2P = sb("B2P", [128, 3 * 1024], F32)
        RW = sb("RW", [128, 16 * 8], F32)
        SSV = sb("SSV", [128, 48], F32)
        ES = sb("ES", [128, 512], F32)
        ES2 = sb("ES2", [128, 512], F32)
        LNS = sb("LNS", [128, 1024], F32)
        STAT = sb("STAT", [128, 32], F32)
        CONST = sb("CONST", [128, 4], F32)
        IDENT = sb("IDENT", [128, 128], F32)
        ONESR = sb("ONESR", [1, 128], F32)
        ONESB = sb("ONESB", [128, 128], F32)
        PT = ps("PT", [128, 512])
        PH1 = ps("PH1", [128, 512])
        PH3 = ps("PH3", [128, 512])
        PY = [ps("PY0", [128, 1024]), ps("PY1", [128, 1024])]
        PLOG = ps("PLOG", [128, 512])

        p.ve(lambda e: e.memset(CONST[:, 1:2], 1e-5))
        ones_ready = p.ve(lambda e: e.memset(ONESB[:, :], 1.0))
        p.ld(lambda e: e.dma_start(out=IDENT[:, :], in_=ident_d))
        p.ld(lambda e: e.dma_start(out=ONESR[:, :], in_=ones_d))
        p.ld(lambda e: e.dma_start(out=B1PT[:, :], in_=b1_d))
        p.ld(lambda e: e.dma_start(out=B3PT[:, :], in_=b3_d))
        p.ld(lambda e: e.dma_start(out=B2P[:, :], in_=b2_d))
        lgb = p.ld(lambda e: e.dma_start(out=SSV[0:1, 0:8], in_=gb_d))
        bcast_row(p, PLOG, GBC, SSV[0:1, 0:8], ONESR[0:1, :], 8, waits=[("LD", lgb)])
        wl = WeightLoader(p, STG, 1536)
        for c in range(8):
            wl.load(gw_d[c * 128:(c + 1) * 128, :], GWB, c * 8, 8)
        gw_ready = p.c("GP")

        for e in (range(NE) if not io.get("preconverted") else []):
            for c in range(8):
                ldw = p.ld(lambda e_, e=e, c=c: e_.dma_start(
                    out=STG[:, 0:1408], in_=w1_d[e, c * 128:(c + 1) * 128, :]),
                    waits=[("GD", p.counts.get("_bscfree", 0))])
                g = p.gp(lambda e_: e_.tensor_copy(BSC[:, 0:1408], STG[:, 0:1408]),
                         waits=[("LD", ldw), ("GD", p.counts.get("_bscfree", 0))])
                st = p.gld(lambda e_, e=e, c=c: e_.dma_start(
                    out=w13s[e, :, :, c * 128:(c + 1) * 128].rearrange("k p f -> p k f"),
                    in_=BSC[:, 0:1408].rearrange("p (k f) -> p k f", f=128)),
                    waits=[("GP", g)])
                p.counts["_bscfree"] = st
                ldw = p.ld(lambda e_, e=e, c=c: e_.dma_start(
                    out=STG[:, 0:1408], in_=w3_d[e, c * 128:(c + 1) * 128, :]),
                    waits=[("GD", p.counts.get("_bscfree", 0))])
                g = p.gp(lambda e_: e_.tensor_copy(BSC[:, 0:1408], STG[:, 0:1408]),
                         waits=[("LD", ldw), ("GD", p.counts.get("_bscfree", 0))])
                st = p.gld(lambda e_, e=e, c=c: e_.dma_start(
                    out=w13s[e, :, :, 1024 + c * 128:1024 + (c + 1) * 128].rearrange("k p f -> p k f"),
                    in_=BSC[:, 0:1408].rearrange("p (k f) -> p k f", f=128)),
                    waits=[("GP", g)])
                p.counts["_bscfree"] = st
            for k in range(11):
                ldw = p.ld(lambda e_, e=e, k=k: e_.dma_start(
                    out=STG[:, 0:1024], in_=w2_d[e, k * 128:(k + 1) * 128, :]),
                    waits=[("GD", p.counts.get("_bscfree", 0))])
                g = p.gp(lambda e_: e_.tensor_copy(BSC[:, 0:1024], STG[:, 0:1024]),
                         waits=[("LD", ldw), ("GD", p.counts.get("_bscfree", 0))])
                st = p.gld(lambda e_, e=e, k=k: e_.dma_start(
                    out=w2s[e, :, k * 1024:(k + 1) * 1024], in_=BSC[:, 0:1024]),
                    waits=[("GP", g)])
                p.counts["_bscfree"] = st
        scr_ready = p.c("GD")

        X3TV = X3T[:, :].rearrange("p (c t) -> p c t", t=1024)
        for half in range(4):
            h0 = half * 1024
            for qc in range(2):
                t0 = h0 + qc * 512
                lo = p.ld(lambda e, t0=t0: e.dma_start(
                    out=X3C[:, :].rearrange("p (s n) -> p s n", n=1024),
                    in_=x3_d[t0:t0 + 512, :].rearrange("(s p) n -> p s n", p=128)),
                    waits=[("TE", p.counts.get("_x3cte", 0)), ("VE", p.counts.get("_x3cve", 0))])
                for sub in range(4):
                    emit_transpose_to_fm(
                        p, lambda hh, sub=sub, qc=qc: X3TV[:, :, qc * 512 + sub * 128:qc * 512 + sub * 128 + 128][:, hh * 4:hh * 4 + 4, :],
                        X3C[:, sub * 1024:(sub + 1) * 1024], PT, IDENT[:, :], waits=[("LD", lo)])
                p.counts["_x3cte"] = p.c("TE")
                for sub in range(4):
                    st_i = qc * 4 + sub
                    for c in range(8):
                        p.te(lambda e, c=c, qc=qc, sub=sub: e.matmul(
                            PLOG[:, 0:8], X3TV[:, c, qc * 512 + sub * 128:qc * 512 + sub * 128 + 128],
                            GWB[:, c * 8:(c + 1) * 8],
                            start=(c == 0), stop=(c == 7)),
                            waits=[("GP", gw_ready), ("VE", p.c("VE")),
                                   ("VE", p.counts.get("_plogfree", 0))],
                            tick=(c == 7))
                    tl = p.c("TE")
                    vq = p.ve(lambda e: e.tensor_copy(SSV[:, 8:16], PLOG[:, 0:8]), waits=[("TE", tl)])
                    p.counts["_plogfree"] = vq
                    vq = p.ve(lambda e: e.reduce_max(STAT[:, 0:1], SSV[:, 8:16], axis=AX.X, negate=True),
                              waits=[("VE", vq)])
                    se = p.sc(lambda e: e.activation(SSV[:, 16:24], SSV[:, 8:16], AF.Exp, bias=STAT[:, 0:1]),
                              waits=[("VE", vq)])
                    vq = p.ve(lambda e: e.reduce_sum(STAT[:, 1:2], SSV[:, 16:24], axis=AX.X),
                              waits=[("SC", se)])
                    vq = p.ve(lambda e: e.reciprocal(STAT[:, 2:3], STAT[:, 1:2]), waits=[("VE", vq)])
                    vq = p.ve(lambda e: e.tensor_scalar(SSV[:, 16:24], SSV[:, 16:24], STAT[:, 2:3], None, ALU.mult),
                              waits=[("VE", vq)])
                    vq = p.ve(lambda e: e.tensor_tensor(SSV[:, 24:32], SSV[:, 16:24], GBC[:, 0:8], ALU.add),
                              waits=[("VE", vq)])
                    vq = p.ve(lambda e: e.max(STAT[:, 8:16], SSV[:, 24:32]), waits=[("VE", vq)])
                    vq = p.ve(lambda e: e.tensor_scalar(SSV[:, 32:40], SSV[:, 24:32], STAT[:, 9:10], None, ALU.is_ge),
                              waits=[("VE", vq)])
                    vq = p.ve(lambda e, st_i=st_i: e.tensor_tensor(RW[:, st_i * 8:(st_i + 1) * 8],
                                                                   SSV[:, 32:40], SSV[:, 16:24], ALU.mult),
                              waits=[("VE", vq)])
            xt_ready = p.c("VE")
            rw_ready = p.c("VE")
            vz = None
            for sub in range(8):
                vz = p.ve(lambda e, sub=sub: e.memset(YS[:, sub * 1024:(sub + 1) * 1024], 0.0),
                          waits=[("SC", p.counts.get("_ysread", 0)), ("ST", p.counts.get("_ysst", 0))])
            for ex in range(NE):
                for _h3 in range(3):
                    _c0, _c1 = _h3 * 4096, min((_h3 + 1) * 4096, 11 * 1024)
                    lw2 = p.ld(lambda e, ex=ex, _c0=_c0, _c1=_c1: e.dma_start(
                        out=W2E[:, _c0:_c1], in_=w2s[ex, :, _c0:_c1]),
                        waits=[("GD", scr_ready), ("TE", p.counts.get("_w2used", 0))])
                wg = [None, None, None, None]

                def load_w13(dffc, buf, ex=ex):
                    ld = p.ld(lambda e, dffc=dffc, buf=buf, ex=ex: e.dma_start(
                        out=W13C[buf][:, :], in_=w13s[ex, dffc, :, :]),
                        waits=[("GD", scr_ready), ("TE", p.counts.get("_w13used%d" % buf, 0))])
                    return ld

                for _i in range(4):
                    wg[_i] = load_w13(_i, _i)
                for dffc in range(11):
                    buf = dffc % 4
                    for qc in range(2):
                        for c in range(8):
                            p.te(lambda e, c=c, qc=qc, buf=buf: e.matmul(
                                PH1[:, :], W13C[buf][:, c * 128:(c + 1) * 128],
                                X3TV[:, c, qc * 512:(qc + 1) * 512],
                                start=(c == 0), stop=(c == 7)),
                                waits=[("LD", wg[buf]), ("VE", xt_ready),
                                       ("SC", p.counts.get("_ph1free", 0))])
                        for c in range(8):
                            p.te(lambda e, c=c, qc=qc, buf=buf: e.matmul(
                                PH3[:, :], W13C[buf][:, 1024 + c * 128:1024 + (c + 1) * 128],
                                X3TV[:, c, qc * 512:(qc + 1) * 512],
                                start=(c == 0), stop=(c == 7)),
                                waits=[("SC", p.counts.get("_ph3free", 0))],
                                tick=(c == 7))
                        th = p.c("TE")
                        s1 = p.sc(lambda e, ex=ex, dffc=dffc: e.activation(
                            ES[:, :], PH1[:, :], AF.Identity, bias=B1PT[:, ex * 11 + dffc:ex * 11 + dffc + 1]),
                            waits=[("TE", th), ("VE", p.counts.get("_esfreev", 0))])
                        p.counts["_ph1free"] = s1
                        s3 = p.sc(lambda e, ex=ex, dffc=dffc: e.activation(
                            ES2[:, :], PH3[:, :], AF.Identity, bias=B3PT[:, ex * 11 + dffc:ex * 11 + dffc + 1]),
                            waits=[("TE", th), ("VE", p.counts.get("_esfreev", 0))])
                        p.counts["_ph3free"] = s3
                        v1 = p.ve(lambda e: e.tensor_tensor(ES[:, :], ES[:, :], ES2[:, :], ALU.mult),
                                  waits=[("SC", s3), ("SC", s1)])
                        p.counts["_esfreev"] = v1
                        sh = p.sc(lambda e, dffc=dffc, qc=qc: e.activation(
                            HBF[:, dffc * 1024 + qc * 512:dffc * 1024 + (qc + 1) * 512], ES[:, :], AF.Silu),
                            waits=[("VE", v1), ("TE", p.counts.get("_hbfused", 0))])
                    p.counts["_w13used%d" % buf] = p.c("TE")
                    if dffc + 4 < 11:
                        wg[buf] = load_w13(dffc + 4, buf)
                for _i in range(4):
                    p.counts["_w13used%d" % _i] = p.c("TE")
                h_ready = p.c("SC")
                for sub in range(8):
                    pb = sub % 2
                    for dffc in range(11):
                        for nh in range(2):
                            p.te(lambda e, dffc=dffc, nh=nh, sub=sub, pb=pb: e.matmul(
                                PY[pb][:, nh * 512:(nh + 1) * 512],
                                HBF[:, dffc * 1024 + sub * 128:dffc * 1024 + (sub + 1) * 128],
                                W2E[:, dffc * 1024 + nh * 512:dffc * 1024 + (nh + 1) * 512],
                                start=(dffc == 0), stop=False),
                                waits=[("SC", h_ready), ("LD", lw2),
                                       ("SC", p.counts.get("_pyfree%d" % pb, 0)),
                                       ("VE", p.counts.get("_pyfreev%d" % pb, 0))])
                    bpart = 32 * (ex // 3)
                    bcol = (ex % 3) * 1024
                    for nh in range(2):
                        p.te(lambda e, nh=nh, sub=sub, pb=pb, bpart=bpart, bcol=bcol: e.matmul(
                            PY[pb][:, nh * 512:(nh + 1) * 512],
                            ONESB[bpart:bpart + 1, 0:128],
                            B2P[bpart:bpart + 1, bcol + nh * 512:bcol + (nh + 1) * 512],
                            start=False, stop=(nh == 1)),
                            waits=[("VE", ones_ready)], tick=(nh == 1))
                    ty = p.c("TE")
                    if ex < 8:
                        sg = p.sc(lambda e, ex=ex, sub=sub, pb=pb: e.activation(
                            LNS[:, 0:1024], PY[pb][:, :], AF.Identity,
                            scale=RW[:, sub * 8 + ex:sub * 8 + ex + 1]),
                            waits=[("TE", ty), ("VE", p.counts.get("_lnsv", 0)), ("VE", rw_ready)])
                    else:
                        sg = p.sc(lambda e, pb=pb: e.activation(
                            LNS[:, 0:1024], PY[pb][:, :], AF.Identity),
                            waits=[("TE", ty), ("VE", p.counts.get("_lnsv", 0))])
                    p.counts["_pyfree%d" % pb] = sg
                    vy = p.ve(lambda e, sub=sub: e.tensor_tensor(
                        YS[:, sub * 1024:(sub + 1) * 1024], YS[:, sub * 1024:(sub + 1) * 1024],
                        LNS[:, 0:1024], ALU.add),
                        waits=[("SC", sg), ("VE", vz)])
                    p.counts["_lnsv"] = vy
                    p.counts["_pyfreev%d" % pb] = vy
                p.counts["_hbfused"] = p.c("TE")
                p.counts["_w2used"] = p.c("TE")
            for sub in range(8):
                lx = p.ld(lambda e, h0=h0, sub=sub: e.dma_start(
                    out=X3C[:, 0:1024], in_=x3_d[h0 + sub * 128:h0 + sub * 128 + 128, :]),
                    waits=[("VE", p.counts.get("_x3cfree", 0)), ("TE", p.counts.get("_x3cte", 0))])
                ysub = YS[:, sub * 1024:(sub + 1) * 1024]
                vv = p.ve(lambda e, ysub=ysub: e.tensor_tensor(ysub, ysub, X3C[:, 0:1024], ALU.add),
                          waits=[("LD", lx), ("VE", p.c("VE"))])
                p.counts["_x3cfree"] = vv
                p.counts["_x3cve"] = vv
                r2 = emit_ln_tile(p, ysub, ysub, LNS, STAT[:, 16:24], CONST[:, 1:2],
                                  waits=[("VE", vv)])
                p.counts["_lnsv"] = max(p.counts.get("_lnsv", 0), vv)
                p.counts["_ysst"] = p.stv(
                    lambda e, h0=h0, sub=sub, ysub=ysub: e.dma_start(
                        out=out_d[h0 + sub * 128:h0 + sub * 128 + 128, :], in_=ysub),
                    waits=[("SC", r2.get("SC", 0)), ("VE", r2.get("VE", 0))])
            p.counts["_ysread"] = p.c("SC")
        p.emit("sync", lambda e: e.nop(), waits=[("ST", p.c("ST"))])
        p.fence()
        with nc.Block() as block:
            p.flush(block)


def build_l1_moe():
    nc = bass.Bass("TRN2", target_bir_lowering=False)
    x3_d = nc.dram_tensor("x3", [N_TOK, D], F32, kind="ExternalInput").ap()
    w1_d = nc.dram_tensor("fw1", [NE, D, DFF], F32, kind="ExternalInput").ap()
    w3_d = nc.dram_tensor("fw3", [NE, D, DFF], F32, kind="ExternalInput").ap()
    w2_d = nc.dram_tensor("fw2", [NE, DFF, D], F32, kind="ExternalInput").ap()
    b1_d = nc.dram_tensor("fb1pt", [128, NE * 11], F32, kind="ExternalInput").ap()
    b3_d = nc.dram_tensor("fb3pt", [128, NE * 11], F32, kind="ExternalInput").ap()
    b2_d = nc.dram_tensor("fb2p", [128, 3 * D], F32, kind="ExternalInput").ap()
    gw_d = nc.dram_tensor("fgwt", [D, 8], F32, kind="ExternalInput").ap()
    gb_d = nc.dram_tensor("fgbrow", [1, 8], F32, kind="ExternalInput").ap()
    ident_d = nc.dram_tensor("ident", [128, 128], F32, kind="ExternalInput").ap()
    ones_d = nc.dram_tensor("ones_row", [1, 128], F32, kind="ExternalInput").ap()
    out_d = nc.dram_tensor("out", [N_TOK, D], F32, kind="ExternalOutput").ap()
    w13s = nc.dram_tensor("w13scr", [NE, 11, 128, 2048], BF16)
    w2s = nc.dram_tensor("w2scr", [NE, 128, 11 * D], BF16)
    io = dict(x3_d=x3_d, w1_d=w1_d, w3_d=w3_d, w2_d=w2_d, b1_d=b1_d, b3_d=b3_d, b2_d=b2_d, gw_d=gw_d, gb_d=gb_d, ident_d=ident_d, ones_d=ones_d, out_d=out_d, w13s=w13s, w2s=w2s)
    with ExitStack() as _st:
        p = Prog(nc, _st)
        emit_l1_moe(nc, p, io)
    return nc


def run_l1_moe(x3_full, gate_w, gate_b, w1, b1, w3, b3, w2, b2, sw1, sb1, sw3, sb3, sw2, sb2):
    consts = make_consts()
    nc = build_l1_moe()
    NE = 9
    w1a = np.concatenate([w1, sw1[None]], 0)
    w3a = np.concatenate([w3, sw3[None]], 0)
    w2a = np.concatenate([w2, sw2[None]], 0)
    b1a = np.concatenate([b1, sb1[None]], 0)
    b3a = np.concatenate([b3, sb3[None]], 0)
    b2a = np.concatenate([b2, sb2[None]], 0)

    def pt11(b):
        out = np.zeros((128, NE * 11), np.float32)
        for e in range(NE):
            out[:, e * 11:(e + 1) * 11] = b[e].reshape(11, 128).T
        return np.ascontiguousarray(out)

    b2p = np.zeros((128, 3 * 1024), np.float32)
    for e in range(NE):
        b2p[32 * (e // 3), (e % 3) * 1024:(e % 3 + 1) * 1024] = b2a[e]
    in_maps = []
    for c in range(8):
        in_maps.append({
            "x3": _f(x3_full[c]), "fw1": _f(w1a), "fw3": _f(w3a), "fw2": _f(w2a),
            "fb1pt": pt11(b1a), "fb3pt": pt11(b3a), "fb2p": _f(b2p),
            "fgwt": _f(np.asarray(gate_w).T), "fgbrow": _f(gate_b).reshape(1, 8),
            "ident": consts["ident"], "ones_row": consts["ones_row"],
        })
    res = run_bass_kernel_spmd(nc, in_maps, core_ids=list(range(8)))
    return np.stack([np.asarray(r["out"]) for r in res.results])


def emit_l1_attn(nc, p, io):
    x2_d = io["x2_d"]
    wq_d = io["wq_d"]
    wk_d = io["wk_d"]
    wv_d = io["wv_d"]
    wo_d = io["wo_d"]
    bq_d = io["bq_d"]
    bk_d = io["bk_d"]
    bv_d = io["bv_d"]
    bo_d = io["bo_d"]
    cos_d = io["cos_d"]
    sin_d = io["sin_d"]
    rot_d = io["rot_d"]
    ind_d = io["ind_d"]
    ident_d = io["ident_d"]
    ones_d = io["ones_d"]
    x3_d = io["x3_d"]
    qsp = io["qsp"]
    wscr = io["wscr"]


    stack = ExitStack()
    with stack:
        sb = lambda name, shape, dt=F32: stack.enter_context(nc.sbuf_tensor(name, shape, dt, allow_name_mangling=True))
        ps = lambda name, shape: stack.enter_context(nc.psum_tensor(name, shape, F32, allow_name_mangling=True))
        KT = sb("KT", [128, 8 * N_TOK], BF16)
        VTM = sb("VTM", [128, 32 * 16 * 65], BF16)
        WB = sb("WB", [128, 8 * 1024], BF16)
        STG = sb("STG", [128, 1024], F32)
        BSC = sb("BSC", [128, 1024], BF16)
        X2TILE = sb("X2TILE", [128, 1024], F32)
        BB = sb("BB", [128, 1024], F32)
        RQK = sb("RQK", [2, 32], F32)
        NEGB = sb("NEGB", [128, 1], F32)
        LNS = sb("LNS", [128, 1024], F32)
        STAT = sb("STAT", [128, 32], F32)
        CONST = sb("CONST", [128, 4], F32)
        IDENT = sb("IDENT", [128, 128], F32)
        ONESR = sb("ONESR", [1, 128], F32)
        LNS2 = STG
        PTR = ps("PTR", [128, 512])
        PQK = ps("PQK", [128, 512])
        PV = ps("PV", [128, 1024])
        SPS1 = ps("SPS1", [128, 1024])
        OPS = [ps("OPS0", [65, 512]), ps("OPS1", [65, 512])]

        SCRH = [0]
        stackA = ExitStack()
        with stackA:
            sbA = lambda name, shape, dt=F32: stackA.enter_context(nc.sbuf_tensor(name, shape, dt, allow_name_mangling=True))
            X2T = sbA("X2T", [128, 8 * 512], BF16)
            QF = sbA("QF", [128, 512], F32)
            TMPA = sbA("TMPA", [128, 512], F32)
            TMPB = sbA("TMPB", [128, 512], F32)
            SQF = sbA("SQF", [128, 512], F32)
            QSTG = sbA("QSTG", [128, 512], BF16)
            COSC = sbA("COSC", [128, 512], F32)
            SINC = sbA("SINC", [128, 512], F32)
            ROTM = sbA("ROTM", [128, 128], F32)
            INDM = sbA("INDM", [128, 2], F32)
            BQPT = sbA("BQPT", [128, 8], F32)
            BKPT = sbA("BKPT", [128, 8], F32)
            PTRKEY = "_ptfree%d" % id(PTR)
            p.ve(lambda e: e.memset(CONST[:, 1:2], 1e-5))
            vtmv = VTM[:, :].rearrange("p (g c) -> p g c", c=65)
            p.ve(lambda e: e.memset(vtmv[:, :, 64:65], 1.0))
            p.ve(lambda e: e.memset(RQK[0:2, 0:32], 0.0))
            for d, dst in [(ident_d, IDENT), (ones_d, ONESR), (rot_d, ROTM), (ind_d, INDM),
                           (bq_d, BQPT), (bk_d, BKPT)]:
                p.ld(lambda e, d=d, dst=dst: e.dma_start(out=dst[:, :], in_=d))
            lbv = p.ld(lambda e: e.dma_start(out=LNS[0:1, 0:1024], in_=bv_d))
            bcast_row(p, PV, BB, LNS[0:1, 0:1024], ONESR[0:1, :], 1024, waits=[("LD", lbv)])
            bv_done = p.c("VE")
            p.counts["_pvfree"] = bv_done

            STGB = sbA("STGB", [128, 1024], F32)
            BSCB = sbA("BSCB", [128, 1024], BF16)
            jobs = [(wd[c * 128:(c + 1) * 128, :], wscr[wi, c, :, :], 1024)
                    for wi, wd in enumerate([wq_d, wk_d, wv_d, wo_d]) for c in range(8)]
            SCRH[0] = emit_wconv(nc, p, [STG, STGB], [BSC, BSCB], jobs, "a")
            if "mw1_d" in io:
                STGC = sbA("STGC", [128, 1408], F32)
                BSCC = sbA("BSCC", [128, 1408], BF16)
                mjobs = []
                for e_ in range(9):
                    for c in range(8):
                        mjobs.append((io["mw1_d"][e_, c * 128:(c + 1) * 128, :],
                                      io["mw13s"][e_, :, :, c * 128:(c + 1) * 128].rearrange("k p f -> p k f"), 1408))
                        mjobs.append((io["mw3_d"][e_, c * 128:(c + 1) * 128, :],
                                      io["mw13s"][e_, :, :, 1024 + c * 128:1024 + (c + 1) * 128].rearrange("k p f -> p k f"), 1408))
                    for k_ in range(11):
                        mjobs.append((io["mw2_d"][e_, k_ * 128:(k_ + 1) * 128, :],
                                      io["mw2s"][e_, :, k_ * 1024:(k_ + 1) * 1024], 1024))
                emit_wconv(nc, p, [STGC], [BSCC], mjobs, "m", via_gp=True)

            def load_wb(wi):
                return p.ld(lambda e, wi=wi: e.dma_start(
                    out=WB[:, :].rearrange("p (c n) -> p c n", n=1024),
                    in_=wscr[wi].rearrange("c p n -> p c n")),
                    waits=[("GD", SCRH[0]), ("TE", p.counts.get("_wbused", 0))])

            X2TV = X2T[:, :].rearrange("p (c t) -> p c t", t=512)
            for ch in range(8):
                t0 = ch * 512
                lc = p.ld(lambda e, t0=t0: e.dma_start(out=COSC[:, :], in_=cos_d[:, t0:t0 + 512]),
                          waits=[("VE", p.counts.get("_coscfree", 0))])
                lc = p.ld(lambda e, t0=t0: e.dma_start(out=SINC[:, :], in_=sin_d[:, t0:t0 + 512]))
                for sub in range(4):
                    lx = p.ld(lambda e, t0=t0, sub=sub: e.dma_start(
                        out=X2TILE[:, :], in_=x2_d[t0 + sub * 128:t0 + sub * 128 + 128, :]),
                        waits=[("TE", p.counts.get("_x2tfree", 0))])
                    emit_transpose_to_fm(
                        p, lambda hh, sub=sub: X2TV[:, :, sub * 128:sub * 128 + 128][:, hh * 4:hh * 4 + 4, :],
                        X2TILE[:, :], PTR, IDENT[:, :], waits=[("LD", lx)])
                    p.counts["_x2tfree"] = p.c("TE")
                xt_ready = p.c("VE")
                for which, bpt in [(0, BQPT), (1, BKPT)]:
                    wld = load_wb(which)
                    for j in range(8):
                        for c in range(8):
                            p.te(lambda e, j=j, c=c: e.matmul(
                                PQK[:, :], WB[:, c * 1024 + j * 128:c * 1024 + (j + 1) * 128],
                                X2TV[:, c, :], start=(c == 0), stop=(c == 7)),
                                waits=[("LD", wld), ("VE", xt_ready),
                                       ("SC", p.counts.get("_pqkfree", 0))],
                                tick=(c == 7))
                        tq = p.c("TE")
                        sev = p.sc(lambda e, j=j, bpt=bpt: e.activation(
                            QF[:, :], PQK[:, :], AF.Identity, bias=bpt[:, j:j + 1]),
                            waits=[("TE", tq), ("VE", p.counts.get("_qffree", 0))])
                        p.counts["_pqkfree"] = sev
                        ssq = p.sc(lambda e: e.activation(SQF[:, :], QF[:, :], AF.Square),
                                   waits=[("SC", sev), ("TE", p.counts.get("_sqffree", 0))])
                        tn = p.te(lambda e: e.matmul(PTR[0:2, 0:512], INDM[:, :], SQF[:, :],
                                                     start=True, stop=True),
                                  waits=[("SC", ssq), ("VE", p.counts.get(PTRKEY, 0))], tick=True)
                        p.counts["_sqffree"] = tn
                        vn = p.ve(lambda e: e.tensor_reduce(STAT[0:2, 0:1], PTR[0:2, 0:512],
                                                            axis=AX.X, op=ALU.max),
                                  waits=[("TE", tn)])
                        p.counts[PTRKEY] = vn
                        col = 16 * which + j
                        vn = p.ve(lambda e, col=col: e.tensor_tensor(
                            RQK[0:2, col:col + 1], RQK[0:2, col:col + 1], STAT[0:2, 0:1], ALU.max),
                            waits=[("VE", vn)])
                        tr = p.te(lambda e: e.matmul(PTR[:, :], ROTM[:, :], QF[:, :],
                                                     start=True, stop=True),
                                  waits=[("SC", sev), ("VE", p.counts.get(PTRKEY, 0))], tick=True)
                        va = p.ve(lambda e: e.tensor_tensor(TMPA[:, :], QF[:, :], COSC[:, :], ALU.mult),
                                  waits=[("SC", sev), ("LD", lc), ("VE", p.counts.get("_tmpafree", 0))])
                        p.counts["_qffree"] = va
                        vb = p.ve(lambda e: e.tensor_tensor(TMPB[:, :], PTR[:, :], SINC[:, :], ALU.mult),
                                  waits=[("TE", tr), ("LD", lc), ("VE", va)])
                        p.counts[PTRKEY] = vb
                        if which == 1:
                            vc = p.ve(lambda e, j=j, t0=t0: e.tensor_tensor(
                                KT[:, j * N_TOK + t0:j * N_TOK + t0 + 512], TMPA[:, :], TMPB[:, :], ALU.add),
                                waits=[("VE", vb)])
                        else:
                            vc = p.ve(lambda e: e.tensor_tensor(QSTG[:, :], TMPA[:, :], TMPB[:, :], ALU.add),
                                      waits=[("VE", vb), ("ST", p.counts.get("_qstgfree", 0))])
                            p.counts["_qstgfree"] = p.st(
                                lambda e, j=j, t0=t0: e.dma_start(
                                    out=qsp[j, :, t0:t0 + 512], in_=QSTG[:, :]),
                                waits=[("VE", vc)])
                        p.counts["_tmpafree"] = vc
                    p.counts["_wbused"] = p.c("TE")
                wld = load_wb(2)
                for sub in range(4):
                    for c in range(8):
                        for nh in range(2):
                            p.te(lambda e, sub=sub, c=c, nh=nh: e.matmul(
                                PV[:, nh * 512:(nh + 1) * 512],
                                X2TV[:, c, sub * 128:sub * 128 + 128],
                                WB[:, c * 1024 + nh * 512:c * 1024 + (nh + 1) * 512],
                                start=(c == 0), stop=(c == 7)),
                                waits=[("LD", wld), ("VE", xt_ready),
                                       ("VE", p.counts.get("_pvfree", 0))],
                                tick=(c == 7 and nh == 1))
                    tv = p.c("TE")
                    tile = ch * 4 + sub
                    vv = p.ve(lambda e, tile=tile: e.tensor_tensor(
                        vtmv[:, tile * 16:(tile + 1) * 16, 0:64],
                        PV[:, :].rearrange("p (h c) -> p h c", c=64),
                        BB[:, :].rearrange("p (h c) -> p h c", c=64), ALU.add),
                        waits=[("TE", tv), ("VE", bv_done)])
                    p.counts["_pvfree"] = vv
                p.counts["_wbused"] = p.c("TE")
            vq = p.ve(lambda e: e.tensor_reduce(RQK[0:2, 16:17], RQK[0:2, 0:8], axis=AX.X, op=ALU.max),
                      waits=[("VE", p.c("VE"))])
            vq = p.ve(lambda e: e.tensor_reduce(RQK[0:2, 17:18], RQK[0:2, 8:16], axis=AX.X, op=ALU.max),
                      waits=[("VE", vq)])
            tb = p.te(lambda e: e.transpose(PTR[0:2, 0:2], RQK[0:2, 16:18], IDENT[0:2, 0:2]),
                      waits=[("VE", vq), ("VE", p.counts.get(PTRKEY, 0))], tick=True)
            vq = p.ve(lambda e: e.tensor_copy(STAT[0:2, 0:2], PTR[0:2, 0:2]), waits=[("TE", tb)])
            p.counts[PTRKEY] = vq
            vq = p.ve(lambda e: e.tensor_reduce(STAT[0:2, 2:3], STAT[0:2, 0:2], axis=AX.X, op=ALU.max),
                      waits=[("VE", vq)])
            sq = p.sc(lambda e: e.activation(STAT[0:2, 3:4], STAT[0:2, 2:3], AF.Sqrt),
                      waits=[("VE", vq)])
            tb = p.te(lambda e: e.transpose(PTR[0:1, 0:2], STAT[0:2, 3:4], IDENT[0:2, 0:2]),
                      waits=[("SC", sq), ("VE", p.counts.get(PTRKEY, 0))], tick=True)
            vq = p.ve(lambda e: e.tensor_copy(STAT[0:1, 4:6], PTR[0:1, 0:2]), waits=[("TE", tb)])
            p.counts[PTRKEY] = vq
            vq = p.ve(lambda e: e.tensor_tensor(STAT[0:1, 6:7], STAT[0:1, 4:5], STAT[0:1, 5:6], ALU.mult),
                      waits=[("VE", vq)])
            vq = p.ve(lambda e: e.tensor_scalar(STAT[0:1, 7:8], STAT[0:1, 6:7], -0.125, None, ALU.mult),
                      waits=[("VE", vq)])
            tb = p.te(lambda e: e.matmul(PQK[:, 0:1], ONESR[0:1, :], STAT[0:1, 7:8],
                                         start=True, stop=True),
                      waits=[("VE", vq), ("SC", p.counts.get("_pqkfree", 0))], tick=True)
            vq = p.ve(lambda e: e.tensor_copy(NEGB[:, :], PQK[:, 0:1]), waits=[("TE", tb)])
            lbo = p.ld(lambda e: e.dma_start(out=LNS[0:1, 0:1024], in_=bo_d),
                       waits=[("VE", p.c("VE"))])
            for c0 in range(0, 1024, 512):
                tb = p.te(lambda e, c0=c0: e.matmul(PQK[:, 0:512], ONESR[0:1, :],
                                                    LNS[0:1, c0:c0 + 512], start=True, stop=True),
                          waits=[("LD", lbo), ("VE", p.counts.get("_bbk", 0))], tick=True)
                vq = p.ve(lambda e, c0=c0: e.tensor_copy(BB[:, c0:c0 + 512], PQK[:, 0:512]),
                          waits=[("TE", tb)])
                p.counts["_bbk"] = vq
            bo_done = p.c("VE")
            p.fence()
            with nc.Block() as blockA:
                p.flush(blockA)
        stackB = ExitStack()
        with stackB:
            sbB = lambda name, shape, dt=F32: stackB.enter_context(nc.sbuf_tensor(name, shape, dt, allow_name_mangling=True))
            QBD = [sbB("QBD0", [128, 512], BF16), sbB("QBD1", [128, 512], BF16)]
            PT16 = [sbB("PT16_0", [128, 1024], BF16), sbB("PT16_1", [128, 1024], BF16)]
            OA = sbB("OA", [64, 512], F32)
            RL = sbB("RL", [1, 512], F32)
            ATT = sbB("ATT", [128, 8 * 512], BF16)
            wld_o = load_wb(3)

            zq = p.ve(lambda e: e.memset(QBD[0][64:128, :], 0.0))
            zq = p.ve(lambda e: e.memset(QBD[1][0:64, :], 0.0))

            def qbd_load(h, q0, b):
                j = h // 2
                par = (h % 2) * 64
                l1 = p.ld(lambda e, j=j, par=par, q0=q0, b=b: e.dma_start(
                    out=QBD[b][par:par + 64, :], in_=qsp[j, par:par + 64, q0:q0 + 512]),
                    waits=[("TE", p.counts.get("_qbdfree%d" % b, 0)), ("VE", zq)])
                return l1

            qbl = {(0, 0): qbd_load(0, 0, 0)}
            sps = [PV, SPS1]
            for qc in range(8):
                q0 = qc * 512

                def emit_s_exp(h, kcp):
                    b = h % 2
                    par = (h % 2) * 64
                    sbk = kcp % 2
                    for kk in range(2):
                        kc = kcp * 2 + kk
                        p.te(lambda e, h=h, kc=kc, kk=kk, sbk=sbk, b=b: e.matmul(
                            sps[sbk][:, kk * 512:(kk + 1) * 512],
                            KT[:, (h // 2) * N_TOK + kc * 128:(h // 2) * N_TOK + (kc + 1) * 128],
                            QBD[b][:, :], start=True, stop=True),
                            waits=[("LD", qbl[(qc, h)]),
                                   ("SC", p.counts.get("_spsfree%d" % sbk, 0))],
                            tick=(kk == 1))
                    ts_ = p.c("TE")
                    se = p.sc(lambda e, sbk=sbk: e.activation(
                        PT16[sbk][:, :], sps[sbk][:, :], AF.Exp, bias=NEGB[:, 0:1], scale=0.125),
                        waits=[("TE", ts_), ("TE", p.counts.get("_pt16free%d" % sbk, 0))])
                    p.counts["_spsfree%d" % sbk] = se
                    return se

                def emit_av(h, kcp, se):
                    sbk = kcp % 2
                    ob = h % 2
                    for kk in range(2):
                        kc = kcp * 2 + kk
                        p.te(lambda e, h=h, kc=kc, kk=kk, sbk=sbk, ob=ob: e.matmul(
                            OPS[ob][:, :],
                            VTM[:, (kc * 16 + h) * 65:(kc * 16 + h) * 65 + 65],
                            PT16[sbk][:, kk * 512:(kk + 1) * 512],
                            start=(kc == 0), stop=(kc == 31)),
                            waits=[("SC", se), ("SC", p.counts.get("_opsfree%d" % ob, 0)),
                                   ("VE", p.counts.get("_opsfreev%d" % ob, 0))],
                            tick=(kk == 1))
                    p.counts["_pt16free%d" % sbk] = p.c("TE")

                def emit_norm(h, to_):
                    ob = h % 2
                    par = (h % 2) * 64
                    vr = p.ve(lambda e, ob=ob: e.reciprocal(RL[0:1, :], OPS[ob][64:65, :]),
                              waits=[("TE", to_), ("TE", p.counts.get("_rlfree", 0))])
                    so = p.sc(lambda e, ob=ob: e.activation(OA[0:64, :], OPS[ob][0:64, :], AF.Identity),
                              waits=[("TE", to_), ("VE", p.counts.get("_oafree", 0))])
                    p.counts["_opsfree%d" % ob] = so
                    tb2 = p.te(lambda e: e.matmul(PTR[0:64, :], ONESR[0:1, 0:64], RL[0:1, :],
                                                  start=True, stop=True),
                               waits=[("VE", vr), ("VE", p.counts.get(PTRKEY, 0))], tick=True)
                    p.counts["_rlfree"] = tb2
                    va = p.ve(lambda e, h=h, par=par: e.tensor_tensor(
                        ATT[par:par + 64, (h // 2) * 512:(h // 2) * 512 + 512],
                        OA[0:64, :], PTR[0:64, :], ALU.mult),
                        waits=[("TE", tb2), ("SC", so), ("TE", p.counts.get("_attfree", 0))])
                    p.counts[PTRKEY] = va
                    p.counts["_oafree"] = va
                    p.counts["_opsfreev%d" % ob] = va

                pend_norm = None
                for h in range(16):
                    b = h % 2
                    pend_av = None
                    for kcp in range(16):
                        se = emit_s_exp(h, kcp)
                        if pend_av is not None:
                            emit_av(h, pend_av[0], pend_av[1])
                        pend_av = (kcp, se)
                        if kcp == 0 and pend_norm is not None:
                            emit_norm(*pend_norm)
                            pend_norm = None
                    emit_av(h, pend_av[0], pend_av[1])
                    to_ = p.c("TE")
                    p.counts["_qbdfree%d" % b] = to_
                    if h + 1 < 16:
                        qbl[(qc, h + 1)] = qbd_load(h + 1, q0, (h + 1) % 2)
                    elif qc + 1 < 8:
                        qbl[(qc + 1, 0)] = qbd_load(0, (qc + 1) * 512, 0)
                    pend_norm = (h, to_)
                emit_norm(*pend_norm)
                att_ready = p.c("VE")
                for sub in range(4):
                    pw = [SPS1, PV][sub % 2]
                    for jc in range(8):
                        for nh in range(2):
                            p.te(lambda e, jc=jc, nh=nh, sub=sub, pw=pw: e.matmul(
                                pw[:, nh * 512:(nh + 1) * 512],
                                ATT[:, jc * 512 + sub * 128:jc * 512 + (sub + 1) * 128],
                                WB[:, jc * 1024 + nh * 512:jc * 1024 + (nh + 1) * 512],
                                start=(jc == 0), stop=(jc == 7)),
                                waits=[("VE", att_ready), ("LD", wld_o),
                                       ("SC", p.counts.get("_pwfree%d" % (sub % 2), 0)),
                                       ("SC", p.counts.get("_spsfree%d" % (1 - sub % 2), 0))],
                                tick=(jc == 7 and nh == 1))
                    tw = p.c("TE")
                    lx = p.ld(lambda e, q0=q0, sub=sub: e.dma_start(
                        out=X2TILE[:, :], in_=x2_d[q0 + sub * 128:q0 + sub * 128 + 128, :]),
                        waits=[("VE", p.counts.get("_x2resfree", 0))])
                    sy = p.sc(lambda e, pw=pw: e.activation(LNS[:, :], pw[:, :], AF.Identity),
                              waits=[("TE", tw), ("VE", p.counts.get("_lnsv", 0)),
                                     ("ST", p.counts.get("_lnst", 0))])
                    p.counts["_pwfree%d" % (sub % 2)] = sy
                    vy = p.ve(lambda e: e.tensor_tensor(LNS[:, :], LNS[:, :], BB[:, :], ALU.add),
                              waits=[("SC", sy), ("VE", bo_done)])
                    vy = p.ve(lambda e: e.tensor_tensor(LNS[:, :], LNS[:, :], X2TILE[:, :], ALU.add),
                              waits=[("VE", vy), ("LD", lx)])
                    p.counts["_x2resfree"] = vy
                    r2 = emit_ln_tile(p, LNS, LNS, LNS2, STAT[:, 16:24], CONST[:, 1:2],
                                      waits=[("VE", vy)])
                    p.counts["_lnsv"] = max(p.counts.get("_lnsv", 0), vy)
                    p.counts["_lnst"] = p.st(
                        lambda e, q0=q0, sub=sub: e.dma_start(
                            out=x3_d[q0 + sub * 128:q0 + sub * 128 + 128, :], in_=LNS[:, :]),
                        waits=[("SC", r2.get("SC", 0)), ("VE", r2.get("VE", 0))])
                p.counts["_attfree"] = p.c("TE")
                p.counts["_spsfree0"] = max(p.counts.get("_spsfree0", 0), p.counts.get("_pwfree1", 0))
                p.counts["_spsfree1"] = max(p.counts.get("_spsfree1", 0), p.counts.get("_pwfree0", 0))
            p.emit("sync", lambda e: e.nop(), waits=[("ST", p.c("ST"))])
            p.fence()
            with nc.Block() as blockB:
                p.flush(blockB)


def build_l1_attn():
    nc = bass.Bass("TRN2", target_bir_lowering=False)
    x2_d = nc.dram_tensor("x2", [N_TOK, D], F32, kind="ExternalInput").ap()
    wq_d = nc.dram_tensor("awqT", [D, D], F32, kind="ExternalInput").ap()
    wk_d = nc.dram_tensor("awkT", [D, D], F32, kind="ExternalInput").ap()
    wv_d = nc.dram_tensor("awvT", [D, D], F32, kind="ExternalInput").ap()
    wo_d = nc.dram_tensor("awoT", [D, D], F32, kind="ExternalInput").ap()
    bq_d = nc.dram_tensor("abqpt", [128, 8], F32, kind="ExternalInput").ap()
    bk_d = nc.dram_tensor("abkpt", [128, 8], F32, kind="ExternalInput").ap()
    bv_d = nc.dram_tensor("abvrow", [1, D], F32, kind="ExternalInput").ap()
    bo_d = nc.dram_tensor("aborow", [1, D], F32, kind="ExternalInput").ap()
    cos_d = nc.dram_tensor("cosFt", [128, N_TOK], F32, kind="ExternalInput").ap()
    sin_d = nc.dram_tensor("sinFt", [128, N_TOK], F32, kind="ExternalInput").ap()
    rot_d = nc.dram_tensor("rotM", [128, 128], F32, kind="ExternalInput").ap()
    ind_d = nc.dram_tensor("indM", [128, 2], F32, kind="ExternalInput").ap()
    ident_d = nc.dram_tensor("ident", [128, 128], F32, kind="ExternalInput").ap()
    ones_d = nc.dram_tensor("ones_row", [1, 128], F32, kind="ExternalInput").ap()
    x3_d = nc.dram_tensor("x3", [N_TOK, D], F32, kind="ExternalOutput").ap()
    qsp = nc.dram_tensor("qspill", [8, 128, N_TOK], BF16)
    wscr = nc.dram_tensor("wscr", [4, 8, 128, D], BF16)
    io = dict(x2_d=x2_d, wq_d=wq_d, wk_d=wk_d, wv_d=wv_d, wo_d=wo_d, bq_d=bq_d, bk_d=bk_d, bv_d=bv_d, bo_d=bo_d, cos_d=cos_d, sin_d=sin_d, rot_d=rot_d, ind_d=ind_d, ident_d=ident_d, ones_d=ones_d, x3_d=x3_d, qsp=qsp, wscr=wscr)
    with ExitStack() as _st:
        p = Prog(nc, _st)
        emit_l1_attn(nc, p, io)
    return nc


def run_l1_attn(x2_full, wq, bq, wk, bk, wv, bv, wo, bo):
    consts = make_consts()
    nc = build_l1_attn()
    rotm = np.zeros((128, 128), np.float32)
    for base in (0, 64):
        for dd in range(32):
            rotm[base + 32 + dd, base + dd] = -1.0
            rotm[base + dd, base + 32 + dd] = 1.0
    indm = np.zeros((128, 2), np.float32)
    indm[0:64, 0] = 1.0
    indm[64:128, 1] = 1.0
    in_maps = []
    for c in range(8):
        in_maps.append({
            "x2": _f(x2_full[c]),
            "awqT": _f(np.asarray(wq).T), "awkT": _f(np.asarray(wk).T),
            "awvT": _f(np.asarray(wv).T), "awoT": _f(np.asarray(wo).T),
            "abqpt": per_part(bq), "abkpt": per_part(bk),
            "abvrow": _f(bv).reshape(1, D), "aborow": _f(bo).reshape(1, D),
            "cosFt": consts["cosF"], "sinFt": consts["sinF"],
            "rotM": rotm, "indM": indm,
            "ident": consts["ident"], "ones_row": consts["ones_row"],
        })
    res = run_bass_kernel_spmd(nc, in_maps, core_ids=list(range(8)))
    return np.stack([np.asarray(r["x3"]) for r in res.results])


def emit_l0_attn(nc, p, io):
    x_d = io["x_d"]
    wq_d = io["wq_d"]
    wk_d = io["wk_d"]
    wv_d = io["wv_d"]
    wo_d = io["wo_d"]
    prj_d = io["prj_d"]
    g1_d = io["g1_d"]
    b1_d = io["b1_d"]
    ident_d = io["ident_d"]
    identb_d = io["identb_d"]
    ones_d = io["ones_d"]
    out1_d = io["out1_d"]
    qsp = io["qsp"]
    wscr = io["wscr"]
    ccin = io["ccin"]
    ccout = io["ccout"]

    DN = 64.0 ** -0.25
    RATIO = 0.125
    EPSK = 1e-6
    REPS = RATIO * EPSK
    REPSN = RATIO * EPSK * N_TOK


    stack = ExitStack()
    with stack:
        sb = lambda name, shape, dt=F32: stack.enter_context(nc.sbuf_tensor(name, shape, dt, allow_name_mangling=True))
        ps = lambda name, shape, dt=F32: stack.enter_context(nc.psum_tensor(name, shape, dt, allow_name_mangling=True))
        KT = sb("KT", [128, 8 * N_TOK], BF16)
        VTM = sb("VTM", [128, 32 * 16 * 65], BF16)
        WB = sb("WB", [128, 8 * 1024], BF16)
        STG = sb("STG", [128, 1024], F32)
        BSC = sb("BSC", [128, 1024], BF16)
        XTILE = sb("XTILE", [128, 1024], F32)
        X0T = sb("X0T", [128, 8 * 512], BF16)
        QKSTG = [sb("QKSTG0", [128, 512], BF16), sb("QKSTG1", [128, 512], BF16)]
        RHS1 = sb("RHS1", [128, 65], BF16)
        RHS2 = sb("RHS2", [128, 65], BF16)
        PRJ = sb("PRJ", [64, 64], F32)
        SQW = [sb("SQW0", [64, 128], BF16), sb("SQW1", [64, 128], BF16)]
        PHK = [sb("PHK0", [128, 65], BF16), sb("PHK1", [128, 65], BF16)]
        PHQF = [sb("PHQF0", [128, 64], F32), sb("PHQF1", [128, 64], F32)]
        PHQB = [sb("PHQB0", [128, 64], BF16), sb("PHQB1", [128, 64], BF16)]
        QFT = [sb("QFT0", [64, 128], BF16), sb("QFT1", [64, 128], BF16)]
        QTL = [sb("QTL0", [128, 8 * 128], BF16), sb("QTL1", [128, 8 * 128], BF16)]
        KVS = sb("KVS", [65, 16 * 65], F32)
        KVM = sb("KVM", [64, 16 * 65], BF16)
        TMPK = sb("TMPK", [64, 64], F32)
        TMPK2 = sb("TMPK2", [64, 64], F32)
        RMX = sb("RMX", [128, 4], F32)
        MXS = sb("MXS", [1, 8], F32)
        MXLB = sb("MXLB", [128, 1], F32)
        SPB = sb("SPB", [128, 1], F32)
        BIA = sb("BIA", [128, 2], F32)
        RD = sb("RD", [128, 2], F32)
        ATT = sb("ATT", [128, 16 * 64], BF16)
        ATF = sb("ATF", [128, 8 * 128], BF16)
        CCS = sb("CCS", [2, 1024], F32)
        G1B = sb("G1B", [128, 1024], F32)
        B1B = sb("B1B", [128, 1024], F32)
        ONESB = sb("ONESB", [128, 128], F32)
        LNS = sb("LNS", [128, 1024], F32)
        STAT = sb("STAT", [128, 32], F32)
        CONST = sb("CONST", [128, 4], F32)
        IDENT = sb("IDENT", [128, 128], F32)
        IDB = sb("IDB", [128, 128], BF16)
        ONESR = sb("ONESR", [1, 128], F32)
        PTR = ps("PTR", [128, 512])
        PQK = ps("PQK", [128, 512])
        PV = ps("PV", [128, 1024])
        DD = [ps("DD0", [128, 512]), ps("DD1", [128, 512])]
        KVP = ps("KVP", [65, 512])
        PTB = ps("PTB", [128, 128], BF16)

        PTRKEY = "_ptfree%d" % id(PTR)
        p.ve(lambda e: e.memset(CONST[:, 0:1], 1e-6))
        vtmv = VTM[:, :].rearrange("p (g c) -> p g c", c=65)
        p.ve(lambda e: e.memset(vtmv[:, :, 64:65], 1.0))
        p.ve(lambda e: e.memset(PHK[0][:, 64:65], 1.0))
        p.ve(lambda e: e.memset(PHK[1][:, 64:65], 1.0))
        p.ve(lambda e: e.memset(RMX[:, 0:1], -1e30))
        p.ve(lambda e: e.memset(CCS[:, :], -1e30))
        p.ve(lambda e: e.memset(ONESB[:, :], 1.0))
        p.ve(lambda e: e.memset(RHS1[:, :], 0.0))
        p.ve(lambda e: e.memset(RHS2[:, 0:64], 0.0))
        p.ve(lambda e: e.memset(RHS2[:, 64:65], -0.0625))
        mset = p.c("VE")
        for d, dst in [(ident_d, IDENT), (identb_d, IDB), (ones_d, ONESR)]:
            p.ld(lambda e, d=d, dst=dst: e.dma_start(out=dst[:, :], in_=d))
        lpr = p.ld(lambda e: e.dma_start(out=PRJ[:, :], in_=prj_d))
        vq = p.ve(lambda e: e.tensor_scalar(PRJ[:, :], PRJ[:, :], DN, None, ALU.mult),
                  waits=[("LD", lpr), ("VE", mset)])
        vq = p.ve(lambda e: e.tensor_copy(RHS1[0:64, 0:64], PRJ[:, :]), waits=[("VE", vq)])
        vq = p.ve(lambda e: e.tensor_copy(RHS1[64:128, 0:64], PRJ[:, :]), waits=[("VE", vq)])
        rhs_ready = vq
        lg = p.ld(lambda e: e.dma_start(out=LNS[0:1, 0:1024], in_=g1_d))
        bcast_row(p, PV, G1B, LNS[0:1, 0:1024], ONESR[0:1, :], 1024, waits=[("LD", lg)])
        lg = p.ld(lambda e: e.dma_start(out=LNS[0:1, 0:1024], in_=b1_d),
                  waits=[("TE", p.c("TE"))])
        bcast_row(p, PV, B1B, LNS[0:1, 0:1024], ONESR[0:1, :], 1024, waits=[("LD", lg)])
        p.counts["_pvfree"] = p.c("VE")

        STGB = sb("STGB", [128, 1024], F32)
        BSCB = sb("BSCB", [128, 1024], BF16)
        jobs = [(wd[c * 128:(c + 1) * 128, :], wscr[wi, c, :, :], 1024)
                for wi, wd in enumerate([wq_d, wk_d, wv_d, wo_d]) for c in range(8)]
        scr_ready = emit_wconv(nc, p, [STG, STGB], [BSC, BSCB], jobs, "a")

        def load_wb(wi):
            return p.ld(lambda e, wi=wi: e.dma_start(
                out=WB[:, :].rearrange("p (c n) -> p c n", n=1024),
                in_=wscr[wi].rearrange("c p n -> p c n")),
                waits=[("GD", scr_ready), ("TE", p.counts.get("_wbused", 0))])

        X0TV = X0T[:, :].rearrange("p (c t) -> p c t", t=512)
        for ch in range(8):
            t0 = ch * 512
            for sub in range(4):
                lx = p.ld(lambda e, t0=t0, sub=sub: e.dma_start(
                    out=XTILE[:, :], in_=x_d[t0 + sub * 128:t0 + sub * 128 + 128, :]),
                    waits=[("TE", p.counts.get("_xtfree", 0))])
                emit_transpose_to_fm(
                    p, lambda hh, sub=sub: X0TV[:, :, sub * 128:sub * 128 + 128][:, hh * 4:hh * 4 + 4, :],
                    XTILE[:, :], PTR, IDENT[:, :], waits=[("LD", lx)])
                p.counts["_xtfree"] = p.c("TE")
            xt_ready = p.c("VE")
            for which in (0, 1):
                wld = load_wb(which)
                for j in range(8):
                    db = j % 2
                    for c in range(8):
                        p.te(lambda e, j=j, c=c, db=db: e.matmul(
                            DD[db][:, :], WB[:, c * 1024 + j * 128:c * 1024 + (j + 1) * 128],
                            X0TV[:, c, :], start=(c == 0), stop=(c == 7)),
                            waits=[("LD", wld), ("VE", xt_ready),
                                   ("SC", p.counts.get("_pqkf%d" % db, 0))],
                            tick=(c == 7))
                    tq = p.c("TE")
                    if which == 1:
                        sev = p.sc(lambda e, j=j, t0=t0, db=db: e.activation(
                            KT[:, j * N_TOK + t0:j * N_TOK + t0 + 512], DD[db][:, :], AF.Identity),
                            waits=[("TE", tq)])
                    else:
                        sev = p.sc(lambda e, db=db: e.activation(QKSTG[db][:, :], DD[db][:, :], AF.Identity),
                                   waits=[("TE", tq), ("ST", p.counts.get("_qkstgfree%d" % db, 0))])
                        p.counts["_qkstgfree%d" % db] = p.st(
                            lambda e, j=j, t0=t0, db=db: e.dma_start(
                                out=qsp[j, :, t0:t0 + 512], in_=QKSTG[db][:, :]),
                            waits=[("SC", sev)])
                    p.counts["_pqkf%d" % db] = sev
                p.counts["_wbused"] = p.c("TE")
            wld = load_wb(2)
            for sub in range(4):
                for c in range(8):
                    for nh in range(2):
                        p.te(lambda e, sub=sub, c=c, nh=nh: e.matmul(
                            PV[:, nh * 512:(nh + 1) * 512],
                            X0TV[:, c, sub * 128:sub * 128 + 128],
                            WB[:, c * 1024 + nh * 512:c * 1024 + (nh + 1) * 512],
                            start=(c == 0), stop=(c == 7)),
                            waits=[("LD", wld), ("VE", xt_ready),
                                   ("VE", p.counts.get("_pvfree", 0))],
                            tick=(c == 7 and nh == 1))
                tv = p.c("TE")
                tile = ch * 4 + sub
                vv = p.ve(lambda e, tile=tile: e.tensor_copy(
                    vtmv[:, tile * 16:(tile + 1) * 16, 0:64],
                    PV[:, :].rearrange("p (h c) -> p h c", c=64)),
                    waits=[("TE", tv), ("VE", mset)])
                p.counts["_pvfree"] = vv
            p.counts["_wbused"] = p.c("TE")
        p.fence()
        for h in range(16):
            par = (h % 2) * 64
            j = h // 2
            for tile in range(32):
                db = tile % 2
                tm = p.te(lambda e, par=par, j=j, tile=tile, db=db: e.matmul(
                    DD[db][:, 0:65],
                    KT[par:par + 64, j * N_TOK + tile * 128:j * N_TOK + (tile + 1) * 128],
                    RHS1[par:par + 64, :], start=True, stop=True),
                    waits=[("VE", rhs_ready), ("VE", p.counts.get("_ddv%d" % db, 0))],
                    tick=True)
                vm = p.ve(lambda e, db=db: e.tensor_reduce(STAT[:, 0:1], DD[db][:, 0:64],
                                                           axis=AX.X, op=ALU.max),
                          waits=[("TE", tm), ("VE", p.counts.get("_statmx", 0))])
                p.counts["_ddv%d" % db] = vm
                vm = p.ve(lambda e: e.tensor_tensor(RMX[:, 0:1], RMX[:, 0:1], STAT[:, 0:1], ALU.max),
                          waits=[("VE", vm)])
                p.counts["_statmx"] = vm
        tmx = p.te(lambda e: e.transpose(PTR[0:1, 0:128], RMX[:, 0:1], IDENT[:, :]),
                   waits=[("VE", p.c("VE")), ("VE", p.counts.get(PTRKEY, 0))], tick=True)
        vm = p.ve(lambda e: e.tensor_copy(LNS[0:1, 0:128], PTR[0:1, 0:128]), waits=[("TE", tmx)])
        p.counts[PTRKEY] = vm
        vm = p.ve(lambda e: e.tensor_reduce(MXS[0:1, 0:1], LNS[0:1, 0:128], axis=AX.X, op=ALU.max),
                  waits=[("VE", vm)])
        vm = p.ve(lambda e: e.tensor_copy(CCS[0:1, 0:1], MXS[0:1, 0:1]), waits=[("VE", vm)])
        gcc = p.gld(lambda e: e.dma_start(out=ccin[:, :], in_=CCS[:, :]), waits=[("VE", vm)])
        p.emit("gpsimd", lambda e: e.collective_compute(
            "AllReduce", mybir.AluOpType.max, replica_groups=[list(range(8))],
            ins=[ccin.ap()], outs=[ccout.ap()]), waits=[("GD", gcc)], incs=[("CC", 1)])
        gcc = p.gld(lambda e: e.dma_start(out=MXS[0:1, 1:2], in_=ccout[0:1, 0:1]),
                    waits=[("CC", 1)])
        tmb = p.te(lambda e: e.matmul(PQK[:, 0:1], ONESR[0:1, :], MXS[0:1, 0:1],
                                      start=True, stop=True),
                   waits=[("VE", vm), ("SC", p.counts.get("_pqkfree", 0))], tick=True)
        vmb = p.ve(lambda e: e.tensor_copy(MXLB[:, :], PQK[:, 0:1]), waits=[("TE", tmb)])
        def emit_ddk(h, tile, db):
            par = (h % 2) * 64
            j = h // 2
            ssq = p.sc(lambda e, par=par, j=j, tile=tile, db=db: e.activation(
                SQW[db][0:64, :], KT[par:par + 64, j * N_TOK + tile * 128:j * N_TOK + (tile + 1) * 128],
                AF.Square),
                waits=[("TE", p.counts.get("_sqwused%d" % db, 0))])
            p.te(lambda e, par=par, j=j, tile=tile, db=db: e.matmul(
                DD[db][:, 0:65],
                KT[par:par + 64, j * N_TOK + tile * 128:j * N_TOK + (tile + 1) * 128],
                RHS1[par:par + 64, :], start=True, stop=False),
                waits=[("VE", p.counts.get("_ddv%d" % db, 0)),
                       ("SC", p.counts.get("_ddexp%d" % db, 0))])
            tm = p.te(lambda e, db=db: e.matmul(
                DD[db][:, 0:65], SQW[db][0:64, :], RHS2[0:64, :], start=False, stop=True),
                waits=[("SC", ssq)], tick=True)
            p.counts["_sqwused%d" % db] = tm
            vb = p.ve(lambda e, db=db: e.tensor_tensor(
                BIA[:, db:db + 1], DD[db][:, 64:65], MXLB[:, :], ALU.subtract),
                waits=[("TE", tm), ("VE", vmb), ("SC", p.counts.get("_biused%d" % db, 0))])
            p.counts["_ddv%d" % db] = vb
            sx = p.sc(lambda e, db=db: e.activation(
                PHK[db][:, 0:64], DD[db][:, 0:64], AF.Exp, bias=BIA[:, db:db + 1]),
                waits=[("VE", vb), ("TE", p.counts.get("_phkused%d" % db, 0))])
            p.counts["_ddexp%d" % db] = sx
            p.counts["_biused%d" % db] = sx
            return sx

        for h in range(16):
            pend = None
            for tile in range(32):
                db = tile % 2
                sx = emit_ddk(h, tile, db)
                if pend is not None:
                    ptile, psx = pend
                    tkv = p.te(lambda e, h=h, ptile=ptile, pdb=ptile % 2: e.matmul(
                        KVP[:, 0:65], PHK[pdb][:, :],
                        VTM[:, (ptile * 16 + h) * 65:(ptile * 16 + h) * 65 + 65],
                        start=(ptile == 0), stop=False),
                        waits=[("SC", psx), ("SC", p.counts.get("_kvpfree", 0))], tick=True)
                    p.counts["_phkused%d" % (ptile % 2)] = tkv
                pend = (tile, sx)
            tkv = p.te(lambda e, h=h: e.matmul(
                KVP[:, 0:65], PHK[31 % 2][:, :], VTM[:, (31 * 16 + h) * 65:(31 * 16 + h) * 65 + 65],
                start=False, stop=True),
                waits=[("SC", pend[1])], tick=True)
            p.counts["_phkused%d" % (31 % 2)] = tkv
            skv = p.sc(lambda e, h=h: e.activation(KVS[0:65, h * 65:(h + 1) * 65], KVP[:, 0:65],
                                                   AF.Identity),
                       waits=[("TE", p.c("TE"))])
            p.counts["_kvpfree"] = skv
        kvs_ready = p.c("SC")
        vm = p.ve(lambda e: e.tensor_tensor(MXS[0:1, 2:3], MXS[0:1, 0:1], MXS[0:1, 1:2], ALU.subtract),
                  waits=[("GD", gcc), ("SC", kvs_ready)])
        sm = p.sc(lambda e: e.activation(MXS[0:1, 3:4], MXS[0:1, 2:3], AF.Exp), waits=[("VE", vm)])
        vm = p.ve(lambda e: e.tensor_scalar(MXS[0:1, 4:5], MXS[0:1, 3:4], RATIO, None, ALU.mult),
                  waits=[("SC", sm)])
        tsp = p.te(lambda e: e.matmul(PQK[:, 0:1], ONESR[0:1, :], MXS[0:1, 4:5],
                                      start=True, stop=True),
                   waits=[("VE", vm)], tick=True)
        vsp = p.ve(lambda e: e.tensor_copy(SPB[:, :], PQK[:, 0:1]), waits=[("TE", tsp)])
        for h in range(16):
            tb = p.te(lambda e, h=h: e.matmul(
                DD[0][:, 0:64][0:64, :], ONESB[64:65, 0:64], KVS[64:65, h * 65:h * 65 + 64],
                start=True, stop=True),
                waits=[("SC", kvs_ready), ("VE", p.counts.get("_ddv0", 0))], tick=True)
            v1 = p.ve(lambda e, h=h: e.tensor_scalar(
                TMPK[:, :], KVS[0:64, h * 65:h * 65 + 64], SPB[0:64, 0:1], None, ALU.mult),
                waits=[("VE", vsp), ("SC", kvs_ready), ("VE", p.counts.get("_tmpkv", 0))])
            v2 = p.ve(lambda e: e.tensor_scalar(
                TMPK2[:, :], DD[0][0:64, 0:64], REPS, None, ALU.mult),
                waits=[("TE", tb), ("VE", v1)])
            p.counts["_ddv0"] = v2
            v3 = p.ve(lambda e, h=h: e.tensor_tensor(
                KVM[0:64, h * 65:h * 65 + 64], TMPK[:, :], TMPK2[:, :], ALU.add),
                waits=[("VE", v2)])
            p.counts["_tmpkv"] = v3
            v4 = p.ve(lambda e, h=h: e.tensor_scalar(
                KVM[0:64, h * 65 + 64:h * 65 + 65], KVS[0:64, h * 65 + 64:h * 65 + 65],
                SPB[0:64, 0:1], REPSN, ALU.mult, ALU.add),
                waits=[("VE", vsp), ("SC", kvs_ready)])
        kvm_ready = p.c("VE")
        p.fence()
        wld_o = load_wb(3)
        for tile in range(32):
            t0 = tile * 128
            qb_ = tile % 2
            QTLB = QTL[qb_]
            lq = p.ld(lambda e, t0=t0, QTLB=QTLB: e.dma_start(
                out=QTLB[:, :].rearrange("p (j t) -> p j t", t=128),
                in_=qsp[:, :, t0:t0 + 128].rearrange("j p t -> p j t")),
                waits=[("TE", p.counts.get("_qtlfree%d" % qb_, 0)), ("SC", p.counts.get("_qtlfree_sc%d" % qb_, 0))])
            lx = p.ld(lambda e, t0=t0: e.dma_start(
                out=XTILE[:, :], in_=x_d[t0:t0 + 128, :]),
                waits=[("VE", p.counts.get("_xtilev", 0))])
            def q_stage1(h, lq):
                par = (h % 2) * 64
                jj = h // 2
                db = h % 2
                ssq = p.sc(lambda e, par=par, jj=jj, db=db, QTLB=QTLB: e.activation(
                    SQW[db][0:64, :], QTLB[par:par + 64, jj * 128:(jj + 1) * 128], AF.Square),
                    waits=[("LD", lq), ("TE", p.counts.get("_sqwused%d" % db, 0))])
                p.te(lambda e, par=par, jj=jj, db=db, QTLB=QTLB: e.matmul(
                    DD[db][:, 0:65], QTLB[par:par + 64, jj * 128:(jj + 1) * 128],
                    RHS1[par:par + 64, :], start=True, stop=False),
                    waits=[("LD", lq), ("VE", p.counts.get("_ddv%d" % db, 0)),
                           ("SC", p.counts.get("_ddexp%d" % db, 0))])
                tm = p.te(lambda e, db=db: e.matmul(
                    DD[db][:, 0:65], SQW[db][0:64, :], RHS2[0:64, :], start=False, stop=True),
                    waits=[("SC", ssq)], tick=True)
                p.counts["_sqwused%d" % db] = tm
                vmx = p.ve(lambda e, db=db: e.tensor_reduce(
                    STAT[:, 1 + db:2 + db], DD[db][:, 0:64], axis=AX.X, op=ALU.max),
                    waits=[("TE", tm), ("SC", p.counts.get("_statq%d" % db, 0))])
                vb = p.ve(lambda e, db=db: e.tensor_tensor(
                    BIA[:, db:db + 1], DD[db][:, 64:65], STAT[:, 1 + db:2 + db], ALU.subtract),
                    waits=[("VE", vmx), ("SC", p.counts.get("_biq%d" % db, 0))])
                p.counts["_ddv%d" % db] = vb
                sx = p.sc(lambda e, db=db: e.activation(
                    PHQF[db][:, :], DD[db][:, 0:64], AF.Exp, bias=BIA[:, db:db + 1]),
                    waits=[("VE", vb), ("VE", p.counts.get("_phqused%d" % db, 0))])
                p.counts["_statq%d" % db] = sx
                p.counts["_biq%d" % db] = sx
                p.counts["_ddexp%d" % db] = sx
                vph = p.ve(lambda e, db=db: e.tensor_scalar(
                    PHQB[db][:, :], PHQF[db][:, :], EPSK, None, ALU.add),
                    waits=[("SC", sx), ("TE", p.counts.get("_phqbused%d" % db, 0))])
                p.counts["_phqused%d" % db] = vph
                return vph

            def q_stage2(h, vph):
                db = h % 2
                tph = p.te(lambda e, db=db: e.transpose(PTB[0:64, :], PHQB[db][:, :], IDB[:, :]),
                           waits=[("VE", vph), ("VE", p.counts.get("_ptbv", 0))], tick=True)
                p.counts["_phqbused%d" % db] = tph
                vqf = p.ve(lambda e, db=db: e.tensor_copy(QFT[db][0:64, :], PTB[0:64, :]),
                           waits=[("TE", tph), ("TE", p.counts.get("_qftused%d" % db, 0))])
                p.counts["_ptbv"] = vqf
                tnm = p.te(lambda e, h=h, db=db: e.matmul(
                    DD[db][:, 0:65], QFT[db][0:64, :], KVM[0:64, h * 65:(h + 1) * 65],
                    start=True, stop=True),
                    waits=[("VE", vqf), ("VE", p.counts.get("_ddv%d" % db, 0))], tick=True)
                p.counts["_qftused%d" % db] = tnm
                vrd = p.ve(lambda e, db=db: e.reciprocal(RD[:, db:db + 1], DD[db][:, 64:65]),
                           waits=[("TE", tnm)])
                vat = p.ve(lambda e, h=h, db=db: e.tensor_scalar(
                    ATT[:, h * 64:(h + 1) * 64], DD[db][:, 0:64], RD[:, db:db + 1], None, ALU.mult),
                    waits=[("VE", vrd), ("TE", p.counts.get("_attused", 0))])
                p.counts["_ddv%d" % db] = vat

            pend_q = None
            for h in range(16):
                s1 = q_stage1(h, lq)
                if pend_q is not None:
                    q_stage2(*pend_q)
                pend_q = (h, s1)
            q_stage2(*pend_q)
            att_ready = p.c("VE")
            for jc in range(8):
                tat = p.te(lambda e, jc=jc: e.transpose(
                    PTB[:, :], ATT[:, jc * 128:(jc + 1) * 128], IDB[:, :]),
                    waits=[("VE", att_ready), ("VE", p.counts.get("_ptbv", 0))], tick=True)
                vaf = p.ve(lambda e, jc=jc: e.tensor_copy(ATF[:, jc * 128:(jc + 1) * 128], PTB[:, :]),
                           waits=[("TE", tat), ("TE", p.counts.get("_atfused", 0))])
                p.counts["_ptbv"] = vaf
            atf_ready = p.c("VE")
            for jc in range(8):
                for nh in range(2):
                    p.te(lambda e, jc=jc, nh=nh: e.matmul(
                        PV[:, nh * 512:(nh + 1) * 512],
                        ATF[:, jc * 128:(jc + 1) * 128],
                        WB[:, jc * 1024 + nh * 512:jc * 1024 + (nh + 1) * 512],
                        start=(jc == 0), stop=(jc == 7)),
                        waits=[("VE", atf_ready), ("LD", wld_o),
                               ("SC", p.counts.get("_pwfree", 0))],
                        tick=(jc == 7 and nh == 1))
            tw = p.c("TE")
            p.counts["_attused"] = tw
            p.counts["_atfused"] = tw
            sy = p.sc(lambda e: e.activation(LNS[:, :], PV[:, :], AF.Identity),
                      waits=[("TE", tw), ("VE", p.counts.get("_lnsv", 0)),
                             ("ST", p.counts.get("_lnst", 0))])
            p.counts["_pwfree"] = sy
            p.counts["_qtlfree_sc%d" % qb_] = sy
            vy = p.ve(lambda e: e.tensor_tensor(LNS[:, :], LNS[:, :], XTILE[:, :], ALU.add),
                      waits=[("SC", sy), ("LD", lx)])
            p.counts["_xtilev"] = vy
            r2 = emit_ln_tile(p, LNS, LNS, STG, STAT[:, 16:24], CONST[:, 0:1],
                              g_ap=G1B[:, :], b_ap=B1B[:, :], waits=[("VE", vy)])
            p.counts["_lnsv"] = max(p.counts.get("_lnsv", 0), r2.get("VE", 0), vy)
            p.counts["_qtlfree%d" % qb_] = p.c("TE")
            p.counts["_lnst"] = p.stv(
                lambda e, t0=t0: e.dma_start(out=out1_d[t0:t0 + 128, :], in_=LNS[:, :]),
                waits=[("SC", r2.get("SC", 0)), ("VE", r2.get("VE", 0))])
        p.emit("sync", lambda e: e.nop(), waits=[("ST", p.c("ST"))])
        p.fence()
        with nc.Block() as block:
            p.flush(block)


def build_l0_attn():
    nc = bass.Bass("TRN2", target_bir_lowering=False)
    x_d = nc.dram_tensor("x", [N_TOK, D], F32, kind="ExternalInput").ap()
    wq_d = nc.dram_tensor("pWq", [D, D], F32, kind="ExternalInput").ap()
    wk_d = nc.dram_tensor("pWk", [D, D], F32, kind="ExternalInput").ap()
    wv_d = nc.dram_tensor("pWv", [D, D], F32, kind="ExternalInput").ap()
    wo_d = nc.dram_tensor("pWo", [D, D], F32, kind="ExternalInput").ap()
    prj_d = nc.dram_tensor("projT", [64, 64], F32, kind="ExternalInput").ap()
    g1_d = nc.dram_tensor("g1row", [1, D], F32, kind="ExternalInput").ap()
    b1_d = nc.dram_tensor("b1row", [1, D], F32, kind="ExternalInput").ap()
    ident_d = nc.dram_tensor("ident", [128, 128], F32, kind="ExternalInput").ap()
    identb_d = nc.dram_tensor("identb", [128, 128], BF16, kind="ExternalInput").ap()
    ones_d = nc.dram_tensor("ones_row", [1, 128], F32, kind="ExternalInput").ap()
    out1_d = nc.dram_tensor("out1", [N_TOK, D], F32, kind="ExternalOutput").ap()
    qsp = nc.dram_tensor("qspill", [8, 128, N_TOK], BF16)
    wscr = nc.dram_tensor("wscr", [4, 8, 128, D], BF16)
    ccin = nc.dram_tensor("ccin", [2, 1024], F32)
    ccout = nc.dram_tensor("ccout", [2, 1024], F32, addr_space="Shared")
    io = dict(x_d=x_d, wq_d=wq_d, wk_d=wk_d, wv_d=wv_d, wo_d=wo_d, prj_d=prj_d, g1_d=g1_d, b1_d=b1_d, ident_d=ident_d, identb_d=identb_d, ones_d=ones_d, out1_d=out1_d, qsp=qsp, wscr=wscr, ccin=ccin, ccout=ccout)
    with ExitStack() as _st:
        p = Prog(nc, _st)
        emit_l0_attn(nc, p, io)
    return nc


def run_l0_attn(x_full, Wq, Wk, Wv, Wo, proj, g1, b1):
    consts = make_consts()
    nc = build_l0_attn()
    in_maps = []
    for c in range(8):
        in_maps.append({
            "x": _f(x_full[c]),
            "pWq": _f(np.asarray(Wq).reshape(D, D)), "pWk": _f(np.asarray(Wk).reshape(D, D)),
            "pWv": _f(np.asarray(Wv).reshape(D, D)), "pWo": _f(np.asarray(Wo).reshape(D, D)),
            "projT": _f(np.asarray(proj).T),
            "g1row": _f(g1).reshape(1, D), "b1row": _f(b1).reshape(1, D),
            "ident": consts["ident"],
            "identb": consts["ident"].astype(ml_dtypes.bfloat16),
            "ones_row": consts["ones_row"],
        })
    res = run_bass_kernel_spmd(nc, in_maps, core_ids=list(range(8)))
    return np.stack([np.asarray(r["out1"]) for r in res.results])


DEBUG_TAPS = False
_PROG_CACHE = {}


def _prog(name, builder):
    if name not in _PROG_CACHE:
        _PROG_CACHE[name] = builder()
    return _PROG_CACHE[name]


def build_fused():
    nc = bass.Bass("TRN2", target_bir_lowering=False)
    T = lambda n, shp, dt=F32, kind=None, space=None: (
        nc.dram_tensor(n, shp, dt, kind=kind, addr_space=space).ap() if kind and space else
        nc.dram_tensor(n, shp, dt, kind=kind).ap() if kind else
        nc.dram_tensor(n, shp, dt, addr_space=space).ap() if space else
        nc.dram_tensor(n, shp, dt).ap())
    EI = "ExternalInput"
    x_d = T("x", [N_TOK, D], kind=EI)
    ident_d = T("ident", [128, 128], kind=EI)
    identb_d = T("identb", [128, 128], BF16, kind=EI)
    ones_d = T("ones_row", [1, 128], kind=EI)
    _ik = "ExternalOutput" if DEBUG_TAPS else None
    out1_d = T("i_out1", [N_TOK, D], kind=_ik)
    x1_d = T("i_x1", [N_TOK, D], kind=_ik)
    x2_d = T("i_x2", [N_TOK, D], kind=_ik)
    x3_d = T("i_x3", [N_TOK, D], kind=_ik)
    out_d = T("out", [N_TOK, D], kind="ExternalOutput")

    io1 = dict(
        x_d=x_d, ident_d=ident_d, identb_d=identb_d, ones_d=ones_d, out1_d=out1_d,
        wq_d=T("pWq", [D, D], kind=EI), wk_d=T("pWk", [D, D], kind=EI),
        wv_d=T("pWv", [D, D], kind=EI), wo_d=T("pWo", [D, D], kind=EI),
        prj_d=T("projT", [64, 64], kind=EI),
        g1_d=T("g1row", [1, D], kind=EI), b1_d=T("b1row", [1, D], kind=EI),
        qsp=nc.dram_tensor("l0qspill", [8, 128, N_TOK], BF16),
        wscr=nc.dram_tensor("l0wscr", [4, 8, 128, D], BF16),
        ccin=nc.dram_tensor("ccin", [2, 1024], F32),
        ccout=nc.dram_tensor("ccout", [2, 1024], F32, addr_space="Shared"),
    )
    io2 = dict(
        out1_d=out1_d, x_d=x_d, ident_d=ident_d, ones_d=ones_d, x1_d=x1_d,
        w1_d=T("W1", [D, 4096], kind=EI), w2_d=T("W2", [4096, D], kind=EI),
        w1scr_d=nc.dram_tensor("w1scr", [32, 8, 128, 128], BF16).ap(),
        b1_d=T("b1pt", [128, 32], kind=EI), b2_d=T("b2row", [1, D], kind=EI),
        g2_d=T("g2row", [1, D], kind=EI), bb2_d=T("bb2row", [1, D], kind=EI),
    )
    io3 = dict(
        x1_d=x1_d, ident_d=ident_d, ones_d=ones_d, x2_d=x2_d,
        w1_d=T("mw1", [8, D, 512], kind=EI), w2_d=T("mw2", [8, 512, D], kind=EI),
        b1_d=T("mb1pt", [128, 32], kind=EI), b2_d=T("mb2", [8, D], kind=EI),
        cen_d=T("cen", [8, D], kind=EI), crt_d=T("crt", [D, 16], kind=EI),
    )
    io4 = dict(
        x2_d=x2_d, ident_d=ident_d, ones_d=ones_d, x3_d=x3_d,
        wq_d=T("awqT", [D, D], kind=EI), wk_d=T("awkT", [D, D], kind=EI),
        wv_d=T("awvT", [D, D], kind=EI), wo_d=T("awoT", [D, D], kind=EI),
        bq_d=T("abqpt", [128, 8], kind=EI), bk_d=T("abkpt", [128, 8], kind=EI),
        bv_d=T("abvrow", [1, D], kind=EI), bo_d=T("aborow", [1, D], kind=EI),
        cos_d=T("cosFt", [128, N_TOK], kind=EI), sin_d=T("sinFt", [128, N_TOK], kind=EI),
        rot_d=T("rotM", [128, 128], kind=EI), ind_d=T("indM", [128, 2], kind=EI),
        qsp=nc.dram_tensor("l1qspill", [8, 128, N_TOK], BF16),
        wscr=nc.dram_tensor("l1wscr", [4, 8, 128, D], BF16),
    )
    NE = 9
    io5 = dict(
        x3_d=x3_d, ident_d=ident_d, ones_d=ones_d, out_d=out_d,
        w1_d=T("fw1", [NE, D, 1408], kind=EI), w3_d=T("fw3", [NE, D, 1408], kind=EI),
        w2_d=T("fw2", [NE, 1408, D], kind=EI),
        b1_d=T("fb1pt", [128, NE * 11], kind=EI), b3_d=T("fb3pt", [128, NE * 11], kind=EI),
        b2_d=T("fb2p", [128, 3 * D], kind=EI),
        gw_d=T("fgwt", [D, 8], kind=EI), gb_d=T("fgbrow", [1, 8], kind=EI),
        w13s=nc.dram_tensor("w13scr", [NE, 11, 128, 2048], BF16),
        w2s=nc.dram_tensor("w2scr", [NE, 128, 11 * D], BF16),
    )
    for tgt, rng in ((io3, (0, 4)), (io4, (4, 9))):
        tgt["mw1_d"] = io5["w1_d"]
        tgt["mw3_d"] = io5["w3_d"]
        tgt["mw2_d"] = io5["w2_d"]
        tgt["mw13s"] = io5["w13s"]
        tgt["mw2s"] = io5["w2s"]
        tgt["mconv_range"] = rng
    io5["preconverted"] = True
    with ExitStack() as _st:
        p = Prog(nc, _st)
        emit_l0_attn(nc, p, io1)
        emit_l0_ffn_half(nc, p, io2)
        emit_l0_moe(nc, p, io3)
        emit_l1_attn(nc, p, io4)
        emit_l1_moe(nc, p, io5)
    return nc


def kernel(**inp):
    consts = make_consts()
    cores = list(range(8))
    x = np.ascontiguousarray(inp["L0_attn_x"], dtype=np.float32)

    rotm = np.zeros((128, 128), np.float32)
    for base in (0, 64):
        for dd in range(32):
            rotm[base + 32 + dd, base + dd] = -1.0
            rotm[base + dd, base + 32 + dd] = 1.0
    indm = np.zeros((128, 2), np.float32)
    indm[0:64, 0] = 1.0
    indm[64:128, 1] = 1.0
    crt = np.ascontiguousarray(np.concatenate(
        [np.asarray(inp["L0_ffn_centroids"]).T, np.asarray(inp["L0_ffn_w_route"]).T], axis=1),
        dtype=np.float32)
    b1pt3 = np.ascontiguousarray(
        np.asarray(inp["L0_ffn_b1"], dtype=np.float32).reshape(8, 4, 128).transpose(2, 0, 1).reshape(128, 32))
    NE = 9
    w1a = np.concatenate([inp["L1_ffn_w1"], inp["L1_ffn_sw1"][None]], 0)
    w3a = np.concatenate([inp["L1_ffn_w3"], inp["L1_ffn_sw3"][None]], 0)
    w2a = np.concatenate([inp["L1_ffn_w2"], inp["L1_ffn_sw2"][None]], 0)
    b1a = np.concatenate([inp["L1_ffn_b1"], inp["L1_ffn_sb1"][None]], 0)
    b3a = np.concatenate([inp["L1_ffn_b3"], inp["L1_ffn_sb3"][None]], 0)
    b2a = np.concatenate([inp["L1_ffn_b2"], inp["L1_ffn_sb2"][None]], 0)

    def pt11(b):
        out = np.zeros((128, NE * 11), np.float32)
        for e in range(NE):
            out[:, e * 11:(e + 1) * 11] = np.asarray(b[e], dtype=np.float32).reshape(11, 128).T
        return np.ascontiguousarray(out)

    b2p = np.zeros((128, 3 * 1024), np.float32)
    for e in range(NE):
        b2p[32 * (e // 3), (e % 3) * 1024:(e % 3 + 1) * 1024] = b2a[e]

    common = {
        "ident": consts["ident"], "identb": consts["ident"].astype(ml_dtypes.bfloat16),
        "ones_row": consts["ones_row"],
        "pWq": _f(inp["L0_attn_Wq"].reshape(D, D)), "pWk": _f(inp["L0_attn_Wk"].reshape(D, D)),
        "pWv": _f(inp["L0_attn_Wv"].reshape(D, D)), "pWo": _f(inp["L0_attn_Wo"].reshape(D, D)),
        "projT": _f(np.asarray(inp["L0_attn_proj"]).T),
        "g1row": _f(inp["L0_attn_ln1_g"]).reshape(1, D), "b1row": _f(inp["L0_attn_ln1_b"]).reshape(1, D),
        "W1": _f(inp["L0_attn_W1"]), "W2": _f(inp["L0_attn_W2"]),
        "b1pt": per_part(inp["L0_attn_b1"]), "b2row": _f(inp["L0_attn_b2"]).reshape(1, D),
        "g2row": _f(inp["L0_attn_ln2_g"]).reshape(1, D), "bb2row": _f(inp["L0_attn_ln2_b"]).reshape(1, D),
        "mw1": _f(inp["L0_ffn_w1"]), "mw2": _f(inp["L0_ffn_w2"]),
        "mb1pt": b1pt3, "mb2": _f(inp["L0_ffn_b2"]), "cen": _f(inp["L0_ffn_centroids"]), "crt": crt,
        "awqT": _f(np.asarray(inp["L1_attn_wq"]).T), "awkT": _f(np.asarray(inp["L1_attn_wk"]).T),
        "awvT": _f(np.asarray(inp["L1_attn_wv"]).T), "awoT": _f(np.asarray(inp["L1_attn_wo"]).T),
        "abqpt": per_part(inp["L1_attn_bq"]), "abkpt": per_part(inp["L1_attn_bk"]),
        "abvrow": _f(inp["L1_attn_bv"]).reshape(1, D), "aborow": _f(inp["L1_attn_bo"]).reshape(1, D),
        "cosFt": consts["cosF"], "sinFt": consts["sinF"], "rotM": rotm, "indM": indm,
        "fw1": _f(w1a), "fw3": _f(w3a), "fw2": _f(w2a),
        "fb1pt": pt11(b1a), "fb3pt": pt11(b3a), "fb2p": _f(b2p),
        "fgwt": _f(np.asarray(inp["L1_ffn_gate_w"]).T), "fgbrow": _f(inp["L1_ffn_gate_b"]).reshape(1, 8),
    }
    nc = _prog("fused", build_fused)
    in_maps = [dict(common, x=_f(x[c])) for c in cores]
    res = run_bass_kernel_spmd(nc, in_maps, core_ids=cores)
    return np.stack([np.asarray(r["out"]) for r in res.results]).astype(np.float32)


# version 14 (final)
# speedup vs baseline: 1.0099x; 1.0088x over previous
import math
from contextlib import ExitStack

import numpy as np
import ml_dtypes
import concourse.bass as bass
import concourse.mybir as mybir
from concourse.bass_utils import run_bass_kernel_spmd

F32 = mybir.dt.float32
BF16 = mybir.dt.bfloat16
AF = mybir.ActivationFunctionType
ALU = mybir.AluOpType
AX = mybir.AxisListType

D = 1024
N_TOK = 4096
N_TILES = N_TOK // 128


class Prog:

    ENGINES = ("sync", "vector", "scalar", "tensor", "gpsimd")

    def __init__(self, nc, stack):
        self.nc = nc
        self.stack = stack
        self.ops = {e: [] for e in self.ENGINES}
        self.sems = {}
        self.counts = {}

    def sem(self, name):
        if name not in self.sems:
            self.sems[name] = self.stack.enter_context(self.nc.semaphore(name))
            self.counts[name] = 0
        return self.sems[name]

    def emit(self, eng, fn, waits=(), incs=()):
        out = {}
        incs = list(incs)
        for s, n in incs:
            self.sem(s)
            self.counts[s] += n
            out[s] = self.counts[s]
        waits = [(s, v) for (s, v) in waits if v and v > 0]
        for s, _v in waits:
            self.sem(s)
        self.ops[eng].append((list(waits), fn, incs))
        return out

    def ld(self, fn, waits=()):
        return self.emit("sync", fn, waits, [("LD", 16)])["LD"]

    def st(self, fn, waits=()):
        return self.emit("sync", fn, waits, [("ST", 16)])["ST"]

    def gld(self, fn, waits=()):
        return self.emit("gpsimd", fn, waits, [("GD", 16)])["GD"]

    def stv(self, fn, waits=()):
        return self.emit("scalar", fn, waits, [("ST", 16)])["ST"]

    def te(self, fn, waits=(), tick=False):
        incs = [("TE", 1)] if tick else []
        r = self.emit("tensor", fn, waits, incs)
        return r.get("TE")

    def ve(self, fn, waits=(), tick=True):
        incs = [("VE", 1)] if tick else []
        r = self.emit("vector", fn, waits, incs)
        return r.get("VE")

    def sc(self, fn, waits=(), tick=True):
        incs = [("SC", 1)] if tick else []
        r = self.emit("scalar", fn, waits, incs)
        return r.get("SC")

    def gp(self, fn, waits=(), tick=True):
        incs = [("GP", 1)] if tick else []
        r = self.emit("gpsimd", fn, waits, incs)
        return r.get("GP")

    def c(self, name):
        return self.counts.get(name, 0)

    def fence(self):
        marks = {}
        for eng, sem in [("sync", "FS"), ("vector", "FV"), ("scalar", "FC"),
                         ("tensor", "FT"), ("gpsimd", "FG")]:
            marks[sem] = self.emit(eng, lambda e: e.nop(), [], [(sem, 1)])[sem]
        dma_waits = [(s, self.c(s)) for s in ("LD", "ST", "GD") if self.c(s) > 0]
        for eng in self.ENGINES:
            waits = [(s, v) for s, v in marks.items()] + dma_waits
            self.emit(eng, lambda e: e.nop(), waits, [])
        for k in [k for k in self.counts if k.startswith("_")]:
            del self.counts[k]

    def flush(self, block):
        self.build(block)
        self.ops = {e: [] for e in self.ENGINES}

    def build(self, block):
        for eng_name in self.ENGINES:
            ops = self.ops[eng_name]
            if not ops:
                continue

            def body(eng, ops=ops):
                done = {}
                for waits, fn, incs in ops:
                    for s, v in waits:
                        if done.get(s, -1) < v:
                            eng.wait_ge(self.sems[s], v)
                            done[s] = v
                    inst = fn(eng)
                    for s, n in incs:
                        inst = inst.then_inc(self.sems[s], n)

            getattr(block, eng_name)(body)


def ceil_div(a, b):
    return (a + b - 1) // b


def bcast_row(p, psum, dst, src_row, ones_col, n, waits=(), npart=128):
    key = "_bcfree%d" % id(psum)
    for c0 in range(0, n, 512):
        w = min(512, n - c0)
        t = p.te(lambda e, c0=c0, w=w: e.matmul(
            psum[0:npart, c0:c0 + w], ones_col[0:1, 0:npart], src_row[0:1, c0:c0 + w],
            start=True, stop=True),
            waits=list(waits) + [("VE", p.counts.get(key, 0))], tick=(c0 + 512 >= n))
    v = p.ve(lambda e: e.tensor_copy(dst[0:npart, 0:n], psum[0:npart, 0:n]),
             waits=[("TE", p.c("TE"))])
    p.counts[key] = v
    return v


def emit_ln_tile(p, dst, xa, scratch, stat, eps_ap, g_ap=None, b_ap=None, waits=(),
                 out_dtype_bf=None):
    c_sc = p.sc(lambda e: e.activation(scratch[:, 0:D], xa[:, 0:D], AF.Square,
                                              accum_out=stat[:, 0:1]), waits=waits)
    c_sc = p.sc(lambda e: e.copy(scratch[:, 0:1], stat[:, 0:1]), waits=[("SC", c_sc)])
    c_ve = p.ve(lambda e: e.reduce_sum(stat[:, 1:2], xa[:, 0:D], axis=AX.X),
                waits=waits)
    c_ve = p.ve(lambda e: e.tensor_scalar(stat[:, 2:3], stat[:, 1:2], 1.0 / D, None, ALU.mult),
                waits=[("VE", c_ve)])
    c_sc2 = p.sc(lambda e: e.activation(stat[:, 3:4], stat[:, 2:3], AF.Square),
                 waits=[("VE", c_ve)])
    c_ve = p.ve(lambda e: e.tensor_scalar(stat[:, 4:5], stat[:, 0:1], 1.0 / D, None, ALU.mult),
                waits=[("SC", c_sc)])
    c_ve = p.ve(lambda e: e.tensor_tensor(stat[:, 4:5], stat[:, 4:5], stat[:, 3:4], ALU.subtract),
                waits=[("SC", c_sc2), ("VE", c_ve)])
    c_sc3 = p.sc(lambda e: e.activation(stat[:, 5:6], stat[:, 4:5], AF.Sqrt, bias=eps_ap),
                 waits=[("VE", c_ve)])
    c_ve = p.ve(lambda e: e.reciprocal(stat[:, 6:7], stat[:, 5:6]),
                waits=[("SC", c_sc3)])
    c_ve = p.ve(lambda e: e.tensor_scalar(stat[:, 7:8], stat[:, 2:3], stat[:, 6:7], -1.0, ALU.mult, ALU.mult),
                waits=[("VE", c_ve)])
    tgt = dst if g_ap is None else scratch
    c_sc4 = p.sc(lambda e: e.activation(tgt[:, 0:D], xa[:, 0:D], AF.Identity,
                                               bias=stat[:, 7:8], scale=stat[:, 6:7]),
                 waits=[("VE", c_ve)])
    if g_ap is not None:
        c_ve = p.ve(lambda e: e.tensor_tensor(scratch[:, 0:D], scratch[:, 0:D], g_ap, ALU.mult),
                    waits=[("SC", c_sc4)])
        c_ve = p.ve(lambda e: e.tensor_tensor(dst[:, 0:D], scratch[:, 0:D], b_ap, ALU.add),
                    waits=[("VE", c_ve)])
        return {"VE": c_ve}
    return {"SC": c_sc4}


def emit_transpose_to_fm(p, dst_view_of_half, src_f32, pst, ident, waits=()):
    key = "_ptfree%d" % id(pst)
    for half in range(2):
        tecnt = None
        for i in range(4):
            c = half * 4 + i
            tecnt = p.te(lambda e, c=c, i=i: e.transpose(
                pst[:, i * 128:(i + 1) * 128], src_f32[:, c * 128:(c + 1) * 128], ident),
                waits=list(waits) + [("VE", p.counts.get(key, 0))],
                tick=(i == 3))
        v = p.ve(lambda e, half=half: e.tensor_copy(
            dst_view_of_half(half),
            pst[:, 0:512].rearrange("p (c t) -> p c t", t=128)),
            waits=[("TE", tecnt)])
        p.counts[key] = v
    return p.c("VE")


class WeightLoader:

    def __init__(self, p, stg, stg_bytes_cols):
        self.p = p
        self.stg = stg
        self.cols = stg_bytes_cols
        self.busy_until = 0

    def load(self, dram_ap, dst_bf, dst_col0, ncols, extra_waits=()):
        p = self.p
        assert ncols <= self.cols
        shp = dram_ap.shape
        if len(shp) == 3:
            stg_view = self.stg[:, 0:ncols].rearrange("p (a n) -> p a n", n=shp[2])
        else:
            stg_view = self.stg[:, 0:ncols]
        ld = p.ld(lambda e: e.dma_start(out=stg_view, in_=dram_ap),
                  waits=[("GP", self.busy_until)] + list(extra_waits))
        g = p.gp(lambda e: e.tensor_copy(dst_bf[:, dst_col0:dst_col0 + ncols],
                                                self.stg[:, 0:ncols]),
                 waits=[("LD", ld)])
        self.busy_until = g
        return g


def emit_wconv(nc, p, stgs, bscs, jobs, tag, via_gp=False):
    n = len(stgs)
    emit_ld = p.gld if via_gp else p.ld
    ldsem = "GD" if via_gp else "LD"
    for i, (src, dst, ncols) in enumerate(jobs):
        bi = i % n
        stg_view = (stgs[bi][:, 0:ncols] if len(src.shape) == 2 else
                    stgs[bi][:, 0:ncols].rearrange("p (a n) -> p a n", n=src.shape[-1]))
        ld = emit_ld(lambda e, src=src, sv=stg_view: e.dma_start(out=sv, in_=src),
                     waits=[("GP", p.counts.get("_wcg%s%d" % (tag, bi), 0))])
        g = p.gp(lambda e, bi=bi, ncols=ncols: e.tensor_copy(
            bscs[bi][:, 0:ncols], stgs[bi][:, 0:ncols]),
            waits=[(ldsem, ld), ("GD", p.counts.get("_wcd%s%d" % (tag, bi), 0))])
        p.counts["_wcg%s%d" % (tag, bi)] = g
        st = p.gld(lambda e, dst=dst, bi=bi, ncols=ncols: e.dma_start(
            out=dst, in_=bscs[bi][:, 0:ncols].rearrange("p (a n) -> p a n", n=dst.shape[-1])
            if len(dst.shape) == 3 else bscs[bi][:, 0:ncols]),
            waits=[("GP", g)])
        p.counts["_wcd%s%d" % (tag, bi)] = st
    return p.c("GD")



def _f(x):
    return np.ascontiguousarray(x, dtype=np.float32)


def make_consts():
    c = {}
    c["ident"] = np.eye(128, dtype=np.float32)
    c["identb"] = np.eye(128, dtype=np.float32)
    c["ones_row"] = np.ones((1, 128), dtype=np.float32)
    hd = 64
    inv_freq = 1.0 / (10000.0 ** (np.arange(0, hd, 2, dtype=np.float32) / hd))
    t = np.arange(N_TOK, dtype=np.float32)
    freqs = np.einsum("i,j->ij", t, inv_freq)
    emb = np.concatenate([freqs, freqs], axis=1)
    cosT = np.cos(emb).T.astype(np.float32)
    sinT = np.sin(emb).T.astype(np.float32)
    c["cosF"] = np.tile(cosT, (2, 1))
    c["sinF"] = np.tile(sinT, (2, 1))
    return c


def per_part(v, width=128):
    v = np.asarray(v, dtype=np.float32)
    n = v.shape[0] // width
    return np.ascontiguousarray(v.reshape(n, width).T)


def emit_l0_ffn_half(nc, p, io):
    w1scr_d = io["w1scr_d"]
    out1_d = io["out1_d"]
    x_d = io["x_d"]
    w1_d = io["w1_d"]
    w2_d = io["w2_d"]
    b1_d = io["b1_d"]
    b2_d = io["b2_d"]
    g2_d = io["g2_d"]
    bb2_d = io["bb2_d"]
    ident_d = io["ident_d"]
    ones_d = io["ones_d"]
    x1_d = io["x1_d"]


    stack = ExitStack()
    with stack:
        sb = lambda name, shape, dt=F32: stack.enter_context(nc.sbuf_tensor(name, shape, dt, allow_name_mangling=True))
        ps = lambda name, shape: stack.enter_context(nc.psum_tensor(name, shape, F32, allow_name_mangling=True))
        W1S = [sb("W1S0", [128, 8 * 128], BF16), sb("W1S1", [128, 8 * 128], BF16)]
        W2B = sb("W2B", [128, 32 * 1024], BF16)
        STG = sb("STG", [128, 2048], F32)
        OC = sb("OC", [128, 4 * 1024], F32)
        XT = sb("XT", [128, 8 * 512], BF16)
        HFM = sb("HFM", [128, 32 * 512], BF16)
        B1 = sb("B1", [128, 32], F32)
        B2T = sb("B2T", [128, 1024], F32)
        G2T = sb("G2T", [128, 1024], F32)
        BB2T = sb("BB2T", [128, 1024], F32)
        IDENT = sb("IDENT", [128, 128], F32)
        ONESR = sb("ONESR", [1, 128], F32)
        XTILE = sb("XTILE", [128, 1024], F32)
        ES = sb("ES", [128, 2 * 512], F32)
        ES2 = sb("ES2", [128, 2 * 512], F32)
        YS = sb("YS", [128, 1024], F32)
        LNS = sb("LNS", [128, 1024], F32)
        STAT = sb("STAT", [128, 16], F32)
        CONST = sb("CONST", [128, 4], F32)
        PT = ps("PT", [128, 512])
        PH = [ps("PH0", [128, 512]), ps("PH1", [128, 512])]
        PY = [ps("PY0", [128, 1024]), ps("PY1", [128, 1024])]
        PB = PY[0]

        p.ve(lambda e: e.memset(CONST[:, 0:1], 1e-6))
        p.ve(lambda e: e.memset(CONST[:, 1:2], 1e-5))
        p.ld(lambda e: e.dma_start(out=IDENT[:, :], in_=ident_d))
        p.ld(lambda e: e.dma_start(out=ONESR[:, :], in_=ones_d))
        p.ld(lambda e: e.dma_start(out=B1[:, :], in_=b1_d))
        ROW = sb("ROW", [1, 3 * 1024], F32)
        p.ld(lambda e: e.dma_start(out=ROW[0:1, 0:1024], in_=b2_d))
        p.ld(lambda e: e.dma_start(out=ROW[0:1, 1024:2048], in_=g2_d))
        ldc = p.ld(lambda e: e.dma_start(out=ROW[0:1, 2048:3072], in_=bb2_d))
        bcast_row(p, PB, B2T, ROW[0:1, 0:1024], ONESR[0:1, :], 1024, waits=[("LD", ldc)])
        bcast_row(p, PB, G2T, ROW[0:1, 1024:2048], ONESR[0:1, :], 1024, waits=[("LD", ldc)])
        bcast_row(p, PB, BB2T, ROW[0:1, 2048:3072], ONESR[0:1, :], 1024, waits=[("LD", ldc)])

        wl = WeightLoader(p, STG, 2048)
        for c in range(16):
            wl.load(w2_d[c * 256:(c + 1) * 256, :].rearrange("(a p) n -> p a n", p=128),
                    W2B, c * 2048, 2048)
        w_ready = p.c("GP")
        BSC1 = sb("BSC1", [128, 1024], BF16)
        BSC2 = sb("BSC2", [128, 1024], BF16)
        STG2 = sb("STG2", [128, 1024], F32)
        w1jobs = [(w1_d[:, dffc * 128:(dffc + 1) * 128].rearrange("(c p) n -> p c n", p=128),
                   w1scr_d[dffc].rearrange("c p f -> p c f"), 1024) for dffc in range(32)]
        p.counts["_wcga0"] = wl.busy_until
        emit_wconv(nc, p, [STG, STG2], [BSC1, BSC2], w1jobs, "a")
        w1scr_ready = p.c("GD")

        def load_w1(dffc, buf):
            ld = p.ld(lambda e, dffc=dffc, buf=buf: e.dma_start(
                out=W1S[buf][:, :].rearrange("p (c f) -> p c f", f=128),
                in_=w1scr_d[dffc].rearrange("c p f -> p c f")),
                waits=[("GD", w1scr_ready), ("TE", p.counts.get("_w1used%d" % buf, 0))])
            return ld

        n_chunk = 8
        XTV = XT[:, :].rearrange("p (c t) -> p c t", t=512)
        for ch in range(n_chunk):
            t0 = ch * 512
            lo = p.ld(lambda e, t0=t0: e.dma_start(out=OC[:, :].rearrange("p (s n) -> p s n", n=1024),
                                                        in_=out1_d[t0:t0 + 512, :].rearrange("(s p) n -> p s n", p=128)),
                      waits=[("TE", p.counts.get("_octe", 0)), ("VE", p.counts.get("_ocve", 0))])
            for sub in range(4):
                emit_transpose_to_fm(
                    p, lambda half, sub=sub: XTV[:, :, sub * 128:sub * 128 + 128][:, half * 4:half * 4 + 4, :],
                    OC[:, sub * 1024:(sub + 1) * 1024], PT, IDENT[:, :], waits=[("LD", lo)])
            p.counts["_octe"] = p.c("TE")
            xt_ready = p.c("VE")
            w1g = [None, None]
            w1g[0] = load_w1(0, 0)
            w1g[1] = load_w1(1, 1)
            for dffc in range(32):
                pb = dffc % 2
                eb = pb * 512
                for kc in range(8):
                    p.te(lambda e, pb=pb, kc=kc: e.matmul(
                        PH[pb][:, :],
                        W1S[pb][:, kc * 128:(kc + 1) * 128],
                        XT[:, kc * 512:(kc + 1) * 512],
                        start=(kc == 0), stop=(kc == 7)),
                        waits=[("LD", w1g[pb]), ("GP", w_ready), ("VE", xt_ready),
                               ("SC", p.counts.get("_phfree%d" % pb, 0))],
                        tick=(kc == 7))
                mmc = p.c("TE")
                p.counts["_w1used%d" % pb] = mmc
                if dffc + 2 < 32:
                    w1g[pb] = load_w1(dffc + 2, pb)
                s1 = p.sc(lambda e, pb=pb, eb=eb, dffc=dffc: e.activation(
                    ES[:, eb:eb + 512], PH[pb][:, :], AF.Identity, bias=B1[:, dffc:dffc + 1]),
                    waits=[("TE", mmc), ("VE", p.counts.get("_esfree%d" % pb, 0))])
                p.counts["_phfree%d" % pb] = s1
                v1 = p.ve(lambda e, eb=eb: e.tensor_scalar(
                    ES2[:, eb:eb + 512], ES[:, eb:eb + 512], 0.0, None, ALU.min),
                    waits=[("SC", s1)])
                s2 = p.sc(lambda e, eb=eb: e.activation(
                    ES2[:, eb:eb + 512], ES2[:, eb:eb + 512], AF.Exp),
                    waits=[("VE", v1)])
                v2 = p.ve(lambda e, eb=eb: e.tensor_scalar(
                    ES[:, eb:eb + 512], ES[:, eb:eb + 512], 0.0, -1.0, ALU.max, ALU.add),
                    waits=[("SC", s1), ("VE", v1)])
                vh = p.ve(lambda e, eb=eb, dffc=dffc: e.tensor_tensor(
                    HFM[:, dffc * 512:(dffc + 1) * 512], ES2[:, eb:eb + 512], ES[:, eb:eb + 512], ALU.add),
                    waits=[("SC", s2), ("VE", v2)])
                p.counts["_esfree%d" % pb] = vh
            h_ready = p.c("VE")
            for sub in range(4):
                pb = sub % 2
                for dffc in range(32):
                    for nh in range(2):
                        p.te(lambda e, pb=pb, dffc=dffc, nh=nh, sub=sub: e.matmul(
                            PY[pb][:, nh * 512:(nh + 1) * 512],
                            HFM[:, dffc * 512 + sub * 128:dffc * 512 + (sub + 1) * 128],
                            W2B[:, dffc * 1024 + nh * 512:dffc * 1024 + (nh + 1) * 512],
                            start=(dffc == 0), stop=(dffc == 31)),
                            waits=[("VE", h_ready), ("SC", p.counts.get("_pyfree%d" % pb, 0))],
                            tick=(dffc == 31 and nh == 1))
                mmy = p.c("TE")
                lx = p.ld(lambda e, t0=t0, sub=sub: e.dma_start(
                    out=XTILE[:, :], in_=x_d[t0 + sub * 128:t0 + sub * 128 + 128, :]),
                    waits=[("VE", p.counts.get("_xtile_free", 0))])
                sy = p.sc(lambda e, pb=pb: e.activation(
                    YS[:, :], PY[pb][:, :], AF.Identity),
                    waits=[("TE", mmy), ("ST", p.counts.get("_ysfree", 0))])
                p.counts["_pyfree%d" % pb] = sy
                vy = p.ve(lambda e: e.tensor_tensor(YS[:, :], YS[:, :], B2T[:, :], ALU.add),
                          waits=[("SC", sy)])
                vy = p.ve(lambda e, sub=sub: e.tensor_tensor(
                    YS[:, :], YS[:, :], OC[:, sub * 1024:(sub + 1) * 1024], ALU.add),
                    waits=[("VE", vy)])
                p.counts["_ocve"] = vy
                r = emit_ln_tile(p, YS, YS, LNS, STAT, CONST[:, 0:1], g_ap=G2T[:, :], b_ap=BB2T[:, :],
                                 waits=[("VE", vy)])
                vv = p.ve(lambda e: e.tensor_tensor(YS[:, :], YS[:, :], XTILE[:, :], ALU.add),
                          waits=[("LD", lx), ("VE", r.get("VE", 0)), ("SC", r.get("SC", 0))])
                p.counts["_xtile_free"] = vv
                r2 = emit_ln_tile(p, YS, YS, LNS, STAT, CONST[:, 1:2], waits=[("VE", vv)])
                p.counts["_ysfree"] = p.stv(
                    lambda e, t0=t0, sub=sub: e.dma_start(
                        out=x1_d[t0 + sub * 128:t0 + sub * 128 + 128, :], in_=YS[:, :]),
                    waits=[("SC", r2.get("SC", 0)), ("VE", r2.get("VE", 0))])
        p.emit("sync", lambda e: e.nop(), waits=[("ST", p.c("ST"))])

        p.fence()
        with nc.Block() as block:
            p.flush(block)


def build_l0_ffn_half():
    nc = bass.Bass("TRN2", target_bir_lowering=False)
    out1_d = nc.dram_tensor("out1", [N_TOK, D], F32, kind="ExternalInput").ap()
    x_d = nc.dram_tensor("x", [N_TOK, D], F32, kind="ExternalInput").ap()
    w1_d = nc.dram_tensor("W1", [D, 4096], F32, kind="ExternalInput").ap()
    w2_d = nc.dram_tensor("W2", [4096, D], F32, kind="ExternalInput").ap()
    b1_d = nc.dram_tensor("b1pt", [128, 32], F32, kind="ExternalInput").ap()
    b2_d = nc.dram_tensor("b2row", [1, D], F32, kind="ExternalInput").ap()
    g2_d = nc.dram_tensor("g2row", [1, D], F32, kind="ExternalInput").ap()
    bb2_d = nc.dram_tensor("bb2row", [1, D], F32, kind="ExternalInput").ap()
    ident_d = nc.dram_tensor("ident", [128, 128], F32, kind="ExternalInput").ap()
    ones_d = nc.dram_tensor("ones_row", [1, 128], F32, kind="ExternalInput").ap()
    x1_d = nc.dram_tensor("x1", [N_TOK, D], F32, kind="ExternalOutput").ap()
    w1scr_d = nc.dram_tensor("w1scr", [32, 8, 128, 128], BF16)
    io = dict(out1_d=out1_d, x_d=x_d, w1_d=w1_d, w2_d=w2_d, b1_d=b1_d, b2_d=b2_d, g2_d=g2_d, bb2_d=bb2_d, ident_d=ident_d, ones_d=ones_d, x1_d=x1_d, w1scr_d=w1scr_d)
    with ExitStack() as _st:
        p = Prog(nc, _st)
        emit_l0_ffn_half(nc, p, io)
    return nc


def run_l0_ffn_half(out1_full, x_full, W1, W2, b1, b2, g2, bb2, n_launch_counter=None):
    consts = make_consts()
    nc = build_l0_ffn_half()
    in_maps = []
    for c in range(8):
        in_maps.append({
            "out1": _f(out1_full[c]), "x": _f(x_full[c]),
            "W1": _f(W1), "W2": _f(W2),
            "b1pt": per_part(b1), "b2row": _f(b2).reshape(1, D),
            "g2row": _f(g2).reshape(1, D), "bb2row": _f(bb2).reshape(1, D),
            "ident": consts["ident"], "ones_row": consts["ones_row"],
        })
    res = run_bass_kernel_spmd(nc, in_maps, core_ids=list(range(8)))
    return np.stack([np.asarray(r["x1"]) for r in res.results])


def emit_l0_moe(nc, p, io):
    x1_d = io["x1_d"]
    w1_d = io["w1_d"]
    w2_d = io["w2_d"]
    b1_d = io["b1_d"]
    b2_d = io["b2_d"]
    cen_d = io["cen_d"]
    crt_d = io["crt_d"]
    ident_d = io["ident_d"]
    ones_d = io["ones_d"]
    x2_d = io["x2_d"]


    stack = ExitStack()
    with stack:
        sb = lambda name, shape, dt=F32: stack.enter_context(nc.sbuf_tensor(name, shape, dt, allow_name_mangling=True))
        ps = lambda name, shape: stack.enter_context(nc.psum_tensor(name, shape, F32, allow_name_mangling=True))
        W1B = sb("W1B", [128, 8 * 4096], BF16)
        W2B = sb("W2B", [128, 8 * 4096], BF16)
        STG = sb("STG", [128, 2048], F32)
        X1C = sb("X1C", [128, 4 * 1024], F32)
        X1T = sb("X1T", [128, 8 * 512], BF16)
        HBF = sb("HBF", [128, 4 * 512], BF16)
        YS = sb("YSB", [128, 4 * 1024], F32)
        CRS = sb("CRS", [128, 8 * 16], BF16)
        CEN = sb("CEN", [8, 1024], F32)
        B2A = sb("B2A", [8, 1024], BF16)
        INVCB = sb("INVCB", [128, 8], F32)
        IDENT = sb("IDENT", [128, 128], F32)
        ONESR = sb("ONESR", [1, 128], F32)
        B1PT = sb("B1PT", [128, 32], F32)
        GATES = sb("GATES", [128, 4 * 8], F32)
        GTB = sb("GTB", [8, 4 * 128], BF16)
        SSV = sb("SSV", [128, 32], F32)
        ROWS = sb("ROWS", [8, 16], F32)
        LNS = sb("LNS", [128, 1024], F32)
        STAT = sb("STAT", [128, 32], F32)
        CONST = sb("CONST", [128, 4], F32)
        PT = ps("PT", [128, 512])
        PHE = [ps("PHE0", [128, 512]), ps("PHE1", [128, 512])]
        PYE = [ps("PYE0", [128, 1024]), ps("PYE1", [128, 1024])]
        PLOG = ps("PLOG", [128, 512])

        p.ve(lambda e: e.memset(CONST[:, 0:1], 1e-6))
        p.ve(lambda e: e.memset(CONST[:, 1:2], 1e-5))
        p.ve(lambda e: e.memset(CONST[:, 2:3], 1e-12))
        p.ld(lambda e: e.dma_start(out=IDENT[:, :], in_=ident_d))
        p.ld(lambda e: e.dma_start(out=ONESR[:, :], in_=ones_d))
        lcen = p.ld(lambda e: e.dma_start(out=CEN[:, :], in_=cen_d))
        p.ld(lambda e: e.dma_start(out=B1PT[:, :], in_=b1_d))
        wl = WeightLoader(p, STG, 2048)
        for c in range(8):
            wl.load(crt_d[c * 128:(c + 1) * 128, :], CRS, c * 16, 16)
        ldb2 = p.ld(lambda e: e.dma_start(out=STG[0:8, 0:1024], in_=b2_d[:, :]),
                    waits=[("GP", wl.busy_until)])
        wl.busy_until = p.gp(lambda e: e.tensor_copy(B2A[0:8, :], STG[0:8, 0:1024]),
                             waits=[("LD", ldb2)])
        for e in range(8):
            for c in range(4):
                wl.load(w1_d[e, c * 256:(c + 1) * 256, :].rearrange("(a p) n -> p a n", p=128),
                        W1B, e * 4096 + c * 1024, 1024)
        for e in range(8):
            for c in range(4):
                wl.load(w2_d[e, c * 128:(c + 1) * 128, :], W2B, e * 4096 + c * 1024, 1024)
        w_ready = p.c("GP")

        s0 = p.sc(lambda e: e.activation(LNS[0:8, 0:1024], CEN[:, :], AF.Square,
                                         accum_out=ROWS[0:8, 0:1]), waits=[("LD", lcen)])
        s0 = p.sc(lambda e: e.copy(LNS[0:8, 0:1], ROWS[0:8, 0:1]), waits=[("SC", s0)])
        s1 = p.sc(lambda e: e.activation(ROWS[0:8, 1:2], ROWS[0:8, 0:1], AF.Sqrt),
                  waits=[("SC", s0)])
        v1 = p.ve(lambda e: e.tensor_scalar(ROWS[0:8, 2:3], ROWS[0:8, 1:2], 1e-12, None, ALU.max),
                  waits=[("SC", s1)])
        v1 = p.ve(lambda e: e.reciprocal(ROWS[0:8, 3:4], ROWS[0:8, 2:3]), waits=[("VE", v1)])
        t1 = p.te(lambda e: e.transpose(PT[0:1, 0:8], ROWS[0:8, 3:4], IDENT[0:8, 0:8]),
                  waits=[("VE", v1), ("LD", p.c("LD"))], tick=True)
        v1 = p.ve(lambda e: e.tensor_copy(SSV[0:1, 0:8], PT[0:1, 0:8]), waits=[("TE", t1)])
        bcast_row(p, PLOG, INVCB, SSV[0:1, 0:8], ONESR[0:1, :], 8, waits=[("VE", v1)])

        n_chunk = 8
        X1TV = X1T[:, :].rearrange("p (c t) -> p c t", t=512)
        for ch in range(n_chunk):
            t0 = ch * 512
            lo = p.ld(lambda e, t0=t0: e.dma_start(out=X1C[:, :].rearrange("p (s n) -> p s n", n=1024),
                                                   in_=x1_d[t0:t0 + 512, :].rearrange("(s p) n -> p s n", p=128)),
                      waits=[("TE", p.counts.get("_x1cte", 0)), ("VE", p.counts.get("_x1cve", 0)),
                             ("SC", p.counts.get("_x1csc", 0))])
            for sub in range(4):
                emit_transpose_to_fm(
                    p, lambda half, sub=sub: X1TV[:, :, sub * 128:sub * 128 + 128][:, half * 4:half * 4 + 4, :],
                    X1C[:, sub * 1024:(sub + 1) * 1024], PT, IDENT[:, :], waits=[("LD", lo)])
            p.counts["_x1cte"] = p.c("TE")
            xt_ready = p.c("VE")
            for sub in range(4):
                sq = p.sc(lambda e, sub=sub: e.activation(LNS[:, 0:1024], X1C[:, sub * 1024:(sub + 1) * 1024],
                                                          AF.Square, accum_out=SSV[:, sub * 4:sub * 4 + 1]),
                          waits=[("LD", lo), ("VE", p.counts.get("_lnsv", 0))])
                sq = p.sc(lambda e, sub=sub: e.copy(LNS[:, 0:1], SSV[:, sub * 4:sub * 4 + 1]),
                          waits=[("SC", sq)])
                p.counts["_x1csc"] = sq
                sq = p.sc(lambda e, sub=sub: e.activation(SSV[:, sub * 4 + 1:sub * 4 + 2],
                                                          SSV[:, sub * 4:sub * 4 + 1], AF.Sqrt),
                          waits=[("SC", sq)])
                vq = p.ve(lambda e, sub=sub: e.tensor_scalar(SSV[:, sub * 4 + 1:sub * 4 + 2],
                                                             SSV[:, sub * 4 + 1:sub * 4 + 2],
                                                             CONST[:, 2:3], None, ALU.max),
                          waits=[("SC", sq)])
                vq = p.ve(lambda e, sub=sub: e.reciprocal(SSV[:, sub * 4 + 2:sub * 4 + 3],
                                                          SSV[:, sub * 4 + 1:sub * 4 + 2]),
                          waits=[("VE", vq)])
                for c in range(8):
                    p.te(lambda e, c=c, sub=sub: e.matmul(
                        PLOG[:, 0:16], X1TV[:, c, sub * 128:sub * 128 + 128],
                        CRS[:, c * 16:(c + 1) * 16],
                        start=(c == 0), stop=(c == 7)),
                        waits=[("GP", w_ready), ("VE", xt_ready),
                               ("VE", p.counts.get("_plogfree", 0))],
                        tick=(c == 7))
                tl = p.c("TE")
                vq = p.ve(lambda e: e.tensor_copy(SSV[:, 16:32], PLOG[:, 0:16]),
                          waits=[("TE", tl), ("VE", vq)])
                p.counts["_plogfree"] = vq
                vq = p.ve(lambda e, sub=sub: e.tensor_scalar(SSV[:, 16:24], SSV[:, 16:24],
                                                             SSV[:, sub * 4 + 2:sub * 4 + 3], None, ALU.mult),
                          waits=[("VE", vq)])
                vq = p.ve(lambda e: e.tensor_tensor(SSV[:, 16:24], SSV[:, 16:24], INVCB[:, 0:8], ALU.mult),
                          waits=[("VE", vq)])
                vq = p.ve(lambda e: e.tensor_tensor(SSV[:, 16:24], SSV[:, 16:24], SSV[:, 24:32], ALU.add),
                          waits=[("VE", vq)])
                vq = p.ve(lambda e: e.max(STAT[:, 0:8], SSV[:, 16:24]), waits=[("VE", vq)])
                vq = p.ve(lambda e: e.tensor_scalar(STAT[:, 8:16], SSV[:, 16:24], STAT[:, 1:2], None, ALU.is_ge),
                          waits=[("VE", vq)])
                vq = p.ve(lambda e: e.tensor_scalar(STAT[:, 1:2], STAT[:, 0:1], -1.0, None, ALU.mult),
                          waits=[("VE", vq)])
                se = p.sc(lambda e: e.activation(SSV[:, 16:24], SSV[:, 16:24], AF.Exp, bias=STAT[:, 1:2]),
                          waits=[("VE", vq)])
                vq = p.ve(lambda e: e.tensor_tensor(SSV[:, 16:24], SSV[:, 16:24], STAT[:, 8:16], ALU.mult),
                          waits=[("SC", se)])
                vq = p.ve(lambda e: e.reduce_sum(STAT[:, 2:3], SSV[:, 16:24], axis=AX.X),
                          waits=[("VE", vq)])
                vq = p.ve(lambda e: e.reciprocal(STAT[:, 3:4], STAT[:, 2:3]), waits=[("VE", vq)])
                vq = p.ve(lambda e, sub=sub: e.tensor_scalar(GATES[:, sub * 8:(sub + 1) * 8],
                                                             SSV[:, 16:24], STAT[:, 3:4], None, ALU.mult),
                          waits=[("VE", vq)])
                tg = p.te(lambda e, sub=sub: e.transpose(PT[0:8, 0:128], GATES[:, sub * 8:(sub + 1) * 8],
                                                         IDENT[:, :]),
                          waits=[("VE", vq), ("VE", p.counts.get("_ptfree%d" % id(PT), 0))], tick=True)
                vq = p.ve(lambda e, sub=sub: e.tensor_copy(GTB[0:8, sub * 128:(sub + 1) * 128], PT[0:8, 0:128]),
                          waits=[("TE", tg)])
                p.counts["_ptfree%d" % id(PT)] = vq
                tb = None
                for nh in range(2):
                    tb = p.te(lambda e, sub=sub, nh=nh: e.matmul(
                        PYE[sub % 2][:, nh * 512:(nh + 1) * 512],
                        GTB[0:8, sub * 128:(sub + 1) * 128],
                        B2A[0:8, nh * 512:(nh + 1) * 512],
                        start=True, stop=True),
                        waits=[("VE", vq), ("SC", p.counts.get("_pyefree%d" % (sub % 2), 0)),
                               ("VE", p.counts.get("_pyefree_v%d" % (sub % 2), 0))],
                        tick=(nh == 1))
                vy = p.ve(lambda e, sub=sub: e.tensor_copy(YS[:, sub * 1024:(sub + 1) * 1024],
                                                           PYE[sub % 2][:, :]),
                          waits=[("TE", tb), ("ST", p.counts.get("_ys_st%d" % sub, 0))])
                p.counts["_pyefree_v%d" % (sub % 2)] = vy
            gates_ready_ve = p.c("VE")
            for ex in range(8):
                for dffc in range(4):
                    pb = dffc % 2
                    for c in range(8):
                        p.te(lambda e, ex=ex, dffc=dffc, c=c, pb=pb: e.matmul(
                            PHE[pb][:, :],
                            W1B[:, ex * 4096 + c * 512 + dffc * 128:ex * 4096 + c * 512 + (dffc + 1) * 128],
                            X1T[:, c * 512:(c + 1) * 512],
                            start=(c == 0), stop=(c == 7)),
                            waits=[("GP", w_ready), ("VE", xt_ready),
                                   ("SC", p.counts.get("_phefree%d" % pb, 0))],
                            tick=(c == 7))
                    th = p.c("TE")
                    sg = p.sc(lambda e, ex=ex, dffc=dffc, pb=pb: e.activation(
                        HBF[:, dffc * 512:(dffc + 1) * 512], PHE[pb][:, :], AF.Gelu,
                        bias=B1PT[:, ex * 4 + dffc:ex * 4 + dffc + 1]),
                        waits=[("TE", th), ("TE", p.counts.get("_hbfused", 0))])
                    p.counts["_phefree%d" % pb] = sg
                for sub in range(4):
                    pb = sub % 2
                    for dffc in range(4):
                        for nh in range(2):
                            p.te(lambda e, ex=ex, dffc=dffc, nh=nh, sub=sub, pb=pb: e.matmul(
                                PYE[pb][:, nh * 512:(nh + 1) * 512],
                                HBF[:, dffc * 512 + sub * 128:dffc * 512 + (sub + 1) * 128],
                                W2B[:, ex * 4096 + dffc * 1024 + nh * 512:ex * 4096 + dffc * 1024 + (nh + 1) * 512],
                                start=(dffc == 0), stop=(dffc == 3)),
                                waits=[("SC", p.c("SC")), ("SC", p.counts.get("_pyefree%d" % pb, 0)),
                                       ("VE", p.counts.get("_pyefree_v%d" % pb, 0))],
                                tick=(dffc == 3 and nh == 1))
                    ty = p.c("TE")
                    sg = p.sc(lambda e, ex=ex, sub=sub, pb=pb: e.activation(
                        LNS[:, 0:1024], PYE[pb][:, :], AF.Identity,
                        scale=GATES[:, sub * 8 + ex:sub * 8 + ex + 1]),
                        waits=[("TE", ty), ("VE", p.counts.get("_lnsv", 0))])
                    p.counts["_pyefree%d" % pb] = sg
                    vy = p.ve(lambda e, sub=sub: e.tensor_tensor(
                        YS[:, sub * 1024:(sub + 1) * 1024], YS[:, sub * 1024:(sub + 1) * 1024],
                        LNS[:, 0:1024], ALU.add),
                        waits=[("SC", sg), ("VE", gates_ready_ve)])
                    p.counts["_lnsv"] = vy
                p.counts["_hbfused"] = p.c("TE")
            for sub in range(4):
                ysub = YS[:, sub * 1024:(sub + 1) * 1024]
                vv = p.ve(lambda e, ysub=ysub, sub=sub: e.tensor_tensor(
                    ysub, ysub, X1C[:, sub * 1024:(sub + 1) * 1024], ALU.add),
                    waits=[("VE", p.c("VE"))])
                p.counts["_x1cve"] = vv
                r2 = emit_ln_tile(p, ysub, ysub, LNS, STAT[:, 16:24], CONST[:, 1:2],
                                  waits=[("VE", vv)])
                p.counts["_lnsv"] = max(p.counts.get("_lnsv", 0), r2.get("VE", 0))
                p.counts["_ys_st%d" % sub] = p.stv(
                    lambda e, t0=t0, sub=sub, ysub=ysub: e.dma_start(
                        out=x2_d[t0 + sub * 128:t0 + sub * 128 + 128, :], in_=ysub),
                    waits=[("SC", r2.get("SC", 0)), ("VE", r2.get("VE", 0))])
        p.emit("sync", lambda e: e.nop(), waits=[("ST", p.c("ST"))])
        p.fence()
        with nc.Block() as block:
            p.flush(block)


def build_l0_moe():
    nc = bass.Bass("TRN2", target_bir_lowering=False)
    x1_d = nc.dram_tensor("x1", [N_TOK, D], F32, kind="ExternalInput").ap()
    w1_d = nc.dram_tensor("mw1", [8, D, 512], F32, kind="ExternalInput").ap()
    w2_d = nc.dram_tensor("mw2", [8, 512, D], F32, kind="ExternalInput").ap()
    b1_d = nc.dram_tensor("mb1pt", [128, 32], F32, kind="ExternalInput").ap()
    b2_d = nc.dram_tensor("mb2", [8, D], F32, kind="ExternalInput").ap()
    cen_d = nc.dram_tensor("cen", [8, D], F32, kind="ExternalInput").ap()
    crt_d = nc.dram_tensor("crt", [D, 16], F32, kind="ExternalInput").ap()
    ident_d = nc.dram_tensor("ident", [128, 128], F32, kind="ExternalInput").ap()
    ones_d = nc.dram_tensor("ones_row", [1, 128], F32, kind="ExternalInput").ap()
    x2_d = nc.dram_tensor("x2", [N_TOK, D], F32, kind="ExternalOutput").ap()
    io = dict(x1_d=x1_d, w1_d=w1_d, w2_d=w2_d, b1_d=b1_d, b2_d=b2_d, cen_d=cen_d, crt_d=crt_d, ident_d=ident_d, ones_d=ones_d, x2_d=x2_d)
    with ExitStack() as _st:
        p = Prog(nc, _st)
        emit_l0_moe(nc, p, io)
    return nc


def run_l0_moe(x1_full, w1, b1, w2, b2, centroids, w_route):
    consts = make_consts()
    nc = build_l0_moe()
    crt = np.ascontiguousarray(np.concatenate([np.asarray(centroids).T, np.asarray(w_route).T], axis=1), dtype=np.float32)
    b1pt = np.ascontiguousarray(np.asarray(b1, dtype=np.float32).reshape(8, 4, 128).transpose(2, 0, 1).reshape(128, 32))
    in_maps = []
    for c in range(8):
        in_maps.append({
            "x1": _f(x1_full[c]), "mw1": _f(w1), "mw2": _f(w2),
            "mb1pt": b1pt, "mb2": _f(b2), "cen": _f(centroids), "crt": crt,
            "ident": consts["ident"], "ones_row": consts["ones_row"],
        })
    res = run_bass_kernel_spmd(nc, in_maps, core_ids=list(range(8)))
    return np.stack([np.asarray(r["x2"]) for r in res.results])


def emit_l1_moe(nc, p, io):
    x3_d = io["x3_d"]
    w1_d = io["w1_d"]
    w3_d = io["w3_d"]
    w2_d = io["w2_d"]
    b1_d = io["b1_d"]
    b3_d = io["b3_d"]
    b2_d = io["b2_d"]
    gw_d = io["gw_d"]
    gb_d = io["gb_d"]
    ident_d = io["ident_d"]
    ones_d = io["ones_d"]
    out_d = io["out_d"]
    w13s = io["w13s"]
    w2s = io["w2s"]
    DFF = 1408
    NE = 9


    stack = ExitStack()
    with stack:
        sb = lambda name, shape, dt=F32: stack.enter_context(nc.sbuf_tensor(name, shape, dt, allow_name_mangling=True))
        ps = lambda name, shape: stack.enter_context(nc.psum_tensor(name, shape, F32, allow_name_mangling=True))
        STG = sb("STG", [128, 1536], F32)
        BSC = sb("BSC", [128, 1536], BF16)
        X3C = sb("X3C", [128, 4 * 1024], F32)
        X3T = sb("X3T", [128, 8 * 1024], BF16)
        HBF = sb("HBF", [128, 11 * 1024], BF16)
        YS = sb("YSB", [128, 8 * 1024], F32)
        W2E = sb("W2E", [128, 11 * 1024], BF16)
        W13C = [sb("W13C%d" % _i, [128, 2 * 8 * 128], BF16) for _i in range(4)]
        GWB = sb("GWB", [128, 8 * 8], BF16)
        GBC = sb("GBC", [128, 8], F32)
        B1PT = sb("B1PT", [128, NE * 11], F32)
        B3PT = sb("B3PT", [128, NE * 11], F32)
        B2P = sb("B2P", [128, 3 * 1024], F32)
        RW = sb("RW", [128, 16 * 8], F32)
        SSV = sb("SSV", [128, 48], F32)
        ES = sb("ES", [128, 512], F32)
        ES2 = sb("ES2", [128, 512], F32)
        LNS = sb("LNS", [128, 1024], F32)
        STAT = sb("STAT", [128, 32], F32)
        CONST = sb("CONST", [128, 4], F32)
        IDENT = sb("IDENT", [128, 128], F32)
        ONESR = sb("ONESR", [1, 128], F32)
        ONESB = sb("ONESB", [128, 128], F32)
        PT = ps("PT", [128, 512])
        PH1 = ps("PH1", [128, 512])
        PH3 = ps("PH3", [128, 512])
        PY = [ps("PY0", [128, 1024]), ps("PY1", [128, 1024])]
        PLOG = ps("PLOG", [128, 512])

        p.ve(lambda e: e.memset(CONST[:, 1:2], 1e-5))
        ones_ready = p.ve(lambda e: e.memset(ONESB[:, :], 1.0))
        p.ld(lambda e: e.dma_start(out=IDENT[:, :], in_=ident_d))
        p.ld(lambda e: e.dma_start(out=ONESR[:, :], in_=ones_d))
        p.ld(lambda e: e.dma_start(out=B1PT[:, :], in_=b1_d))
        p.ld(lambda e: e.dma_start(out=B3PT[:, :], in_=b3_d))
        p.ld(lambda e: e.dma_start(out=B2P[:, :], in_=b2_d))
        lgb = p.ld(lambda e: e.dma_start(out=SSV[0:1, 0:8], in_=gb_d))
        bcast_row(p, PLOG, GBC, SSV[0:1, 0:8], ONESR[0:1, :], 8, waits=[("LD", lgb)])
        wl = WeightLoader(p, STG, 1536)
        for c in range(8):
            wl.load(gw_d[c * 128:(c + 1) * 128, :], GWB, c * 8, 8)
        gw_ready = p.c("GP")

        for e in (range(NE) if not io.get("preconverted") else []):
            for c in range(8):
                ldw = p.ld(lambda e_, e=e, c=c: e_.dma_start(
                    out=STG[:, 0:1408], in_=w1_d[e, c * 128:(c + 1) * 128, :]),
                    waits=[("GD", p.counts.get("_bscfree", 0))])
                g = p.gp(lambda e_: e_.tensor_copy(BSC[:, 0:1408], STG[:, 0:1408]),
                         waits=[("LD", ldw), ("GD", p.counts.get("_bscfree", 0))])
                st = p.gld(lambda e_, e=e, c=c: e_.dma_start(
                    out=w13s[e, :, :, c * 128:(c + 1) * 128].rearrange("k p f -> p k f"),
                    in_=BSC[:, 0:1408].rearrange("p (k f) -> p k f", f=128)),
                    waits=[("GP", g)])
                p.counts["_bscfree"] = st
                ldw = p.ld(lambda e_, e=e, c=c: e_.dma_start(
                    out=STG[:, 0:1408], in_=w3_d[e, c * 128:(c + 1) * 128, :]),
                    waits=[("GD", p.counts.get("_bscfree", 0))])
                g = p.gp(lambda e_: e_.tensor_copy(BSC[:, 0:1408], STG[:, 0:1408]),
                         waits=[("LD", ldw), ("GD", p.counts.get("_bscfree", 0))])
                st = p.gld(lambda e_, e=e, c=c: e_.dma_start(
                    out=w13s[e, :, :, 1024 + c * 128:1024 + (c + 1) * 128].rearrange("k p f -> p k f"),
                    in_=BSC[:, 0:1408].rearrange("p (k f) -> p k f", f=128)),
                    waits=[("GP", g)])
                p.counts["_bscfree"] = st
            for k in range(11):
                ldw = p.ld(lambda e_, e=e, k=k: e_.dma_start(
                    out=STG[:, 0:1024], in_=w2_d[e, k * 128:(k + 1) * 128, :]),
                    waits=[("GD", p.counts.get("_bscfree", 0))])
                g = p.gp(lambda e_: e_.tensor_copy(BSC[:, 0:1024], STG[:, 0:1024]),
                         waits=[("LD", ldw), ("GD", p.counts.get("_bscfree", 0))])
                st = p.gld(lambda e_, e=e, k=k: e_.dma_start(
                    out=w2s[e, :, k * 1024:(k + 1) * 1024], in_=BSC[:, 0:1024]),
                    waits=[("GP", g)])
                p.counts["_bscfree"] = st
        scr_ready = p.c("GD")

        X3TV = X3T[:, :].rearrange("p (c t) -> p c t", t=1024)
        for half in range(4):
            h0 = half * 1024
            for qc in range(2):
                t0 = h0 + qc * 512
                lo = p.ld(lambda e, t0=t0: e.dma_start(
                    out=X3C[:, :].rearrange("p (s n) -> p s n", n=1024),
                    in_=x3_d[t0:t0 + 512, :].rearrange("(s p) n -> p s n", p=128)),
                    waits=[("TE", p.counts.get("_x3cte", 0)), ("VE", p.counts.get("_x3cve", 0))])
                for sub in range(4):
                    emit_transpose_to_fm(
                        p, lambda hh, sub=sub, qc=qc: X3TV[:, :, qc * 512 + sub * 128:qc * 512 + sub * 128 + 128][:, hh * 4:hh * 4 + 4, :],
                        X3C[:, sub * 1024:(sub + 1) * 1024], PT, IDENT[:, :], waits=[("LD", lo)])
                p.counts["_x3cte"] = p.c("TE")
                for sub in range(4):
                    st_i = qc * 4 + sub
                    for c in range(8):
                        p.te(lambda e, c=c, qc=qc, sub=sub: e.matmul(
                            PLOG[:, 0:8], X3TV[:, c, qc * 512 + sub * 128:qc * 512 + sub * 128 + 128],
                            GWB[:, c * 8:(c + 1) * 8],
                            start=(c == 0), stop=(c == 7)),
                            waits=[("GP", gw_ready), ("VE", p.c("VE")),
                                   ("VE", p.counts.get("_plogfree", 0))],
                            tick=(c == 7))
                    tl = p.c("TE")
                    vq = p.ve(lambda e: e.tensor_copy(SSV[:, 8:16], PLOG[:, 0:8]), waits=[("TE", tl)])
                    p.counts["_plogfree"] = vq
                    vq = p.ve(lambda e: e.reduce_max(STAT[:, 0:1], SSV[:, 8:16], axis=AX.X, negate=True),
                              waits=[("VE", vq)])
                    se = p.sc(lambda e: e.activation(SSV[:, 16:24], SSV[:, 8:16], AF.Exp, bias=STAT[:, 0:1]),
                              waits=[("VE", vq)])
                    vq = p.ve(lambda e: e.reduce_sum(STAT[:, 1:2], SSV[:, 16:24], axis=AX.X),
                              waits=[("SC", se)])
                    vq = p.ve(lambda e: e.reciprocal(STAT[:, 2:3], STAT[:, 1:2]), waits=[("VE", vq)])
                    vq = p.ve(lambda e: e.tensor_scalar(SSV[:, 16:24], SSV[:, 16:24], STAT[:, 2:3], None, ALU.mult),
                              waits=[("VE", vq)])
                    vq = p.ve(lambda e: e.tensor_tensor(SSV[:, 24:32], SSV[:, 16:24], GBC[:, 0:8], ALU.add),
                              waits=[("VE", vq)])
                    vq = p.ve(lambda e: e.max(STAT[:, 8:16], SSV[:, 24:32]), waits=[("VE", vq)])
                    vq = p.ve(lambda e: e.tensor_scalar(SSV[:, 32:40], SSV[:, 24:32], STAT[:, 9:10], None, ALU.is_ge),
                              waits=[("VE", vq)])
                    vq = p.ve(lambda e, st_i=st_i: e.tensor_tensor(RW[:, st_i * 8:(st_i + 1) * 8],
                                                                   SSV[:, 32:40], SSV[:, 16:24], ALU.mult),
                              waits=[("VE", vq)])
            xt_ready = p.c("VE")
            rw_ready = p.c("VE")
            vz = None
            for sub in range(8):
                vz = p.ve(lambda e, sub=sub: e.memset(YS[:, sub * 1024:(sub + 1) * 1024], 0.0),
                          waits=[("SC", p.counts.get("_ysread", 0)), ("ST", p.counts.get("_ysst", 0))])
            for ex in range(NE):
                for _h3 in range(3):
                    _c0, _c1 = _h3 * 4096, min((_h3 + 1) * 4096, 11 * 1024)
                    lw2 = p.ld(lambda e, ex=ex, _c0=_c0, _c1=_c1: e.dma_start(
                        out=W2E[:, _c0:_c1], in_=w2s[ex, :, _c0:_c1]),
                        waits=[("GD", scr_ready), ("TE", p.counts.get("_w2used", 0))])
                wg = [None, None, None, None]

                def load_w13(dffc, buf, ex=ex):
                    ld = p.ld(lambda e, dffc=dffc, buf=buf, ex=ex: e.dma_start(
                        out=W13C[buf][:, :], in_=w13s[ex, dffc, :, :]),
                        waits=[("GD", scr_ready), ("TE", p.counts.get("_w13used%d" % buf, 0))])
                    return ld

                for _i in range(4):
                    wg[_i] = load_w13(_i, _i)
                for dffc in range(11):
                    buf = dffc % 4
                    for qc in range(2):
                        for c in range(8):
                            p.te(lambda e, c=c, qc=qc, buf=buf: e.matmul(
                                PH1[:, :], W13C[buf][:, c * 128:(c + 1) * 128],
                                X3TV[:, c, qc * 512:(qc + 1) * 512],
                                start=(c == 0), stop=(c == 7)),
                                waits=[("LD", wg[buf]), ("VE", xt_ready),
                                       ("SC", p.counts.get("_ph1free", 0))])
                        for c in range(8):
                            p.te(lambda e, c=c, qc=qc, buf=buf: e.matmul(
                                PH3[:, :], W13C[buf][:, 1024 + c * 128:1024 + (c + 1) * 128],
                                X3TV[:, c, qc * 512:(qc + 1) * 512],
                                start=(c == 0), stop=(c == 7)),
                                waits=[("SC", p.counts.get("_ph3free", 0))],
                                tick=(c == 7))
                        th = p.c("TE")
                        s1 = p.sc(lambda e, ex=ex, dffc=dffc: e.activation(
                            ES[:, :], PH1[:, :], AF.Identity, bias=B1PT[:, ex * 11 + dffc:ex * 11 + dffc + 1]),
                            waits=[("TE", th), ("VE", p.counts.get("_esfreev", 0))])
                        p.counts["_ph1free"] = s1
                        s3 = p.sc(lambda e, ex=ex, dffc=dffc: e.activation(
                            ES2[:, :], PH3[:, :], AF.Identity, bias=B3PT[:, ex * 11 + dffc:ex * 11 + dffc + 1]),
                            waits=[("TE", th), ("VE", p.counts.get("_esfreev", 0))])
                        p.counts["_ph3free"] = s3
                        v1 = p.ve(lambda e: e.tensor_tensor(ES[:, :], ES[:, :], ES2[:, :], ALU.mult),
                                  waits=[("SC", s3), ("SC", s1)])
                        p.counts["_esfreev"] = v1
                        sh = p.sc(lambda e, dffc=dffc, qc=qc: e.activation(
                            HBF[:, dffc * 1024 + qc * 512:dffc * 1024 + (qc + 1) * 512], ES[:, :], AF.Silu),
                            waits=[("VE", v1), ("TE", p.counts.get("_hbfused", 0))])
                    p.counts["_w13used%d" % buf] = p.c("TE")
                    if dffc + 4 < 11:
                        wg[buf] = load_w13(dffc + 4, buf)
                for _i in range(4):
                    p.counts["_w13used%d" % _i] = p.c("TE")
                h_ready = p.c("SC")
                for sub in range(8):
                    pb = sub % 2
                    for dffc in range(11):
                        for nh in range(2):
                            p.te(lambda e, dffc=dffc, nh=nh, sub=sub, pb=pb: e.matmul(
                                PY[pb][:, nh * 512:(nh + 1) * 512],
                                HBF[:, dffc * 1024 + sub * 128:dffc * 1024 + (sub + 1) * 128],
                                W2E[:, dffc * 1024 + nh * 512:dffc * 1024 + (nh + 1) * 512],
                                start=(dffc == 0), stop=False),
                                waits=[("SC", h_ready), ("LD", lw2),
                                       ("SC", p.counts.get("_pyfree%d" % pb, 0)),
                                       ("VE", p.counts.get("_pyfreev%d" % pb, 0))])
                    bpart = 32 * (ex // 3)
                    bcol = (ex % 3) * 1024
                    for nh in range(2):
                        p.te(lambda e, nh=nh, sub=sub, pb=pb, bpart=bpart, bcol=bcol: e.matmul(
                            PY[pb][:, nh * 512:(nh + 1) * 512],
                            ONESB[bpart:bpart + 1, 0:128],
                            B2P[bpart:bpart + 1, bcol + nh * 512:bcol + (nh + 1) * 512],
                            start=False, stop=(nh == 1)),
                            waits=[("VE", ones_ready)], tick=(nh == 1))
                    ty = p.c("TE")
                    if ex < 8:
                        sg = p.sc(lambda e, ex=ex, sub=sub, pb=pb: e.activation(
                            LNS[:, 0:1024], PY[pb][:, :], AF.Identity,
                            scale=RW[:, sub * 8 + ex:sub * 8 + ex + 1]),
                            waits=[("TE", ty), ("VE", p.counts.get("_lnsv", 0)), ("VE", rw_ready)])
                    else:
                        sg = p.sc(lambda e, pb=pb: e.activation(
                            LNS[:, 0:1024], PY[pb][:, :], AF.Identity),
                            waits=[("TE", ty), ("VE", p.counts.get("_lnsv", 0))])
                    p.counts["_pyfree%d" % pb] = sg
                    vy = p.ve(lambda e, sub=sub: e.tensor_tensor(
                        YS[:, sub * 1024:(sub + 1) * 1024], YS[:, sub * 1024:(sub + 1) * 1024],
                        LNS[:, 0:1024], ALU.add),
                        waits=[("SC", sg), ("VE", vz)])
                    p.counts["_lnsv"] = vy
                    p.counts["_pyfreev%d" % pb] = vy
                p.counts["_hbfused"] = p.c("TE")
                p.counts["_w2used"] = p.c("TE")
            for sub in range(8):
                lx = p.ld(lambda e, h0=h0, sub=sub: e.dma_start(
                    out=X3C[:, 0:1024], in_=x3_d[h0 + sub * 128:h0 + sub * 128 + 128, :]),
                    waits=[("VE", p.counts.get("_x3cfree", 0)), ("TE", p.counts.get("_x3cte", 0))])
                ysub = YS[:, sub * 1024:(sub + 1) * 1024]
                vv = p.ve(lambda e, ysub=ysub: e.tensor_tensor(ysub, ysub, X3C[:, 0:1024], ALU.add),
                          waits=[("LD", lx), ("VE", p.c("VE"))])
                p.counts["_x3cfree"] = vv
                p.counts["_x3cve"] = vv
                r2 = emit_ln_tile(p, ysub, ysub, LNS, STAT[:, 16:24], CONST[:, 1:2],
                                  waits=[("VE", vv)])
                p.counts["_lnsv"] = max(p.counts.get("_lnsv", 0), vv)
                p.counts["_ysst"] = p.stv(
                    lambda e, h0=h0, sub=sub, ysub=ysub: e.dma_start(
                        out=out_d[h0 + sub * 128:h0 + sub * 128 + 128, :], in_=ysub),
                    waits=[("SC", r2.get("SC", 0)), ("VE", r2.get("VE", 0))])
            p.counts["_ysread"] = p.c("SC")
        p.emit("sync", lambda e: e.nop(), waits=[("ST", p.c("ST"))])
        p.fence()
        with nc.Block() as block:
            p.flush(block)


def build_l1_moe():
    nc = bass.Bass("TRN2", target_bir_lowering=False)
    x3_d = nc.dram_tensor("x3", [N_TOK, D], F32, kind="ExternalInput").ap()
    w1_d = nc.dram_tensor("fw1", [NE, D, DFF], F32, kind="ExternalInput").ap()
    w3_d = nc.dram_tensor("fw3", [NE, D, DFF], F32, kind="ExternalInput").ap()
    w2_d = nc.dram_tensor("fw2", [NE, DFF, D], F32, kind="ExternalInput").ap()
    b1_d = nc.dram_tensor("fb1pt", [128, NE * 11], F32, kind="ExternalInput").ap()
    b3_d = nc.dram_tensor("fb3pt", [128, NE * 11], F32, kind="ExternalInput").ap()
    b2_d = nc.dram_tensor("fb2p", [128, 3 * D], F32, kind="ExternalInput").ap()
    gw_d = nc.dram_tensor("fgwt", [D, 8], F32, kind="ExternalInput").ap()
    gb_d = nc.dram_tensor("fgbrow", [1, 8], F32, kind="ExternalInput").ap()
    ident_d = nc.dram_tensor("ident", [128, 128], F32, kind="ExternalInput").ap()
    ones_d = nc.dram_tensor("ones_row", [1, 128], F32, kind="ExternalInput").ap()
    out_d = nc.dram_tensor("out", [N_TOK, D], F32, kind="ExternalOutput").ap()
    w13s = nc.dram_tensor("w13scr", [NE, 11, 128, 2048], BF16)
    w2s = nc.dram_tensor("w2scr", [NE, 128, 11 * D], BF16)
    io = dict(x3_d=x3_d, w1_d=w1_d, w3_d=w3_d, w2_d=w2_d, b1_d=b1_d, b3_d=b3_d, b2_d=b2_d, gw_d=gw_d, gb_d=gb_d, ident_d=ident_d, ones_d=ones_d, out_d=out_d, w13s=w13s, w2s=w2s)
    with ExitStack() as _st:
        p = Prog(nc, _st)
        emit_l1_moe(nc, p, io)
    return nc


def run_l1_moe(x3_full, gate_w, gate_b, w1, b1, w3, b3, w2, b2, sw1, sb1, sw3, sb3, sw2, sb2):
    consts = make_consts()
    nc = build_l1_moe()
    NE = 9
    w1a = np.concatenate([w1, sw1[None]], 0)
    w3a = np.concatenate([w3, sw3[None]], 0)
    w2a = np.concatenate([w2, sw2[None]], 0)
    b1a = np.concatenate([b1, sb1[None]], 0)
    b3a = np.concatenate([b3, sb3[None]], 0)
    b2a = np.concatenate([b2, sb2[None]], 0)

    def pt11(b):
        out = np.zeros((128, NE * 11), np.float32)
        for e in range(NE):
            out[:, e * 11:(e + 1) * 11] = b[e].reshape(11, 128).T
        return np.ascontiguousarray(out)

    b2p = np.zeros((128, 3 * 1024), np.float32)
    for e in range(NE):
        b2p[32 * (e // 3), (e % 3) * 1024:(e % 3 + 1) * 1024] = b2a[e]
    in_maps = []
    for c in range(8):
        in_maps.append({
            "x3": _f(x3_full[c]), "fw1": _f(w1a), "fw3": _f(w3a), "fw2": _f(w2a),
            "fb1pt": pt11(b1a), "fb3pt": pt11(b3a), "fb2p": _f(b2p),
            "fgwt": _f(np.asarray(gate_w).T), "fgbrow": _f(gate_b).reshape(1, 8),
            "ident": consts["ident"], "ones_row": consts["ones_row"],
        })
    res = run_bass_kernel_spmd(nc, in_maps, core_ids=list(range(8)))
    return np.stack([np.asarray(r["out"]) for r in res.results])


def emit_l1_attn(nc, p, io):
    x2_d = io["x2_d"]
    wq_d = io["wq_d"]
    wk_d = io["wk_d"]
    wv_d = io["wv_d"]
    wo_d = io["wo_d"]
    bq_d = io["bq_d"]
    bk_d = io["bk_d"]
    bv_d = io["bv_d"]
    bo_d = io["bo_d"]
    cos_d = io["cos_d"]
    sin_d = io["sin_d"]
    rot_d = io["rot_d"]
    ind_d = io["ind_d"]
    ident_d = io["ident_d"]
    ones_d = io["ones_d"]
    x3_d = io["x3_d"]
    qsp = io["qsp"]
    wscr = io["wscr"]


    stack = ExitStack()
    with stack:
        sb = lambda name, shape, dt=F32: stack.enter_context(nc.sbuf_tensor(name, shape, dt, allow_name_mangling=True))
        ps = lambda name, shape: stack.enter_context(nc.psum_tensor(name, shape, F32, allow_name_mangling=True))
        KT = sb("KT", [128, 8 * N_TOK], BF16)
        VTM = sb("VTM", [128, 32 * 16 * 65], BF16)
        WB = sb("WB", [128, 8 * 1024], BF16)
        STG = sb("STG", [128, 1024], F32)
        BSC = sb("BSC", [128, 1024], BF16)
        X2TILE = sb("X2TILE", [128, 1024], F32)
        BB = sb("BB", [128, 1024], F32)
        RQK = sb("RQK", [2, 32], F32)
        NEGB = sb("NEGB", [128, 1], F32)
        LNS = sb("LNS", [128, 1024], F32)
        STAT = sb("STAT", [128, 32], F32)
        CONST = sb("CONST", [128, 4], F32)
        IDENT = sb("IDENT", [128, 128], F32)
        ONESR = sb("ONESR", [1, 128], F32)
        LNS2 = STG
        PTR = ps("PTR", [128, 512])
        PQK = ps("PQK", [128, 512])
        PV = ps("PV", [128, 1024])
        SPS1 = ps("SPS1", [128, 1024])
        OPS = [ps("OPS0", [65, 512]), ps("OPS1", [65, 512])]

        SCRH = [0]
        stackA = ExitStack()
        with stackA:
            sbA = lambda name, shape, dt=F32: stackA.enter_context(nc.sbuf_tensor(name, shape, dt, allow_name_mangling=True))
            X2T = sbA("X2T", [128, 8 * 512], BF16)
            QF = sbA("QF", [128, 512], F32)
            TMPA = sbA("TMPA", [128, 512], F32)
            TMPB = sbA("TMPB", [128, 512], F32)
            SQF = sbA("SQF", [128, 512], F32)
            QSTG = sbA("QSTG", [128, 512], BF16)
            COSC = sbA("COSC", [128, 512], F32)
            SINC = sbA("SINC", [128, 512], F32)
            ROTM = sbA("ROTM", [128, 128], F32)
            INDM = sbA("INDM", [128, 2], F32)
            BQPT = sbA("BQPT", [128, 8], F32)
            BKPT = sbA("BKPT", [128, 8], F32)
            PTRKEY = "_ptfree%d" % id(PTR)
            p.ve(lambda e: e.memset(CONST[:, 1:2], 1e-5))
            vtmv = VTM[:, :].rearrange("p (g c) -> p g c", c=65)
            p.ve(lambda e: e.memset(vtmv[:, :, 64:65], 1.0))
            p.ve(lambda e: e.memset(RQK[0:2, 0:32], 0.0))
            for d, dst in [(ident_d, IDENT), (ones_d, ONESR), (rot_d, ROTM), (ind_d, INDM),
                           (bq_d, BQPT), (bk_d, BKPT)]:
                p.ld(lambda e, d=d, dst=dst: e.dma_start(out=dst[:, :], in_=d))
            lbv = p.ld(lambda e: e.dma_start(out=LNS[0:1, 0:1024], in_=bv_d))
            bcast_row(p, PV, BB, LNS[0:1, 0:1024], ONESR[0:1, :], 1024, waits=[("LD", lbv)])
            bv_done = p.c("VE")
            p.counts["_pvfree"] = bv_done

            STGB = sbA("STGB", [128, 1024], F32)
            BSCB = sbA("BSCB", [128, 1024], BF16)
            jobs = [(wd[c * 128:(c + 1) * 128, :], wscr[wi, c, :, :], 1024)
                    for wi, wd in enumerate([wq_d, wk_d, wv_d, wo_d]) for c in range(8)]
            SCRH[0] = emit_wconv(nc, p, [STG, STGB], [BSC, BSCB], jobs, "a")
            if "mw1_d" in io:
                STGC = sbA("STGC", [128, 1408], F32)
                BSCC = sbA("BSCC", [128, 1408], BF16)
                mjobs = []
                for e_ in range(9):
                    for c in range(8):
                        mjobs.append((io["mw1_d"][e_, c * 128:(c + 1) * 128, :],
                                      io["mw13s"][e_, :, :, c * 128:(c + 1) * 128].rearrange("k p f -> p k f"), 1408))
                        mjobs.append((io["mw3_d"][e_, c * 128:(c + 1) * 128, :],
                                      io["mw13s"][e_, :, :, 1024 + c * 128:1024 + (c + 1) * 128].rearrange("k p f -> p k f"), 1408))
                    for k_ in range(11):
                        mjobs.append((io["mw2_d"][e_, k_ * 128:(k_ + 1) * 128, :],
                                      io["mw2s"][e_, :, k_ * 1024:(k_ + 1) * 1024], 1024))
                emit_wconv(nc, p, [STGC], [BSCC], mjobs, "m", via_gp=True)

            def load_wb(wi):
                return p.ld(lambda e, wi=wi: e.dma_start(
                    out=WB[:, :].rearrange("p (c n) -> p c n", n=1024),
                    in_=wscr[wi].rearrange("c p n -> p c n")),
                    waits=[("GD", SCRH[0]), ("TE", p.counts.get("_wbused", 0))])

            X2TV = X2T[:, :].rearrange("p (c t) -> p c t", t=512)
            for ch in range(8):
                t0 = ch * 512
                lc = p.ld(lambda e, t0=t0: e.dma_start(out=COSC[:, :], in_=cos_d[:, t0:t0 + 512]),
                          waits=[("VE", p.counts.get("_coscfree", 0))])
                lc = p.ld(lambda e, t0=t0: e.dma_start(out=SINC[:, :], in_=sin_d[:, t0:t0 + 512]))
                for sub in range(4):
                    lx = p.ld(lambda e, t0=t0, sub=sub: e.dma_start(
                        out=X2TILE[:, :], in_=x2_d[t0 + sub * 128:t0 + sub * 128 + 128, :]),
                        waits=[("TE", p.counts.get("_x2tfree", 0))])
                    emit_transpose_to_fm(
                        p, lambda hh, sub=sub: X2TV[:, :, sub * 128:sub * 128 + 128][:, hh * 4:hh * 4 + 4, :],
                        X2TILE[:, :], PTR, IDENT[:, :], waits=[("LD", lx)])
                    p.counts["_x2tfree"] = p.c("TE")
                xt_ready = p.c("VE")
                for which, bpt in [(0, BQPT), (1, BKPT)]:
                    wld = load_wb(which)
                    for j in range(8):
                        for c in range(8):
                            p.te(lambda e, j=j, c=c: e.matmul(
                                PQK[:, :], WB[:, c * 1024 + j * 128:c * 1024 + (j + 1) * 128],
                                X2TV[:, c, :], start=(c == 0), stop=(c == 7)),
                                waits=[("LD", wld), ("VE", xt_ready),
                                       ("SC", p.counts.get("_pqkfree", 0))],
                                tick=(c == 7))
                        tq = p.c("TE")
                        sev = p.sc(lambda e, j=j, bpt=bpt: e.activation(
                            QF[:, :], PQK[:, :], AF.Identity, bias=bpt[:, j:j + 1]),
                            waits=[("TE", tq), ("VE", p.counts.get("_qffree", 0))])
                        p.counts["_pqkfree"] = sev
                        ssq = p.sc(lambda e: e.activation(SQF[:, :], QF[:, :], AF.Square),
                                   waits=[("SC", sev), ("TE", p.counts.get("_sqffree", 0))])
                        tn = p.te(lambda e: e.matmul(PTR[0:2, 0:512], INDM[:, :], SQF[:, :],
                                                     start=True, stop=True),
                                  waits=[("SC", ssq), ("VE", p.counts.get(PTRKEY, 0))], tick=True)
                        p.counts["_sqffree"] = tn
                        vn = p.ve(lambda e: e.tensor_reduce(STAT[0:2, 0:1], PTR[0:2, 0:512],
                                                            axis=AX.X, op=ALU.max),
                                  waits=[("TE", tn)])
                        p.counts[PTRKEY] = vn
                        col = 16 * which + j
                        vn = p.ve(lambda e, col=col: e.tensor_tensor(
                            RQK[0:2, col:col + 1], RQK[0:2, col:col + 1], STAT[0:2, 0:1], ALU.max),
                            waits=[("VE", vn)])
                        tr = p.te(lambda e: e.matmul(PTR[:, :], ROTM[:, :], QF[:, :],
                                                     start=True, stop=True),
                                  waits=[("SC", sev), ("VE", p.counts.get(PTRKEY, 0))], tick=True)
                        va = p.ve(lambda e: e.tensor_tensor(TMPA[:, :], QF[:, :], COSC[:, :], ALU.mult),
                                  waits=[("SC", sev), ("LD", lc), ("VE", p.counts.get("_tmpafree", 0))])
                        p.counts["_qffree"] = va
                        vb = p.ve(lambda e: e.tensor_tensor(TMPB[:, :], PTR[:, :], SINC[:, :], ALU.mult),
                                  waits=[("TE", tr), ("LD", lc), ("VE", va)])
                        p.counts[PTRKEY] = vb
                        if which == 1:
                            vc = p.ve(lambda e, j=j, t0=t0: e.tensor_tensor(
                                KT[:, j * N_TOK + t0:j * N_TOK + t0 + 512], TMPA[:, :], TMPB[:, :], ALU.add),
                                waits=[("VE", vb)])
                        else:
                            vc = p.ve(lambda e: e.tensor_tensor(QSTG[:, :], TMPA[:, :], TMPB[:, :], ALU.add),
                                      waits=[("VE", vb), ("ST", p.counts.get("_qstgfree", 0))])
                            p.counts["_qstgfree"] = p.st(
                                lambda e, j=j, t0=t0: e.dma_start(
                                    out=qsp[j, :, t0:t0 + 512], in_=QSTG[:, :]),
                                waits=[("VE", vc)])
                        p.counts["_tmpafree"] = vc
                    p.counts["_wbused"] = p.c("TE")
                wld = load_wb(2)
                for sub in range(4):
                    for c in range(8):
                        for nh in range(2):
                            p.te(lambda e, sub=sub, c=c, nh=nh: e.matmul(
                                PV[:, nh * 512:(nh + 1) * 512],
                                X2TV[:, c, sub * 128:sub * 128 + 128],
                                WB[:, c * 1024 + nh * 512:c * 1024 + (nh + 1) * 512],
                                start=(c == 0), stop=(c == 7)),
                                waits=[("LD", wld), ("VE", xt_ready),
                                       ("VE", p.counts.get("_pvfree", 0))],
                                tick=(c == 7 and nh == 1))
                    tv = p.c("TE")
                    tile = ch * 4 + sub
                    vv = p.ve(lambda e, tile=tile: e.tensor_tensor(
                        vtmv[:, tile * 16:(tile + 1) * 16, 0:64],
                        PV[:, :].rearrange("p (h c) -> p h c", c=64),
                        BB[:, :].rearrange("p (h c) -> p h c", c=64), ALU.add),
                        waits=[("TE", tv), ("VE", bv_done)])
                    p.counts["_pvfree"] = vv
                p.counts["_wbused"] = p.c("TE")
            vq = p.ve(lambda e: e.tensor_reduce(RQK[0:2, 16:17], RQK[0:2, 0:8], axis=AX.X, op=ALU.max),
                      waits=[("VE", p.c("VE"))])
            vq = p.ve(lambda e: e.tensor_reduce(RQK[0:2, 17:18], RQK[0:2, 8:16], axis=AX.X, op=ALU.max),
                      waits=[("VE", vq)])
            tb = p.te(lambda e: e.transpose(PTR[0:2, 0:2], RQK[0:2, 16:18], IDENT[0:2, 0:2]),
                      waits=[("VE", vq), ("VE", p.counts.get(PTRKEY, 0))], tick=True)
            vq = p.ve(lambda e: e.tensor_copy(STAT[0:2, 0:2], PTR[0:2, 0:2]), waits=[("TE", tb)])
            p.counts[PTRKEY] = vq
            vq = p.ve(lambda e: e.tensor_reduce(STAT[0:2, 2:3], STAT[0:2, 0:2], axis=AX.X, op=ALU.max),
                      waits=[("VE", vq)])
            sq = p.sc(lambda e: e.activation(STAT[0:2, 3:4], STAT[0:2, 2:3], AF.Sqrt),
                      waits=[("VE", vq)])
            tb = p.te(lambda e: e.transpose(PTR[0:1, 0:2], STAT[0:2, 3:4], IDENT[0:2, 0:2]),
                      waits=[("SC", sq), ("VE", p.counts.get(PTRKEY, 0))], tick=True)
            vq = p.ve(lambda e: e.tensor_copy(STAT[0:1, 4:6], PTR[0:1, 0:2]), waits=[("TE", tb)])
            p.counts[PTRKEY] = vq
            vq = p.ve(lambda e: e.tensor_tensor(STAT[0:1, 6:7], STAT[0:1, 4:5], STAT[0:1, 5:6], ALU.mult),
                      waits=[("VE", vq)])
            vq = p.ve(lambda e: e.tensor_scalar(STAT[0:1, 7:8], STAT[0:1, 6:7], -0.125, None, ALU.mult),
                      waits=[("VE", vq)])
            tb = p.te(lambda e: e.matmul(PQK[:, 0:1], ONESR[0:1, :], STAT[0:1, 7:8],
                                         start=True, stop=True),
                      waits=[("VE", vq), ("SC", p.counts.get("_pqkfree", 0))], tick=True)
            vq = p.ve(lambda e: e.tensor_copy(NEGB[:, :], PQK[:, 0:1]), waits=[("TE", tb)])
            lbo = p.ld(lambda e: e.dma_start(out=LNS[0:1, 0:1024], in_=bo_d),
                       waits=[("VE", p.c("VE"))])
            for c0 in range(0, 1024, 512):
                tb = p.te(lambda e, c0=c0: e.matmul(PQK[:, 0:512], ONESR[0:1, :],
                                                    LNS[0:1, c0:c0 + 512], start=True, stop=True),
                          waits=[("LD", lbo), ("VE", p.counts.get("_bbk", 0))], tick=True)
                vq = p.ve(lambda e, c0=c0: e.tensor_copy(BB[:, c0:c0 + 512], PQK[:, 0:512]),
                          waits=[("TE", tb)])
                p.counts["_bbk"] = vq
            bo_done = p.c("VE")
            p.fence()
            with nc.Block() as blockA:
                p.flush(blockA)
        stackB = ExitStack()
        with stackB:
            sbB = lambda name, shape, dt=F32: stackB.enter_context(nc.sbuf_tensor(name, shape, dt, allow_name_mangling=True))
            QBD = [sbB("QBD0", [128, 512], BF16), sbB("QBD1", [128, 512], BF16)]
            PT16 = [sbB("PT16_0", [128, 1024], BF16), sbB("PT16_1", [128, 1024], BF16)]
            OA = sbB("OA", [64, 512], F32)
            RL = sbB("RL", [1, 512], F32)
            ATT = sbB("ATT", [128, 8 * 512], BF16)
            wld_o = load_wb(3)

            zq = p.ve(lambda e: e.memset(QBD[0][64:128, :], 0.0))
            zq = p.ve(lambda e: e.memset(QBD[1][0:64, :], 0.0))

            def qbd_load(h, q0, b):
                j = h // 2
                par = (h % 2) * 64
                l1 = p.ld(lambda e, j=j, par=par, q0=q0, b=b: e.dma_start(
                    out=QBD[b][par:par + 64, :], in_=qsp[j, par:par + 64, q0:q0 + 512]),
                    waits=[("TE", p.counts.get("_qbdfree%d" % b, 0)), ("VE", zq)])
                return l1

            qbl = {(0, 0): qbd_load(0, 0, 0)}
            sps = [PV, SPS1]
            for qc in range(8):
                q0 = qc * 512

                def emit_s_exp(h, kcp):
                    b = h % 2
                    par = (h % 2) * 64
                    sbk = kcp % 2
                    for kk in range(2):
                        kc = kcp * 2 + kk
                        p.te(lambda e, h=h, kc=kc, kk=kk, sbk=sbk, b=b: e.matmul(
                            sps[sbk][:, kk * 512:(kk + 1) * 512],
                            KT[:, (h // 2) * N_TOK + kc * 128:(h // 2) * N_TOK + (kc + 1) * 128],
                            QBD[b][:, :], start=True, stop=True),
                            waits=[("LD", qbl[(qc, h)]),
                                   ("SC", p.counts.get("_spsfree%d" % sbk, 0))],
                            tick=(kk == 1))
                    ts_ = p.c("TE")
                    se = p.sc(lambda e, sbk=sbk: e.activation(
                        PT16[sbk][:, :], sps[sbk][:, :], AF.Exp, bias=NEGB[:, 0:1], scale=0.125),
                        waits=[("TE", ts_), ("TE", p.counts.get("_pt16free%d" % sbk, 0))])
                    p.counts["_spsfree%d" % sbk] = se
                    return se

                def emit_av(h, kcp, se):
                    sbk = kcp % 2
                    ob = h % 2
                    for kk in range(2):
                        kc = kcp * 2 + kk
                        p.te(lambda e, h=h, kc=kc, kk=kk, sbk=sbk, ob=ob: e.matmul(
                            OPS[ob][:, :],
                            VTM[:, (kc * 16 + h) * 65:(kc * 16 + h) * 65 + 65],
                            PT16[sbk][:, kk * 512:(kk + 1) * 512],
                            start=(kc == 0), stop=(kc == 31)),
                            waits=[("SC", se), ("SC", p.counts.get("_opsfree%d" % ob, 0)),
                                   ("VE", p.counts.get("_opsfreev%d" % ob, 0))],
                            tick=(kk == 1))
                    p.counts["_pt16free%d" % sbk] = p.c("TE")

                def emit_norm(h, to_):
                    ob = h % 2
                    par = (h % 2) * 64
                    vr = p.ve(lambda e, ob=ob: e.reciprocal(RL[0:1, :], OPS[ob][64:65, :]),
                              waits=[("TE", to_), ("TE", p.counts.get("_rlfree", 0))])
                    so = p.sc(lambda e, ob=ob: e.activation(OA[0:64, :], OPS[ob][0:64, :], AF.Identity),
                              waits=[("TE", to_), ("VE", p.counts.get("_oafree", 0))])
                    p.counts["_opsfree%d" % ob] = so
                    tb2 = p.te(lambda e: e.matmul(PTR[0:64, :], ONESR[0:1, 0:64], RL[0:1, :],
                                                  start=True, stop=True),
                               waits=[("VE", vr), ("VE", p.counts.get(PTRKEY, 0))], tick=True)
                    p.counts["_rlfree"] = tb2
                    va = p.ve(lambda e, h=h, par=par: e.tensor_tensor(
                        ATT[par:par + 64, (h // 2) * 512:(h // 2) * 512 + 512],
                        OA[0:64, :], PTR[0:64, :], ALU.mult),
                        waits=[("TE", tb2), ("SC", so), ("TE", p.counts.get("_attfree", 0))])
                    p.counts[PTRKEY] = va
                    p.counts["_oafree"] = va
                    p.counts["_opsfreev%d" % ob] = va

                pend_norm = None
                for h in range(16):
                    b = h % 2
                    pend_av = None
                    for kcp in range(16):
                        se = emit_s_exp(h, kcp)
                        if pend_av is not None:
                            emit_av(h, pend_av[0], pend_av[1])
                        pend_av = (kcp, se)
                        if kcp == 0 and pend_norm is not None:
                            emit_norm(*pend_norm)
                            pend_norm = None
                    emit_av(h, pend_av[0], pend_av[1])
                    to_ = p.c("TE")
                    p.counts["_qbdfree%d" % b] = to_
                    if h + 1 < 16:
                        qbl[(qc, h + 1)] = qbd_load(h + 1, q0, (h + 1) % 2)
                    elif qc + 1 < 8:
                        qbl[(qc + 1, 0)] = qbd_load(0, (qc + 1) * 512, 0)
                    pend_norm = (h, to_)
                emit_norm(*pend_norm)
                att_ready = p.c("VE")
                for sub in range(4):
                    pw = [SPS1, PV][sub % 2]
                    for jc in range(8):
                        for nh in range(2):
                            p.te(lambda e, jc=jc, nh=nh, sub=sub, pw=pw: e.matmul(
                                pw[:, nh * 512:(nh + 1) * 512],
                                ATT[:, jc * 512 + sub * 128:jc * 512 + (sub + 1) * 128],
                                WB[:, jc * 1024 + nh * 512:jc * 1024 + (nh + 1) * 512],
                                start=(jc == 0), stop=(jc == 7)),
                                waits=[("VE", att_ready), ("LD", wld_o),
                                       ("SC", p.counts.get("_pwfree%d" % (sub % 2), 0)),
                                       ("SC", p.counts.get("_spsfree%d" % (1 - sub % 2), 0))],
                                tick=(jc == 7 and nh == 1))
                    tw = p.c("TE")
                    lx = p.ld(lambda e, q0=q0, sub=sub: e.dma_start(
                        out=X2TILE[:, :], in_=x2_d[q0 + sub * 128:q0 + sub * 128 + 128, :]),
                        waits=[("VE", p.counts.get("_x2resfree", 0))])
                    sy = p.sc(lambda e, pw=pw: e.activation(LNS[:, :], pw[:, :], AF.Identity),
                              waits=[("TE", tw), ("VE", p.counts.get("_lnsv", 0)),
                                     ("ST", p.counts.get("_lnst", 0))])
                    p.counts["_pwfree%d" % (sub % 2)] = sy
                    vy = p.ve(lambda e: e.tensor_tensor(LNS[:, :], LNS[:, :], BB[:, :], ALU.add),
                              waits=[("SC", sy), ("VE", bo_done)])
                    vy = p.ve(lambda e: e.tensor_tensor(LNS[:, :], LNS[:, :], X2TILE[:, :], ALU.add),
                              waits=[("VE", vy), ("LD", lx)])
                    p.counts["_x2resfree"] = vy
                    r2 = emit_ln_tile(p, LNS, LNS, LNS2, STAT[:, 16:24], CONST[:, 1:2],
                                      waits=[("VE", vy)])
                    p.counts["_lnsv"] = max(p.counts.get("_lnsv", 0), vy)
                    p.counts["_lnst"] = p.st(
                        lambda e, q0=q0, sub=sub: e.dma_start(
                            out=x3_d[q0 + sub * 128:q0 + sub * 128 + 128, :], in_=LNS[:, :]),
                        waits=[("SC", r2.get("SC", 0)), ("VE", r2.get("VE", 0))])
                p.counts["_attfree"] = p.c("TE")
                p.counts["_spsfree0"] = max(p.counts.get("_spsfree0", 0), p.counts.get("_pwfree1", 0))
                p.counts["_spsfree1"] = max(p.counts.get("_spsfree1", 0), p.counts.get("_pwfree0", 0))
            p.emit("sync", lambda e: e.nop(), waits=[("ST", p.c("ST"))])
            p.fence()
            with nc.Block() as blockB:
                p.flush(blockB)


def build_l1_attn():
    nc = bass.Bass("TRN2", target_bir_lowering=False)
    x2_d = nc.dram_tensor("x2", [N_TOK, D], F32, kind="ExternalInput").ap()
    wq_d = nc.dram_tensor("awqT", [D, D], F32, kind="ExternalInput").ap()
    wk_d = nc.dram_tensor("awkT", [D, D], F32, kind="ExternalInput").ap()
    wv_d = nc.dram_tensor("awvT", [D, D], F32, kind="ExternalInput").ap()
    wo_d = nc.dram_tensor("awoT", [D, D], F32, kind="ExternalInput").ap()
    bq_d = nc.dram_tensor("abqpt", [128, 8], F32, kind="ExternalInput").ap()
    bk_d = nc.dram_tensor("abkpt", [128, 8], F32, kind="ExternalInput").ap()
    bv_d = nc.dram_tensor("abvrow", [1, D], F32, kind="ExternalInput").ap()
    bo_d = nc.dram_tensor("aborow", [1, D], F32, kind="ExternalInput").ap()
    cos_d = nc.dram_tensor("cosFt", [128, N_TOK], F32, kind="ExternalInput").ap()
    sin_d = nc.dram_tensor("sinFt", [128, N_TOK], F32, kind="ExternalInput").ap()
    rot_d = nc.dram_tensor("rotM", [128, 128], F32, kind="ExternalInput").ap()
    ind_d = nc.dram_tensor("indM", [128, 2], F32, kind="ExternalInput").ap()
    ident_d = nc.dram_tensor("ident", [128, 128], F32, kind="ExternalInput").ap()
    ones_d = nc.dram_tensor("ones_row", [1, 128], F32, kind="ExternalInput").ap()
    x3_d = nc.dram_tensor("x3", [N_TOK, D], F32, kind="ExternalOutput").ap()
    qsp = nc.dram_tensor("qspill", [8, 128, N_TOK], BF16)
    wscr = nc.dram_tensor("wscr", [4, 8, 128, D], BF16)
    io = dict(x2_d=x2_d, wq_d=wq_d, wk_d=wk_d, wv_d=wv_d, wo_d=wo_d, bq_d=bq_d, bk_d=bk_d, bv_d=bv_d, bo_d=bo_d, cos_d=cos_d, sin_d=sin_d, rot_d=rot_d, ind_d=ind_d, ident_d=ident_d, ones_d=ones_d, x3_d=x3_d, qsp=qsp, wscr=wscr)
    with ExitStack() as _st:
        p = Prog(nc, _st)
        emit_l1_attn(nc, p, io)
    return nc


def run_l1_attn(x2_full, wq, bq, wk, bk, wv, bv, wo, bo):
    consts = make_consts()
    nc = build_l1_attn()
    rotm = np.zeros((128, 128), np.float32)
    for base in (0, 64):
        for dd in range(32):
            rotm[base + 32 + dd, base + dd] = -1.0
            rotm[base + dd, base + 32 + dd] = 1.0
    indm = np.zeros((128, 2), np.float32)
    indm[0:64, 0] = 1.0
    indm[64:128, 1] = 1.0
    in_maps = []
    for c in range(8):
        in_maps.append({
            "x2": _f(x2_full[c]),
            "awqT": _f(np.asarray(wq).T), "awkT": _f(np.asarray(wk).T),
            "awvT": _f(np.asarray(wv).T), "awoT": _f(np.asarray(wo).T),
            "abqpt": per_part(bq), "abkpt": per_part(bk),
            "abvrow": _f(bv).reshape(1, D), "aborow": _f(bo).reshape(1, D),
            "cosFt": consts["cosF"], "sinFt": consts["sinF"],
            "rotM": rotm, "indM": indm,
            "ident": consts["ident"], "ones_row": consts["ones_row"],
        })
    res = run_bass_kernel_spmd(nc, in_maps, core_ids=list(range(8)))
    return np.stack([np.asarray(r["x3"]) for r in res.results])


def emit_l0_attn(nc, p, io):
    x_d = io["x_d"]
    wq_d = io["wq_d"]
    wk_d = io["wk_d"]
    wv_d = io["wv_d"]
    wo_d = io["wo_d"]
    prj_d = io["prj_d"]
    g1_d = io["g1_d"]
    b1_d = io["b1_d"]
    ident_d = io["ident_d"]
    identb_d = io["identb_d"]
    ones_d = io["ones_d"]
    out1_d = io["out1_d"]
    qsp = io["qsp"]
    wscr = io["wscr"]
    ccin = io["ccin"]
    ccout = io["ccout"]

    DN = 64.0 ** -0.25
    RATIO = 0.125
    EPSK = 1e-6
    REPS = RATIO * EPSK
    REPSN = RATIO * EPSK * N_TOK


    stack = ExitStack()
    with stack:
        sb = lambda name, shape, dt=F32: stack.enter_context(nc.sbuf_tensor(name, shape, dt, allow_name_mangling=True))
        ps = lambda name, shape, dt=F32: stack.enter_context(nc.psum_tensor(name, shape, dt, allow_name_mangling=True))
        KT = sb("KT", [128, 8 * N_TOK], BF16)
        VTM = sb("VTM", [128, 32 * 16 * 65], BF16)
        WB = sb("WB", [128, 8 * 1024], BF16)
        STG = sb("STG", [128, 1024], F32)
        BSC = sb("BSC", [128, 1024], BF16)
        XTILE = sb("XTILE", [128, 1024], F32)
        X0T = sb("X0T", [128, 8 * 512], BF16)
        QKSTG = [sb("QKSTG0", [128, 512], BF16), sb("QKSTG1", [128, 512], BF16)]
        RHS1 = sb("RHS1", [128, 65], BF16)
        RHS2 = sb("RHS2", [128, 65], BF16)
        PRJ = sb("PRJ", [64, 64], F32)
        SQW = [sb("SQW0", [64, 128], BF16), sb("SQW1", [64, 128], BF16)]
        PHK = [sb("PHK0", [128, 65], BF16), sb("PHK1", [128, 65], BF16)]
        PHQF = [sb("PHQF0", [128, 64], F32), sb("PHQF1", [128, 64], F32)]
        PHQB = [sb("PHQB0", [128, 64], BF16), sb("PHQB1", [128, 64], BF16)]
        QFT = [sb("QFT0", [64, 128], BF16), sb("QFT1", [64, 128], BF16)]
        QTL = [sb("QTL0", [128, 8 * 128], BF16), sb("QTL1", [128, 8 * 128], BF16)]
        KVS = sb("KVS", [65, 16 * 65], F32)
        KVM = sb("KVM", [64, 16 * 65], BF16)
        TMPK = sb("TMPK", [64, 64], F32)
        TMPK2 = sb("TMPK2", [64, 64], F32)
        RMX = sb("RMX", [128, 4], F32)
        MXS = sb("MXS", [1, 8], F32)
        MXLB = sb("MXLB", [128, 1], F32)
        SPB = sb("SPB", [128, 1], F32)
        BIA = sb("BIA", [128, 2], F32)
        RD = sb("RD", [128, 2], F32)
        ATT = sb("ATT", [128, 16 * 64], BF16)
        ATF = sb("ATF", [128, 8 * 128], BF16)
        CCS = sb("CCS", [2, 1024], F32)
        G1B = sb("G1B", [128, 1024], F32)
        B1B = sb("B1B", [128, 1024], F32)
        ONESB = sb("ONESB", [128, 128], F32)
        LNS = sb("LNS", [128, 1024], F32)
        STAT = sb("STAT", [128, 32], F32)
        CONST = sb("CONST", [128, 4], F32)
        IDENT = sb("IDENT", [128, 128], F32)
        IDB = sb("IDB", [128, 128], BF16)
        ONESR = sb("ONESR", [1, 128], F32)
        PTR = ps("PTR", [128, 512])
        PQK = ps("PQK", [128, 512])
        PV = ps("PV", [128, 1024])
        DD = [ps("DD0", [128, 512]), ps("DD1", [128, 512])]
        KVP = ps("KVP", [65, 512])
        PTB = ps("PTB", [128, 128], BF16)

        PTRKEY = "_ptfree%d" % id(PTR)
        p.ve(lambda e: e.memset(CONST[:, 0:1], 1e-6))
        vtmv = VTM[:, :].rearrange("p (g c) -> p g c", c=65)
        p.ve(lambda e: e.memset(vtmv[:, :, 64:65], 1.0))
        p.ve(lambda e: e.memset(PHK[0][:, 64:65], 1.0))
        p.ve(lambda e: e.memset(PHK[1][:, 64:65], 1.0))
        p.ve(lambda e: e.memset(RMX[:, 0:1], -1e30))
        p.ve(lambda e: e.memset(CCS[:, :], -1e30))
        p.ve(lambda e: e.memset(ONESB[:, :], 1.0))
        p.ve(lambda e: e.memset(RHS1[:, :], 0.0))
        p.ve(lambda e: e.memset(RHS2[:, 0:64], 0.0))
        p.ve(lambda e: e.memset(RHS2[:, 64:65], -0.0625))
        mset = p.c("VE")
        for d, dst in [(ident_d, IDENT), (identb_d, IDB), (ones_d, ONESR)]:
            p.ld(lambda e, d=d, dst=dst: e.dma_start(out=dst[:, :], in_=d))
        lpr = p.ld(lambda e: e.dma_start(out=PRJ[:, :], in_=prj_d))
        vq = p.ve(lambda e: e.tensor_scalar(PRJ[:, :], PRJ[:, :], DN, None, ALU.mult),
                  waits=[("LD", lpr), ("VE", mset)])
        vq = p.ve(lambda e: e.tensor_copy(RHS1[0:64, 0:64], PRJ[:, :]), waits=[("VE", vq)])
        vq = p.ve(lambda e: e.tensor_copy(RHS1[64:128, 0:64], PRJ[:, :]), waits=[("VE", vq)])
        rhs_ready = vq
        lg = p.ld(lambda e: e.dma_start(out=LNS[0:1, 0:1024], in_=g1_d))
        bcast_row(p, PV, G1B, LNS[0:1, 0:1024], ONESR[0:1, :], 1024, waits=[("LD", lg)])
        lg = p.ld(lambda e: e.dma_start(out=LNS[0:1, 0:1024], in_=b1_d),
                  waits=[("TE", p.c("TE"))])
        bcast_row(p, PV, B1B, LNS[0:1, 0:1024], ONESR[0:1, :], 1024, waits=[("LD", lg)])
        p.counts["_pvfree"] = p.c("VE")

        STGB = sb("STGB", [128, 1024], F32)
        BSCB = sb("BSCB", [128, 1024], BF16)
        jobs = [(wd[c * 128:(c + 1) * 128, :], wscr[wi, c, :, :], 1024)
                for wi, wd in enumerate([wq_d, wk_d, wv_d, wo_d]) for c in range(8)]
        scr_ready = emit_wconv(nc, p, [STG, STGB], [BSC, BSCB], jobs, "a")

        def load_wb(wi):
            return p.ld(lambda e, wi=wi: e.dma_start(
                out=WB[:, :].rearrange("p (c n) -> p c n", n=1024),
                in_=wscr[wi].rearrange("c p n -> p c n")),
                waits=[("GD", scr_ready), ("TE", p.counts.get("_wbused", 0))])

        X0TV = X0T[:, :].rearrange("p (c t) -> p c t", t=512)
        for ch in range(8):
            t0 = ch * 512
            for sub in range(4):
                lx = p.ld(lambda e, t0=t0, sub=sub: e.dma_start(
                    out=XTILE[:, :], in_=x_d[t0 + sub * 128:t0 + sub * 128 + 128, :]),
                    waits=[("TE", p.counts.get("_xtfree", 0))])
                emit_transpose_to_fm(
                    p, lambda hh, sub=sub: X0TV[:, :, sub * 128:sub * 128 + 128][:, hh * 4:hh * 4 + 4, :],
                    XTILE[:, :], PTR, IDENT[:, :], waits=[("LD", lx)])
                p.counts["_xtfree"] = p.c("TE")
            xt_ready = p.c("VE")
            for which in (0, 1):
                wld = load_wb(which)
                for j in range(8):
                    db = j % 2
                    for c in range(8):
                        p.te(lambda e, j=j, c=c, db=db: e.matmul(
                            DD[db][:, :], WB[:, c * 1024 + j * 128:c * 1024 + (j + 1) * 128],
                            X0TV[:, c, :], start=(c == 0), stop=(c == 7)),
                            waits=[("LD", wld), ("VE", xt_ready),
                                   ("SC", p.counts.get("_pqkf%d" % db, 0))],
                            tick=(c == 7))
                    tq = p.c("TE")
                    if which == 1:
                        sev = p.sc(lambda e, j=j, t0=t0, db=db: e.activation(
                            KT[:, j * N_TOK + t0:j * N_TOK + t0 + 512], DD[db][:, :], AF.Identity),
                            waits=[("TE", tq)])
                    else:
                        sev = p.sc(lambda e, db=db: e.activation(QKSTG[db][:, :], DD[db][:, :], AF.Identity),
                                   waits=[("TE", tq), ("ST", p.counts.get("_qkstgfree%d" % db, 0))])
                        p.counts["_qkstgfree%d" % db] = p.st(
                            lambda e, j=j, t0=t0, db=db: e.dma_start(
                                out=qsp[j, :, t0:t0 + 512], in_=QKSTG[db][:, :]),
                            waits=[("SC", sev)])
                    p.counts["_pqkf%d" % db] = sev
                p.counts["_wbused"] = p.c("TE")
            wld = load_wb(2)
            for sub in range(4):
                for c in range(8):
                    for nh in range(2):
                        p.te(lambda e, sub=sub, c=c, nh=nh: e.matmul(
                            PV[:, nh * 512:(nh + 1) * 512],
                            X0TV[:, c, sub * 128:sub * 128 + 128],
                            WB[:, c * 1024 + nh * 512:c * 1024 + (nh + 1) * 512],
                            start=(c == 0), stop=(c == 7)),
                            waits=[("LD", wld), ("VE", xt_ready),
                                   ("VE", p.counts.get("_pvfree", 0))],
                            tick=(c == 7 and nh == 1))
                tv = p.c("TE")
                tile = ch * 4 + sub
                vv = p.ve(lambda e, tile=tile: e.tensor_copy(
                    vtmv[:, tile * 16:(tile + 1) * 16, 0:64],
                    PV[:, :].rearrange("p (h c) -> p h c", c=64)),
                    waits=[("TE", tv), ("VE", mset)])
                p.counts["_pvfree"] = vv
            p.counts["_wbused"] = p.c("TE")
        p.fence()
        for h in range(16):
            par = (h % 2) * 64
            j = h // 2
            for tile in range(32):
                db = tile % 2
                tm = p.te(lambda e, par=par, j=j, tile=tile, db=db: e.matmul(
                    DD[db][:, 0:65],
                    KT[par:par + 64, j * N_TOK + tile * 128:j * N_TOK + (tile + 1) * 128],
                    RHS1[par:par + 64, :], start=True, stop=True),
                    waits=[("VE", rhs_ready), ("VE", p.counts.get("_ddv%d" % db, 0))],
                    tick=True)
                vm = p.ve(lambda e, db=db: e.tensor_reduce(STAT[:, 0:1], DD[db][:, 0:64],
                                                           axis=AX.X, op=ALU.max),
                          waits=[("TE", tm), ("VE", p.counts.get("_statmx", 0))])
                p.counts["_ddv%d" % db] = vm
                vm = p.ve(lambda e: e.tensor_tensor(RMX[:, 0:1], RMX[:, 0:1], STAT[:, 0:1], ALU.max),
                          waits=[("VE", vm)])
                p.counts["_statmx"] = vm
        tmx = p.te(lambda e: e.transpose(PTR[0:1, 0:128], RMX[:, 0:1], IDENT[:, :]),
                   waits=[("VE", p.c("VE")), ("VE", p.counts.get(PTRKEY, 0))], tick=True)
        vm = p.ve(lambda e: e.tensor_copy(LNS[0:1, 0:128], PTR[0:1, 0:128]), waits=[("TE", tmx)])
        p.counts[PTRKEY] = vm
        vm = p.ve(lambda e: e.tensor_reduce(MXS[0:1, 0:1], LNS[0:1, 0:128], axis=AX.X, op=ALU.max),
                  waits=[("VE", vm)])
        vm = p.ve(lambda e: e.tensor_copy(CCS[0:1, 0:1], MXS[0:1, 0:1]), waits=[("VE", vm)])
        gcc = p.gld(lambda e: e.dma_start(out=ccin[:, :], in_=CCS[:, :]), waits=[("VE", vm)])
        p.emit("gpsimd", lambda e: e.collective_compute(
            "AllReduce", mybir.AluOpType.max, replica_groups=[list(range(8))],
            ins=[ccin.ap()], outs=[ccout.ap()]), waits=[("GD", gcc)], incs=[("CC", 1)])
        gcc = p.gld(lambda e: e.dma_start(out=MXS[0:1, 1:2], in_=ccout[0:1, 0:1]),
                    waits=[("CC", 1)])
        tmb = p.te(lambda e: e.matmul(PQK[:, 0:1], ONESR[0:1, :], MXS[0:1, 0:1],
                                      start=True, stop=True),
                   waits=[("VE", vm), ("SC", p.counts.get("_pqkfree", 0))], tick=True)
        vmb = p.ve(lambda e: e.tensor_copy(MXLB[:, :], PQK[:, 0:1]), waits=[("TE", tmb)])
        def emit_ddk(h, tile, db):
            par = (h % 2) * 64
            j = h // 2
            ssq = p.sc(lambda e, par=par, j=j, tile=tile, db=db: e.activation(
                SQW[db][0:64, :], KT[par:par + 64, j * N_TOK + tile * 128:j * N_TOK + (tile + 1) * 128],
                AF.Square),
                waits=[("TE", p.counts.get("_sqwused%d" % db, 0))])
            p.te(lambda e, par=par, j=j, tile=tile, db=db: e.matmul(
                DD[db][:, 0:65],
                KT[par:par + 64, j * N_TOK + tile * 128:j * N_TOK + (tile + 1) * 128],
                RHS1[par:par + 64, :], start=True, stop=False),
                waits=[("VE", p.counts.get("_ddv%d" % db, 0)),
                       ("SC", p.counts.get("_ddexp%d" % db, 0))])
            tm = p.te(lambda e, db=db: e.matmul(
                DD[db][:, 0:65], SQW[db][0:64, :], RHS2[0:64, :], start=False, stop=True),
                waits=[("SC", ssq)], tick=True)
            p.counts["_sqwused%d" % db] = tm
            vb = p.ve(lambda e, db=db: e.tensor_tensor(
                BIA[:, db:db + 1], DD[db][:, 64:65], MXLB[:, :], ALU.subtract),
                waits=[("TE", tm), ("VE", vmb), ("SC", p.counts.get("_biused%d" % db, 0))])
            p.counts["_ddv%d" % db] = vb
            sx = p.sc(lambda e, db=db: e.activation(
                PHK[db][:, 0:64], DD[db][:, 0:64], AF.Exp, bias=BIA[:, db:db + 1]),
                waits=[("VE", vb), ("TE", p.counts.get("_phkused%d" % db, 0))])
            p.counts["_ddexp%d" % db] = sx
            p.counts["_biused%d" % db] = sx
            return sx

        for h in range(16):
            pend = None
            for tile in range(32):
                db = tile % 2
                sx = emit_ddk(h, tile, db)
                if pend is not None:
                    ptile, psx = pend
                    tkv = p.te(lambda e, h=h, ptile=ptile, pdb=ptile % 2: e.matmul(
                        KVP[:, 0:65], PHK[pdb][:, :],
                        VTM[:, (ptile * 16 + h) * 65:(ptile * 16 + h) * 65 + 65],
                        start=(ptile == 0), stop=False),
                        waits=[("SC", psx), ("SC", p.counts.get("_kvpfree", 0))], tick=True)
                    p.counts["_phkused%d" % (ptile % 2)] = tkv
                pend = (tile, sx)
            tkv = p.te(lambda e, h=h: e.matmul(
                KVP[:, 0:65], PHK[31 % 2][:, :], VTM[:, (31 * 16 + h) * 65:(31 * 16 + h) * 65 + 65],
                start=False, stop=True),
                waits=[("SC", pend[1])], tick=True)
            p.counts["_phkused%d" % (31 % 2)] = tkv
            skv = p.sc(lambda e, h=h: e.activation(KVS[0:65, h * 65:(h + 1) * 65], KVP[:, 0:65],
                                                   AF.Identity),
                       waits=[("TE", p.c("TE"))])
            p.counts["_kvpfree"] = skv
        kvs_ready = p.c("SC")
        vm = p.ve(lambda e: e.tensor_tensor(MXS[0:1, 2:3], MXS[0:1, 0:1], MXS[0:1, 1:2], ALU.subtract),
                  waits=[("GD", gcc), ("SC", kvs_ready)])
        sm = p.sc(lambda e: e.activation(MXS[0:1, 3:4], MXS[0:1, 2:3], AF.Exp), waits=[("VE", vm)])
        vm = p.ve(lambda e: e.tensor_scalar(MXS[0:1, 4:5], MXS[0:1, 3:4], RATIO, None, ALU.mult),
                  waits=[("SC", sm)])
        tsp = p.te(lambda e: e.matmul(PQK[:, 0:1], ONESR[0:1, :], MXS[0:1, 4:5],
                                      start=True, stop=True),
                   waits=[("VE", vm)], tick=True)
        vsp = p.ve(lambda e: e.tensor_copy(SPB[:, :], PQK[:, 0:1]), waits=[("TE", tsp)])
        for h in range(16):
            tb = p.te(lambda e, h=h: e.matmul(
                DD[0][:, 0:64][0:64, :], ONESB[64:65, 0:64], KVS[64:65, h * 65:h * 65 + 64],
                start=True, stop=True),
                waits=[("SC", kvs_ready), ("VE", p.counts.get("_ddv0", 0))], tick=True)
            v1 = p.ve(lambda e, h=h: e.tensor_scalar(
                TMPK[:, :], KVS[0:64, h * 65:h * 65 + 64], SPB[0:64, 0:1], None, ALU.mult),
                waits=[("VE", vsp), ("SC", kvs_ready), ("VE", p.counts.get("_tmpkv", 0))])
            v2 = p.ve(lambda e: e.tensor_scalar(
                TMPK2[:, :], DD[0][0:64, 0:64], REPS, None, ALU.mult),
                waits=[("TE", tb), ("VE", v1)])
            p.counts["_ddv0"] = v2
            v3 = p.ve(lambda e, h=h: e.tensor_tensor(
                KVM[0:64, h * 65:h * 65 + 64], TMPK[:, :], TMPK2[:, :], ALU.add),
                waits=[("VE", v2)])
            p.counts["_tmpkv"] = v3
            v4 = p.ve(lambda e, h=h: e.tensor_scalar(
                KVM[0:64, h * 65 + 64:h * 65 + 65], KVS[0:64, h * 65 + 64:h * 65 + 65],
                SPB[0:64, 0:1], REPSN, ALU.mult, ALU.add),
                waits=[("VE", vsp), ("SC", kvs_ready)])
        kvm_ready = p.c("VE")
        p.fence()
        wld_o = load_wb(3)
        for tile in range(32):
            t0 = tile * 128
            qb_ = tile % 2
            QTLB = QTL[qb_]
            lq = p.ld(lambda e, t0=t0, QTLB=QTLB: e.dma_start(
                out=QTLB[:, :].rearrange("p (j t) -> p j t", t=128),
                in_=qsp[:, :, t0:t0 + 128].rearrange("j p t -> p j t")),
                waits=[("TE", p.counts.get("_qtlfree%d" % qb_, 0)), ("SC", p.counts.get("_qtlfree_sc%d" % qb_, 0))])
            lx = p.ld(lambda e, t0=t0: e.dma_start(
                out=XTILE[:, :], in_=x_d[t0:t0 + 128, :]),
                waits=[("VE", p.counts.get("_xtilev", 0))])
            def q_stage1(h, lq):
                par = (h % 2) * 64
                jj = h // 2
                db = h % 2
                ssq = p.sc(lambda e, par=par, jj=jj, db=db, QTLB=QTLB: e.activation(
                    SQW[db][0:64, :], QTLB[par:par + 64, jj * 128:(jj + 1) * 128], AF.Square),
                    waits=[("LD", lq), ("TE", p.counts.get("_sqwused%d" % db, 0))])
                p.te(lambda e, par=par, jj=jj, db=db, QTLB=QTLB: e.matmul(
                    DD[db][:, 0:65], QTLB[par:par + 64, jj * 128:(jj + 1) * 128],
                    RHS1[par:par + 64, :], start=True, stop=False),
                    waits=[("LD", lq), ("VE", p.counts.get("_ddv%d" % db, 0)),
                           ("SC", p.counts.get("_ddexp%d" % db, 0))])
                tm = p.te(lambda e, db=db: e.matmul(
                    DD[db][:, 0:65], SQW[db][0:64, :], RHS2[0:64, :], start=False, stop=True),
                    waits=[("SC", ssq)], tick=True)
                p.counts["_sqwused%d" % db] = tm
                vmx = p.ve(lambda e, db=db: e.tensor_reduce(
                    STAT[:, 1 + db:2 + db], DD[db][:, 0:64], axis=AX.X, op=ALU.max),
                    waits=[("TE", tm), ("SC", p.counts.get("_statq%d" % db, 0))])
                vb = p.ve(lambda e, db=db: e.tensor_tensor(
                    BIA[:, db:db + 1], DD[db][:, 64:65], STAT[:, 1 + db:2 + db], ALU.subtract),
                    waits=[("VE", vmx), ("SC", p.counts.get("_biq%d" % db, 0))])
                p.counts["_ddv%d" % db] = vb
                sx = p.sc(lambda e, db=db: e.activation(
                    PHQF[db][:, :], DD[db][:, 0:64], AF.Exp, bias=BIA[:, db:db + 1]),
                    waits=[("VE", vb), ("VE", p.counts.get("_phqused%d" % db, 0))])
                p.counts["_statq%d" % db] = sx
                p.counts["_biq%d" % db] = sx
                p.counts["_ddexp%d" % db] = sx
                vph = p.ve(lambda e, db=db: e.tensor_scalar(
                    PHQB[db][:, :], PHQF[db][:, :], EPSK, None, ALU.add),
                    waits=[("SC", sx), ("TE", p.counts.get("_phqbused%d" % db, 0))])
                p.counts["_phqused%d" % db] = vph
                return vph

            def q_stage2(h, vph):
                db = h % 2
                tph = p.te(lambda e, db=db: e.transpose(PTB[0:64, :], PHQB[db][:, :], IDB[:, :]),
                           waits=[("VE", vph), ("VE", p.counts.get("_ptbv", 0))], tick=True)
                p.counts["_phqbused%d" % db] = tph
                vqf = p.ve(lambda e, db=db: e.tensor_copy(QFT[db][0:64, :], PTB[0:64, :]),
                           waits=[("TE", tph), ("TE", p.counts.get("_qftused%d" % db, 0))])
                p.counts["_ptbv"] = vqf
                tnm = p.te(lambda e, h=h, db=db: e.matmul(
                    DD[db][:, 0:65], QFT[db][0:64, :], KVM[0:64, h * 65:(h + 1) * 65],
                    start=True, stop=True),
                    waits=[("VE", vqf), ("VE", p.counts.get("_ddv%d" % db, 0))], tick=True)
                p.counts["_qftused%d" % db] = tnm
                vrd = p.ve(lambda e, db=db: e.reciprocal(RD[:, db:db + 1], DD[db][:, 64:65]),
                           waits=[("TE", tnm)])
                vat = p.ve(lambda e, h=h, db=db: e.tensor_scalar(
                    ATT[:, h * 64:(h + 1) * 64], DD[db][:, 0:64], RD[:, db:db + 1], None, ALU.mult),
                    waits=[("VE", vrd), ("TE", p.counts.get("_attused", 0))])
                p.counts["_ddv%d" % db] = vat

            pend_q = None
            for h in range(16):
                s1 = q_stage1(h, lq)
                if pend_q is not None:
                    q_stage2(*pend_q)
                pend_q = (h, s1)
            q_stage2(*pend_q)
            att_ready = p.c("VE")
            for jc in range(8):
                tat = p.te(lambda e, jc=jc: e.transpose(
                    PTB[:, :], ATT[:, jc * 128:(jc + 1) * 128], IDB[:, :]),
                    waits=[("VE", att_ready), ("VE", p.counts.get("_ptbv", 0))], tick=True)
                vaf = p.ve(lambda e, jc=jc: e.tensor_copy(ATF[:, jc * 128:(jc + 1) * 128], PTB[:, :]),
                           waits=[("TE", tat), ("TE", p.counts.get("_atfused", 0))])
                p.counts["_ptbv"] = vaf
            atf_ready = p.c("VE")
            for jc in range(8):
                for nh in range(2):
                    p.te(lambda e, jc=jc, nh=nh: e.matmul(
                        PV[:, nh * 512:(nh + 1) * 512],
                        ATF[:, jc * 128:(jc + 1) * 128],
                        WB[:, jc * 1024 + nh * 512:jc * 1024 + (nh + 1) * 512],
                        start=(jc == 0), stop=(jc == 7)),
                        waits=[("VE", atf_ready), ("LD", wld_o),
                               ("SC", p.counts.get("_pwfree", 0))],
                        tick=(jc == 7 and nh == 1))
            tw = p.c("TE")
            p.counts["_attused"] = tw
            p.counts["_atfused"] = tw
            sy = p.sc(lambda e: e.activation(LNS[:, :], PV[:, :], AF.Identity),
                      waits=[("TE", tw), ("VE", p.counts.get("_lnsv", 0)),
                             ("ST", p.counts.get("_lnst", 0))])
            p.counts["_pwfree"] = sy
            p.counts["_qtlfree_sc%d" % qb_] = sy
            vy = p.ve(lambda e: e.tensor_tensor(LNS[:, :], LNS[:, :], XTILE[:, :], ALU.add),
                      waits=[("SC", sy), ("LD", lx)])
            p.counts["_xtilev"] = vy
            r2 = emit_ln_tile(p, LNS, LNS, STG, STAT[:, 16:24], CONST[:, 0:1],
                              g_ap=G1B[:, :], b_ap=B1B[:, :], waits=[("VE", vy)])
            p.counts["_lnsv"] = max(p.counts.get("_lnsv", 0), r2.get("VE", 0), vy)
            p.counts["_qtlfree%d" % qb_] = p.c("TE")
            p.counts["_lnst"] = p.stv(
                lambda e, t0=t0: e.dma_start(out=out1_d[t0:t0 + 128, :], in_=LNS[:, :]),
                waits=[("SC", r2.get("SC", 0)), ("VE", r2.get("VE", 0))])
        p.emit("sync", lambda e: e.nop(), waits=[("ST", p.c("ST"))])
        p.fence()
        with nc.Block() as block:
            p.flush(block)


def build_l0_attn():
    nc = bass.Bass("TRN2", target_bir_lowering=False)
    x_d = nc.dram_tensor("x", [N_TOK, D], F32, kind="ExternalInput").ap()
    wq_d = nc.dram_tensor("pWq", [D, D], F32, kind="ExternalInput").ap()
    wk_d = nc.dram_tensor("pWk", [D, D], F32, kind="ExternalInput").ap()
    wv_d = nc.dram_tensor("pWv", [D, D], F32, kind="ExternalInput").ap()
    wo_d = nc.dram_tensor("pWo", [D, D], F32, kind="ExternalInput").ap()
    prj_d = nc.dram_tensor("projT", [64, 64], F32, kind="ExternalInput").ap()
    g1_d = nc.dram_tensor("g1row", [1, D], F32, kind="ExternalInput").ap()
    b1_d = nc.dram_tensor("b1row", [1, D], F32, kind="ExternalInput").ap()
    ident_d = nc.dram_tensor("ident", [128, 128], F32, kind="ExternalInput").ap()
    identb_d = nc.dram_tensor("identb", [128, 128], BF16, kind="ExternalInput").ap()
    ones_d = nc.dram_tensor("ones_row", [1, 128], F32, kind="ExternalInput").ap()
    out1_d = nc.dram_tensor("out1", [N_TOK, D], F32, kind="ExternalOutput").ap()
    qsp = nc.dram_tensor("qspill", [8, 128, N_TOK], BF16)
    wscr = nc.dram_tensor("wscr", [4, 8, 128, D], BF16)
    ccin = nc.dram_tensor("ccin", [2, 1024], F32)
    ccout = nc.dram_tensor("ccout", [2, 1024], F32)
    io = dict(x_d=x_d, wq_d=wq_d, wk_d=wk_d, wv_d=wv_d, wo_d=wo_d, prj_d=prj_d, g1_d=g1_d, b1_d=b1_d, ident_d=ident_d, identb_d=identb_d, ones_d=ones_d, out1_d=out1_d, qsp=qsp, wscr=wscr, ccin=ccin, ccout=ccout)
    with ExitStack() as _st:
        p = Prog(nc, _st)
        emit_l0_attn(nc, p, io)
    return nc


def run_l0_attn(x_full, Wq, Wk, Wv, Wo, proj, g1, b1):
    consts = make_consts()
    nc = build_l0_attn()
    in_maps = []
    for c in range(8):
        in_maps.append({
            "x": _f(x_full[c]),
            "pWq": _f(np.asarray(Wq).reshape(D, D)), "pWk": _f(np.asarray(Wk).reshape(D, D)),
            "pWv": _f(np.asarray(Wv).reshape(D, D)), "pWo": _f(np.asarray(Wo).reshape(D, D)),
            "projT": _f(np.asarray(proj).T),
            "g1row": _f(g1).reshape(1, D), "b1row": _f(b1).reshape(1, D),
            "ident": consts["ident"],
            "identb": consts["ident"].astype(ml_dtypes.bfloat16),
            "ones_row": consts["ones_row"],
        })
    res = run_bass_kernel_spmd(nc, in_maps, core_ids=list(range(8)))
    return np.stack([np.asarray(r["out1"]) for r in res.results])


DEBUG_TAPS = False
_PROG_CACHE = {}


def _prog(name, builder):
    if name not in _PROG_CACHE:
        _PROG_CACHE[name] = builder()
    return _PROG_CACHE[name]


def build_fused():
    nc = bass.Bass("TRN2", target_bir_lowering=False)
    T = lambda n, shp, dt=F32, kind=None, space=None: (
        nc.dram_tensor(n, shp, dt, kind=kind, addr_space=space).ap() if kind and space else
        nc.dram_tensor(n, shp, dt, kind=kind).ap() if kind else
        nc.dram_tensor(n, shp, dt, addr_space=space).ap() if space else
        nc.dram_tensor(n, shp, dt).ap())
    EI = "ExternalInput"
    x_d = T("x", [N_TOK, D], kind=EI)
    ident_d = T("ident", [128, 128], kind=EI)
    identb_d = T("identb", [128, 128], BF16, kind=EI)
    ones_d = T("ones_row", [1, 128], kind=EI)
    _ik = "ExternalOutput" if DEBUG_TAPS else None
    out1_d = T("i_out1", [N_TOK, D], kind=_ik)
    x1_d = T("i_x1", [N_TOK, D], kind=_ik)
    x2_d = T("i_x2", [N_TOK, D], kind=_ik)
    x3_d = T("i_x3", [N_TOK, D], kind=_ik)
    out_d = T("out", [N_TOK, D], kind="ExternalOutput")

    io1 = dict(
        x_d=x_d, ident_d=ident_d, identb_d=identb_d, ones_d=ones_d, out1_d=out1_d,
        wq_d=T("pWq", [D, D], kind=EI), wk_d=T("pWk", [D, D], kind=EI),
        wv_d=T("pWv", [D, D], kind=EI), wo_d=T("pWo", [D, D], kind=EI),
        prj_d=T("projT", [64, 64], kind=EI),
        g1_d=T("g1row", [1, D], kind=EI), b1_d=T("b1row", [1, D], kind=EI),
        qsp=nc.dram_tensor("l0qspill", [8, 128, N_TOK], BF16),
        wscr=nc.dram_tensor("l0wscr", [4, 8, 128, D], BF16),
        ccin=nc.dram_tensor("ccin", [2, 1024], F32),
        ccout=nc.dram_tensor("ccout", [2, 1024], F32),
    )
    io2 = dict(
        out1_d=out1_d, x_d=x_d, ident_d=ident_d, ones_d=ones_d, x1_d=x1_d,
        w1_d=T("W1", [D, 4096], kind=EI), w2_d=T("W2", [4096, D], kind=EI),
        w1scr_d=nc.dram_tensor("w1scr", [32, 8, 128, 128], BF16).ap(),
        b1_d=T("b1pt", [128, 32], kind=EI), b2_d=T("b2row", [1, D], kind=EI),
        g2_d=T("g2row", [1, D], kind=EI), bb2_d=T("bb2row", [1, D], kind=EI),
    )
    io3 = dict(
        x1_d=x1_d, ident_d=ident_d, ones_d=ones_d, x2_d=x2_d,
        w1_d=T("mw1", [8, D, 512], kind=EI), w2_d=T("mw2", [8, 512, D], kind=EI),
        b1_d=T("mb1pt", [128, 32], kind=EI), b2_d=T("mb2", [8, D], kind=EI),
        cen_d=T("cen", [8, D], kind=EI), crt_d=T("crt", [D, 16], kind=EI),
    )
    io4 = dict(
        x2_d=x2_d, ident_d=ident_d, ones_d=ones_d, x3_d=x3_d,
        wq_d=T("awqT", [D, D], kind=EI), wk_d=T("awkT", [D, D], kind=EI),
        wv_d=T("awvT", [D, D], kind=EI), wo_d=T("awoT", [D, D], kind=EI),
        bq_d=T("abqpt", [128, 8], kind=EI), bk_d=T("abkpt", [128, 8], kind=EI),
        bv_d=T("abvrow", [1, D], kind=EI), bo_d=T("aborow", [1, D], kind=EI),
        cos_d=T("cosFt", [128, N_TOK], kind=EI), sin_d=T("sinFt", [128, N_TOK], kind=EI),
        rot_d=T("rotM", [128, 128], kind=EI), ind_d=T("indM", [128, 2], kind=EI),
        qsp=nc.dram_tensor("l1qspill", [8, 128, N_TOK], BF16),
        wscr=nc.dram_tensor("l1wscr", [4, 8, 128, D], BF16),
    )
    NE = 9
    io5 = dict(
        x3_d=x3_d, ident_d=ident_d, ones_d=ones_d, out_d=out_d,
        w1_d=T("fw1", [NE, D, 1408], kind=EI), w3_d=T("fw3", [NE, D, 1408], kind=EI),
        w2_d=T("fw2", [NE, 1408, D], kind=EI),
        b1_d=T("fb1pt", [128, NE * 11], kind=EI), b3_d=T("fb3pt", [128, NE * 11], kind=EI),
        b2_d=T("fb2p", [128, 3 * D], kind=EI),
        gw_d=T("fgwt", [D, 8], kind=EI), gb_d=T("fgbrow", [1, 8], kind=EI),
        w13s=nc.dram_tensor("w13scr", [NE, 11, 128, 2048], BF16),
        w2s=nc.dram_tensor("w2scr", [NE, 128, 11 * D], BF16),
    )
    io4["mw1_d"] = io5["w1_d"]
    io4["mw3_d"] = io5["w3_d"]
    io4["mw2_d"] = io5["w2_d"]
    io4["mw13s"] = io5["w13s"]
    io4["mw2s"] = io5["w2s"]
    io5["preconverted"] = True
    with ExitStack() as _st:
        p = Prog(nc, _st)
        emit_l0_attn(nc, p, io1)
        emit_l0_ffn_half(nc, p, io2)
        emit_l0_moe(nc, p, io3)
        emit_l1_attn(nc, p, io4)
        emit_l1_moe(nc, p, io5)
    return nc


def kernel(**inp):
    consts = make_consts()
    cores = list(range(8))
    x = np.ascontiguousarray(inp["L0_attn_x"], dtype=np.float32)

    rotm = np.zeros((128, 128), np.float32)
    for base in (0, 64):
        for dd in range(32):
            rotm[base + 32 + dd, base + dd] = -1.0
            rotm[base + dd, base + 32 + dd] = 1.0
    indm = np.zeros((128, 2), np.float32)
    indm[0:64, 0] = 1.0
    indm[64:128, 1] = 1.0
    crt = np.ascontiguousarray(np.concatenate(
        [np.asarray(inp["L0_ffn_centroids"]).T, np.asarray(inp["L0_ffn_w_route"]).T], axis=1),
        dtype=np.float32)
    b1pt3 = np.ascontiguousarray(
        np.asarray(inp["L0_ffn_b1"], dtype=np.float32).reshape(8, 4, 128).transpose(2, 0, 1).reshape(128, 32))
    NE = 9
    w1a = np.concatenate([inp["L1_ffn_w1"], inp["L1_ffn_sw1"][None]], 0)
    w3a = np.concatenate([inp["L1_ffn_w3"], inp["L1_ffn_sw3"][None]], 0)
    w2a = np.concatenate([inp["L1_ffn_w2"], inp["L1_ffn_sw2"][None]], 0)
    b1a = np.concatenate([inp["L1_ffn_b1"], inp["L1_ffn_sb1"][None]], 0)
    b3a = np.concatenate([inp["L1_ffn_b3"], inp["L1_ffn_sb3"][None]], 0)
    b2a = np.concatenate([inp["L1_ffn_b2"], inp["L1_ffn_sb2"][None]], 0)

    def pt11(b):
        out = np.zeros((128, NE * 11), np.float32)
        for e in range(NE):
            out[:, e * 11:(e + 1) * 11] = np.asarray(b[e], dtype=np.float32).reshape(11, 128).T
        return np.ascontiguousarray(out)

    b2p = np.zeros((128, 3 * 1024), np.float32)
    for e in range(NE):
        b2p[32 * (e // 3), (e % 3) * 1024:(e % 3 + 1) * 1024] = b2a[e]

    common = {
        "ident": consts["ident"], "identb": consts["ident"].astype(ml_dtypes.bfloat16),
        "ones_row": consts["ones_row"],
        "pWq": _f(inp["L0_attn_Wq"].reshape(D, D)), "pWk": _f(inp["L0_attn_Wk"].reshape(D, D)),
        "pWv": _f(inp["L0_attn_Wv"].reshape(D, D)), "pWo": _f(inp["L0_attn_Wo"].reshape(D, D)),
        "projT": _f(np.asarray(inp["L0_attn_proj"]).T),
        "g1row": _f(inp["L0_attn_ln1_g"]).reshape(1, D), "b1row": _f(inp["L0_attn_ln1_b"]).reshape(1, D),
        "W1": _f(inp["L0_attn_W1"]), "W2": _f(inp["L0_attn_W2"]),
        "b1pt": per_part(inp["L0_attn_b1"]), "b2row": _f(inp["L0_attn_b2"]).reshape(1, D),
        "g2row": _f(inp["L0_attn_ln2_g"]).reshape(1, D), "bb2row": _f(inp["L0_attn_ln2_b"]).reshape(1, D),
        "mw1": _f(inp["L0_ffn_w1"]), "mw2": _f(inp["L0_ffn_w2"]),
        "mb1pt": b1pt3, "mb2": _f(inp["L0_ffn_b2"]), "cen": _f(inp["L0_ffn_centroids"]), "crt": crt,
        "awqT": _f(np.asarray(inp["L1_attn_wq"]).T), "awkT": _f(np.asarray(inp["L1_attn_wk"]).T),
        "awvT": _f(np.asarray(inp["L1_attn_wv"]).T), "awoT": _f(np.asarray(inp["L1_attn_wo"]).T),
        "abqpt": per_part(inp["L1_attn_bq"]), "abkpt": per_part(inp["L1_attn_bk"]),
        "abvrow": _f(inp["L1_attn_bv"]).reshape(1, D), "aborow": _f(inp["L1_attn_bo"]).reshape(1, D),
        "cosFt": consts["cosF"], "sinFt": consts["sinF"], "rotM": rotm, "indM": indm,
        "fw1": _f(w1a), "fw3": _f(w3a), "fw2": _f(w2a),
        "fb1pt": pt11(b1a), "fb3pt": pt11(b3a), "fb2p": _f(b2p),
        "fgwt": _f(np.asarray(inp["L1_ffn_gate_w"]).T), "fgbrow": _f(inp["L1_ffn_gate_b"]).reshape(1, 8),
    }
    nc = _prog("fused", build_fused)
    in_maps = [dict(common, x=_f(x[c])) for c in cores]
    res = run_bass_kernel_spmd(nc, in_maps, core_ids=cores)
    return np.stack([np.asarray(r["out"]) for r in res.results]).astype(np.float32)
